# version 11
# baseline (speedup 1.0000x reference)
import numpy as np
import ml_dtypes
from contextlib import ExitStack
import concourse.bass as bass
import concourse.mybir as mybir
import concourse.library_overlay as lov
from concourse.bass_utils import run_bass_kernel_spmd

F32 = mybir.dt.float32
BF16 = mybir.dt.bfloat16
I16 = mybir.dt.int16
I32 = mybir.dt.int32
U16 = mybir.dt.uint16
U32 = mybir.dt.uint32
AF = mybir.ActivationFunctionType
ALU = mybir.AluOpType
AX = mybir.AxisListType

NCORE = 8
TC = 4096
D = 1024
NCH = 8
QKVCH = 24
MEGA = 512
NMEGA = TC // MEGA
NTILE = 4
NTA = TC // 128
NBLK = 8
NSLAB = 64
EPS = 1e-5
E = 64
ELOC = 8
APS = 5
CAP = 256
MT = 256
NROWS = ELOC * MT
MFD = 10368
II = 768
TB = 32768
LAY = 17

SC_P1 = 73
SC_P2 = 15
SC_P2B = SC_P1 * NMEGA + 11 * 8
SC_P3B = SC_P2B + SC_P2 * NMEGA
SC_P4B = SC_P3B + NTA
SC_PE = 6 + 11 * 8
MROW = ELOC * CAP + 1
MREC = 8 * MROW
NREC = 4 * TC
NLD0A = 23
NLD1 = 49
NLD0 = 58
DBG = False
PH = 99
MLIM = 99
SUB = 99
NLDE = 25


def _host_prep(core, inp):
    H = {}
    t0 = core * TC
    perm = np.zeros(3072, np.int64)
    for c in range(16):
        u = 2 * (c % 8) + c // 8
        for p in range(128):
            perm[128 * c + p] = 192 * u + p
    for cv in range(8):
        c = 16 + cv
        for p in range(128):
            g = 2 * cv + (1 if p >= 64 else 0)
            perm[128 * c + p] = 192 * g + 128 + (p % 64)
    H["wq"] = np.ascontiguousarray(inp["L0_attn_w_qkv"][:, perm])
    H["bq"] = np.ascontiguousarray(inp["L0_attn_b_qkv"][perm].reshape(24, 128).T)
    H["wo"] = np.ascontiguousarray(inp["L0_attn_w_out"])
    H["bo"] = inp["L0_attn_b_out"]
    H["su32"] = np.ascontiguousarray(
        (np.arange(NTA)[:, None] < np.arange(NTA)[None, :]).astype(np.float32))
    pos = (t0 + np.arange(TC)) % 16384
    div = np.exp(np.arange(0, 64, 2, dtype=np.float32) * (-np.log(10000.0) / 64.0))
    ang = pos[:, None].astype(np.float32) * div[None, :]
    pe = np.zeros((TC, 64), np.float32)
    pe[:, 0::2] = np.sin(ang)
    pe[:, 1::2] = np.cos(ang)
    H["peT"] = np.ascontiguousarray(pe.T)
    R = 1e9
    mU = np.zeros((9, 128), np.float32)
    mV = np.zeros((9, 128), np.float32)
    mU[0, :] = -R
    mV[0, :] = R
    for tau in range(8):
        for hp in range(2):
            for hh in range(8):
                mU[1 + tau, hp * 64 + hh * 8 + tau] = R
        for g in range(16):
            mV[1 + tau, g * 8 + tau] = R
    H["mU"], H["mV"] = mU, mV
    H["idn"] = np.eye(128, dtype=np.float32)
    H["lstrict"] = np.ascontiguousarray(
        (np.arange(128)[:, None] < np.arange(128)[None, :]).astype(np.float32))
    porder = np.zeros(D, np.int64)
    for sl8 in range(8):
        kap, par = sl8 // 2, sl8 % 2
        porder[128 * sl8:128 * sl8 + 128] = 256 * kap + par + 2 * np.arange(128)
    H["gwT"] = np.ascontiguousarray(inp["L0_ffn_gate_w"].T[porder])
    es = slice(core * ELOC, core * ELOC + ELOC)
    H["gup"] = np.ascontiguousarray(inp["L0_ffn_gate_packed"][es].transpose(0, 2, 1))
    H["uup"] = np.ascontiguousarray(inp["L0_ffn_up_packed"][es].transpose(0, 2, 1))
    H["dnp"] = np.ascontiguousarray(inp["L0_ffn_down_packed"][es].transpose(0, 2, 1))
    H["shg"] = np.ascontiguousarray(inp["L0_ffn_sh_gate_packed"].T)
    H["shu"] = np.ascontiguousarray(inp["L0_ffn_sh_up_packed"].T)
    H["shd"] = np.ascontiguousarray(inp["L0_ffn_sh_down_packed"].T)
    gs = inp["L0_ffn_gate_scales"][es]
    us = inp["L0_ffn_up_scales"][es]
    ds = inp["L0_ffn_down_scales"][es]
    scA = np.zeros((4, 2, 18, II), np.float32)
    scB = np.zeros((3, 2, 9, D), np.float32)
    for k in range(4):
        for h in range(2):
            for e in range(ELOC):
                scA[k, h, 2 * e + 0] = gs[e, :, 2 * k + h]
                scA[k, h, 2 * e + 1] = us[e, :, 2 * k + h]
            scA[k, h, 16] = inp["L0_ffn_sh_gate_scales"][:, 2 * k + h]
            scA[k, h, 17] = inp["L0_ffn_sh_up_scales"][:, 2 * k + h]
    for k in range(3):
        for h in range(2):
            for e in range(ELOC):
                scB[k, h, e] = ds[e, :, 2 * k + h]
            scB[k, h, 8] = inp["L0_ffn_sh_down_scales"][:, 2 * k + h]
    H["scA"], H["scB"] = scA * 0.5, scB * 0.5
    sm = np.zeros((8, 1), np.float32)
    sm[:core, 0] = 1.0
    H["stepmask"] = sm
    H["coreid"] = np.array([core], np.int32)
    ee = np.arange(E)
    H["evec"] = (MROW * (ee // ELOC) + 256 * (ee % ELOC) + CAP).astype(np.float32)
    H["etrash"] = (MROW * (ee // ELOC) + CAP * ELOC).astype(np.float32)
    H["eiota"] = ee.astype(np.float32)
    tt = (t0 + 128 * np.arange(NTA)[None, :] + np.arange(128)[:, None]).astype(np.int64)
    H["tvec"] = (4096 * ((tt % 4096) // 512) + 512 * (tt // 4096)
                 + (tt % 512)).astype(np.float32)
    H["tloc"] = (128.0 * np.arange(NTA)[None, :]
                 + np.arange(128)[:, None]).astype(np.float32)
    cc_ = (ee % ELOC) // 2
    rr_ = ee // ELOC
    rows_c = np.where(cc_ < 3, 512, 513)
    cbase = np.where(cc_ < 3, 4096 * cc_, 12288)
    H["evec2"] = (cbase + rows_c * rr_ + 256 * ((ee % ELOC) % 2) + CAP).astype(np.float32)
    H["etrash2"] = (12288.0 + 513 * rr_ + 512).astype(np.float32)
    H["x"] = np.ascontiguousarray(inp["L0_attn_x"].reshape(TB, D)[t0:t0 + TC])
    return H


_IN_SPECS = [
    ("x", [TC, D], F32), ("wq", [D, 3072], F32), ("bq", [128, 24], F32),
    ("wo", [D, D], F32), ("bo", [D], F32), ("su32", [NTA, NTA], F32),
    ("peT", [64, TC], F32),
    ("mU", [9, 128], F32), ("mV", [9, 128], F32), ("idn", [128, 128], F32),
    ("lstrict", [128, 128], F32), ("gwT", [D, E], F32),
    ("gup", [ELOC, 512, II], I32), ("uup", [ELOC, 512, II], I32),
    ("dnp", [ELOC, 384, D], I32), ("shg", [512, II], I32), ("shu", [512, II], I32),
    ("shd", [384, D], I32), ("scA", [4, 2, 18, II], F32), ("scB", [3, 2, 9, D], F32),
    ("stepmask", [8, 1], F32), ("coreid", [1], I32),
    ("evec", [E], F32), ("etrash", [E], F32), ("eiota", [E], F32),
    ("tvec", [128, NTA], F32), ("tloc", [128, NTA], F32),
    ("evec2", [E], F32), ("etrash2", [E], F32),
]


DEC_SC = 8


def _emit_decode_tile(v, step, s, w, tmps, lo, hi, st, scb, enm):
    a_l, a_h, ts_, t0f, t1f, af, af2 = tmps
    ALU = mybir.AluOpType
    v.wait_ge(s.sch, scb)
    v.tensor_scalar(a_l, w, 7, None, op0=ALU.bitwise_and)
    v.tensor_scalar(a_h, w, 4, 7, op0=ALU.logical_shift_right, op1=ALU.bitwise_and)
    step(v.tensor_scalar(ts_, w, 3, 1, op0=ALU.logical_shift_right,
                         op1=ALU.bitwise_and), enm + "a")
    v.wait_ge(s.sch, scb + 1)
    step(v.tensor_scalar(ts_, w, 7, 1, op0=ALU.logical_shift_right,
                         op1=ALU.bitwise_and), enm + "b")
    v.wait_ge(s.sch, scb + 3)
    step(v.tensor_tensor(lo, lo, af, op=ALU.max), enm + "c")
    v.wait_ge(s.sch, scb + 4)
    step(v.tensor_tensor(lo, lo, af2, op=ALU.max), enm + "d")
    step(v.tensor_tensor(lo, lo, t0f, op=ALU.mult))
    v.wait_ge(s.sch, scb + 7)
    step(v.tensor_tensor(hi, hi, af, op=ALU.max))
    v.wait_ge(s.sch, scb + 8)
    step(v.tensor_tensor(hi, hi, af2, op=ALU.max))
    step(v.tensor_tensor(hi, hi, t1f, op=ALU.mult))
    v.tensor_tensor(lo, lo, st, op=ALU.mult)
    step(v.tensor_tensor(hi, hi, st, op=ALU.mult))


def _emit_decode_tile_sc(sc, sstep, s, ev, tmps, lo, hi, enm):
    a_l, a_h, ts_, t0f, t1f, af, af2 = tmps
    b4 = s.cbias[:, 0:1]
    b16 = s.cbias[:, 1:2]
    sc.wait_ge(s.vch, ev[enm + "a"])
    sstep(sc.activation(out=t0f, in_=ts_, func=AF.Identity, scale=-2.0, bias=1.0))
    sstep(sc.activation(out=lo, in_=a_l, func=AF.Identity, scale=2.0, bias=b4))
    sstep(sc.activation(out=af, in_=a_l, func=AF.Identity, scale=4.0, bias=b16))
    sstep(sc.activation(out=af2, in_=a_l, func=AF.Identity, scale=1.0))
    sstep(sc.activation(out=hi, in_=a_h, func=AF.Identity, scale=2.0, bias=b4))
    sc.wait_ge(s.vch, ev[enm + "b"])
    sstep(sc.activation(out=t1f, in_=ts_, func=AF.Identity, scale=-2.0, bias=1.0))
    sc.wait_ge(s.vch, ev[enm + "c"])
    sstep(sc.activation(out=af, in_=a_h, func=AF.Identity, scale=4.0, bias=b16))
    sc.wait_ge(s.vch, ev[enm + "d"])
    sstep(sc.activation(out=af2, in_=a_h, func=AF.Identity, scale=1.0))


def build_nc():
    nc = bass.Bass("TRN2", target_bir_lowering=False, debug=False, num_swdge_queues=4)
    d = nc.dram_tensor
    DI = {}
    for name, shape, dt_ in _IN_SPECS:
        DI[name] = d(name, shape, dt_, kind="ExternalInput").ap()
    out_d = d("out", [TC, D], F32, kind="ExternalOutput").ap()
    if DBG:
        x2p1_d = d("x2p1", [TC, D], BF16, kind="ExternalOutput").ap()
        dbg_d = d("dbg", [128, 4096], F32, kind="ExternalOutput").ap()
        mmy_d = d("mmy", [MROW * 4], F32, kind="ExternalOutput").ap()
        dbgb_d = d("dbgb", [128, 2048], BF16, kind="ExternalOutput").ap()
        dbgu_d = d("dbgu", [128, 144], U32, kind="ExternalOutput").ap()

    ag1_in = d("ag1_in", [E], F32, kind="Internal").ap()
    ag1_out = d("ag1_out", [8, E], F32, kind="Internal").ap()
    ag2_in = d("ag2_in", [TC * D], BF16, kind="Internal").ap()
    ag2_out = d("ag2_out", [8 * TC * D], BF16, kind="Internal").ap()
    ag4_in = d("ag4_in", [MROW * D], BF16, kind="Internal").ap()
    ag4_out = d("ag4_out", [(3 * 8 * 512 + 8 * 513) * D], BF16, kind="Internal").ap()
    rb_d = d("rb_d", [NTA * E], F32, kind="Internal").ap()
    m_in = d("m_in", [(MREC + NREC) * 4], F32, kind="Internal").ap()
    m_sub = [d(f"m_sub{q}", [(MREC + NREC) * 4], F32, kind="Internal").ap()
             for q in range(8)]
    m_my = d("m_my", [MROW * 4], F32, kind="Internal").ap()
    m_all = d("m_all", [8, MROW * 4], F32, kind="Internal").ap()

    stk = ExitStack()
    sb = lambda n, s_, dt_: stk.enter_context(nc.sbuf_tensor(n, s_, dt_))
    ps = lambda n, s_, dt_: stk.enter_context(nc.psum_tensor(n, s_, dt_))
    sem = lambda n: stk.enter_context(nc.semaphore(n))

    class S:
        pass
    s = S()
    s.x2full = sb("x2full", [128, NTA, D], BF16)
    s.ident = sb("ident", [128, 128], BF16)
    s.lnsc = sb("lnsc", [128, NTILE], F32)
    s.varsum = sb("varsum", [128, NTILE], F32)
    s.rstd = sb("rstd", [128, NTILE], F32)
    s.bias2 = sb("bias2", [128, NTILE], F32)
    s.logits = sb("logits", [128, NTA, E], BF16)
    s.stepm = sb("stepm", [8, 1], F32)
    s.shardi = sb("shardi", [128, 1], I32)
    s.nb4 = sb("nb4", [128, 1], F32)
    s.cbias = sb("cbias", [128, 2], F32)
    s.galn = sb("galn", [128, 16], F32)
    s.gidx32 = sb("gidx32", [128, 16], U32)
    s.RkC32 = sb("RkC32", [128, 4 * NTA], U32)
    s.zf = sb("zf", [128, E], F32)
    s.zt = s.logits.rearrange("p a e -> p (a e)")[:, 0:D]
    g1 = ExitStack()
    sb1 = lambda n, s_, dt_: g1.enter_context(nc.sbuf_tensor(n, s_, dt_))
    s.wqkv = sb1("wqkv", [128, NCH, 3072], BF16)
    s.bqkvP = sb1("bqkvP", [128, QKVCH], F32)
    s.bout = sb1("bout", [1, D], BF16)
    s.ones = sb1("ones", [1, 128], BF16)
    s.maskU = sb1("maskU", [9, 128], BF16)
    s.maskV = sb1("maskV", [9, 128], BF16)
    s.sums = sb1("sums", [128, NSLAB], F32)
    s.recip = sb1("recip", [128, 64], F32)
    s.wout = sb1("wout", [128, NCH, D], BF16)
    s.xot = sb1("xot", [128, 4096], BF16)
    s.xlnT = sb1("xlnT", [128, NCH, MEGA], BF16)
    s.qPb = sb1("qPb", [128, NSLAB * 128], BF16)
    s.qP = s.qPb[0:64, :].rearrange("p (sl m) -> p sl m", m=128)
    s.kst = s.qPb[64:128, :].rearrange("p (c t) -> p c t", t=MEGA)
    s.kL2 = sb1("kL2", [64, 16, MEGA], BF16)
    s.vP = sb1("vP", [128, NSLAB, 64], BF16)
    s.phat = sb1("phat", [128, NBLK, 128], BF16)
    s.pnT = sb1("pnT", [128, NBLK, 128], BF16)
    s.vstk = sb1("vstk", [128, NBLK, 64], BF16)
    s.xln = s.xot.rearrange("p (a d) -> p a d", a=NTILE)
    s.oT = s.xot.rearrange("p (c x) -> p c x", c=NCH)
    g1.close()
    g2 = ExitStack()
    sb2 = lambda n, s_, dt_: g2.enter_context(nc.sbuf_tensor(n, s_, dt_))
    s.gw = sb2("gw", [128, NCH, E], BF16)
    s.dtmpblk = sb2("dtmpblk", [128, 7 * D], BF16)
    s.dtmp = ([s.dtmpblk[:, i * D:(i + 1) * D].bitcast(I16) for i in range(3)]
              + [s.dtmpblk[:, i * D:(i + 1) * D] for i in range(3, 7)])
    s.shwordsA = sb2("shwordsA", [128, 8, II], I16)
    s.shwordsB = sb2("shwordsB", [128, 3, D], I16)
    s.shsctA = sb2("shsctA", [128, 8, II], BF16)
    s.shsctB = sb2("shsctB", [128, 3, D], BF16)
    s.shloA = sb2("shloA", [128, 8, II], BF16)
    s.shloB = sb2("shloB", [128, 3, D], BF16)
    s.shhiA = sb2("shhiA", [128, 8, II], BF16)
    s.shhiB = sb2("shhiB", [128, 3, D], BF16)
    s.y2 = sb2("y2", [128, NTILE, D], BF16)
    s.y2Tp = sb2("y2Tp", [128, NCH, MEGA], BF16)
    s.hshT = sb2("hshT", [128, 6, MEGA], BF16)
    s.hshU = sb2("hshU", [128, 6, MEGA], BF16)
    s.shTs = sb2("shTs", [128, 8, MEGA], BF16)
    s.logT = sb2("logT", [64, MEGA], BF16)
    g2.close()
    g3 = ExitStack()
    sb3 = lambda n, s_, dt_: g3.enter_context(nc.sbuf_tensor(n, s_, dt_))
    s.srcM = sb3("srcM", [128, 128, 4], F32)
    s.capall = sb3("capall", [128, NTA, E], F32)
    s.su32sb = sb3("su32sb", [128, NTA, NTA], BF16)
    s.posR = sb3("posR", [128, NTA, E], F32)
    s.mask01 = sb3("mask01", [128, NTA, E], BF16)
    s.wfull = sb3("wfull", [128, NTA, E], F32)
    s.ltb = sb3("ltb", [128, NTA, E], F32)
    s.keptf = sb3("keptf", [128, NTA, E], BF16)
    s.mx8 = sb3("mx8", [128, NTA, 8], F32)
    s.nmx = sb3("nmx", [128, NTA], F32)
    s.s4 = sb3("s4", [128, NTA], F32)
    s.w8 = sb3("w8", [128, NTA, 8], F32)
    s.i8 = sb3("i8", [128, NTA, 8], U32)
    s.i8f = sb3("i8f", [128, NTA, 8], F32)
    s.gtf = sb3("gtf", [128, 16], F32)
    s.macc = sb3("macc", [128, 512], F32)
    s.mtmp = [sb3(f"mtmp{i}", [128, 512], F32) for i in range(2)]
    s.macct = sb3("macct", [8, 4], F32)
    s.mtmpt = [sb3(f"mtmpt{i}", [8, 4], F32) for i in range(2)]
    s.RkSf = sb3("RkSf", [128, 4, NTA], F32)
    s.RkS32 = sb3("RkS32", [128, 4 * NTA], U32)
    s.Rtmp1 = sb3("Rtmp1", [128, NTA], F32)
    s.Rtmp2 = sb3("Rtmp2", [128, NTA], F32)
    s.tlv = sb3("tlv", [128, NTA], F32)
    s.tvecs = sb3("tvecs", [128, NTA], F32)
    s.posR2 = sb3("posR2", [128, NTA, E], F32)
    s.evec2b = sb3("evec2b", [128, E], F32)
    s.etb2 = sb3("etb2", [128, E], F32)
    s.RkCall = sb3("RkCall", [128, 4, NTA], F32)
    s.evecb = sb3("evecb", [128, E], F32)
    s.etb = sb3("etb", [128, E], F32)
    s.eiob = sb3("eiob", [128, E], F32)
    s.lstr = sb3("lstr", [128, 128], BF16)
    s.rbsb = sb3("rbsb", [NTA, E], F32)
    s.onesc = sb3("onesc", [1, NTA], F32)
    s.onesf = sb3("onesf", [128, 1], BF16)
    s.Rkall = sb3("Rkall", [128, 4, NTA], F32)
    s.baserow = sb3("baserow", [1, E], F32)
    s.countrow = sb3("countrow", [1, E], F32)
    s.agc = sb3("agc", [8, E], F32)
    g3.close()
    g4 = ExitStack()
    sb4 = lambda n, s_, dt_: g4.enter_context(nc.sbuf_tensor(n, s_, dt_))
    s.g4big = sb4("g4big", [128, 16 * D], BF16)
    s.buf = s.g4big.rearrange("p (a d) -> p a d", d=D)
    s.eloA = s.g4big[:, 0:8 * II].rearrange("p (a d) -> p a d", d=II)
    s.eloB = s.g4big[:, 8 * II:8 * II + 3 * D].rearrange("p (a d) -> p a d", d=D)
    EOF_ = 8 * II + 3 * D
    s.dtmp4 = ([s.g4big[:, EOF_ + D * i:EOF_ + D * (i + 1)].bitcast(I16) for i in range(3)]
               + [s.g4big[:, EOF_ + D * i:EOF_ + D * (i + 1)] for i in range(3, 7)])
    s.bufT = sb4("bufT", [128, 8, NROWS], BF16)
    s.ewordsA = sb4("ewordsA", [128, 8, II], I16)
    s.ewordsB = sb4("ewordsB", [128, 3, D], I16)
    s.esctA = sb4("esctA", [128, 8, II], BF16)
    s.esctB = sb4("esctB", [128, 3, D], BF16)
    s.ehiA = sb4("ehiA", [128, 8, II], BF16)
    s.ehiB = sb4("ehiB", [128, 3, D], BF16)
    s.hT = sb4("hT", [128, 6, MT], BF16)
    s.hup = sb4("hup", [128, 6, MT], BF16)
    s.eoTs = sb4("eoTs", [128, 8, MT], BF16)
    s.eor = sb4("eor", [128, 2, D], BF16)
    g4.close()
    g5 = ExitStack()
    sb5 = lambda n, s_, dt_: g5.enter_context(nc.sbuf_tensor(n, s_, dt_))
    s.cbuf = [sb5(f"cbuf{i}", [128, 4, D], BF16) for i in range(2)]
    s.outf = [sb5(f"outf{i}", [128, D], F32) for i in range(2)]
    g5.close()
    s.ps_qkv = [ps("ps_qkv0", [128, 512], F32), ps("ps_qkv1", [128, 512], F32)]
    s.ps_s = [ps(f"ps_s{i}", [128, 512], F32) for i in range(4)]
    s.ps_tr = ps("ps_tr", [128, 8, 128], BF16)
    s.ps_vs = ps("ps_vs", [128, 8, 64], BF16)

    s.s_wl = sem("s_wl")
    s.s_x = sem("s_x")
    s.s_lay = sem("s_lay")
    s.s_xt = sem("s_xt")
    s.s_y2p = sem("s_y2p")
    s.s_qkv = sem("s_qkv")
    s.s_S = sem("s_S")
    s.s_pnt = sem("s_pnt")
    s.s_vst = sem("s_vst")
    s.s_av = sem("s_av")
    s.s_op = sem("s_op")
    s.s_lg = sem("s_lg")
    s.s_sh = sem("s_sh")
    s.s_cnt = sem("s_cnt")
    s.s_ag = sem("s_ag")
    s.s_gth = sem("s_gth")
    s.s_eW = sem("s_eW")
    s.s_ex = sem("s_ex")
    s.s_sc2 = sem("s_sc2")
    s.s_m = sem("s_m")
    s.s_eod = sem("s_eod")
    s.s_cb = [sem(f"s_cb{k}") for k in range(4)]
    s.s_ms = [sem(f"s_ms{q}") for q in range(8)]
    s.s_dbg = sem("s_dbg")
    s.s_ag2 = sem("s_ag2")
    s.s_ag4 = sem("s_ag4")
    s.s_dbg2 = sem("s_dbg2")
    s.s_az = sem("s_az")
    s.s_fin = sem("s_fin")
    s.s_rb = sem("s_rb")
    s.vch = sem("vch")
    s.sch = sem("sch")
    vc = [0]
    ev = {}

    def emit_vector(v):
        def step(ins, name=None):
            vc[0] += 1
            ins.then_inc(s.vch, 1)
            v.wait_ge(s.vch, vc[0])
            if name:
                ev[name] = vc[0]
        v.memset(s.ones[:, :], 1.0)
        v.memset(s.zt[:, :], 0.0)
        v.memset(s.cbias[:, 0:1], -4.0)
        v.memset(s.cbias[:, 1:2], -16.0)
        step(v.memset(s.zf[:, :], 0.0), "init0")
        for m in range(min(MLIM, NMEGA)):
            x = s.x2full[:, NTILE * m:NTILE * m + NTILE, :]
            v.wait_ge(s.s_x, 16 * (m + 1))
            if m > 0:
                v.wait_ge(s.sch, SC_P1 * (m - 1) + NTILE)
            step(v.tensor_reduce(s.lnsc[:, :], x[:, :, :], axis=AX.X, op=ALU.add, negate=True))
            step(v.tensor_scalar_mul(s.lnsc[:, :], s.lnsc[:, :], 1.0 / D), f"negmean{m}")
            v.wait_ge(s.sch, SC_P1 * m + NTILE)
            step(v.tensor_scalar(s.rstd[:, :], s.varsum[:, :], 1.0 / D, EPS, op0=ALU.mult, op1=ALU.add))
            step(v.reciprocal(s.rstd[:, :], s.rstd[:, :]), f"rstd{m}")
            v.wait_ge(s.sch, SC_P1 * m + NTILE + 1)
            step(v.tensor_tensor(s.bias2[:, :], s.lnsc[:, :], s.rstd[:, :], op=ALU.mult), f"bias2{m}")
            for tile in range(NTILE):
                v.wait_ge(s.s_xt, NTILE * m + tile + 1)
                step(v.tensor_copy(s.xlnT[:, :, 128 * tile:128 * tile + 128], s.ps_tr[:, :, :]),
                     f"xte{m}_{tile}")
            if SUB < 2:
                continue
            if m > 0:
                v.wait_ge(s.s_S, NBLK * m)
                v.wait_ge(s.s_vst, NBLK * m)
                v.wait_ge(s.s_lay, 16 * LAY * m)
            for c in range(QKVCH):
                v.wait_ge(s.s_qkv, QKVCH * m + c + 1)
                bank = s.ps_qkv[c % 2]
                if c < 16:
                    m0 = 64 * (c // 8) + 8 * (c % 8)
                    v.tensor_scalar(s.qP[:, :, m0:m0 + 8],
                                    bank[0:64, :].rearrange("p (sl ti) -> p sl ti", ti=8),
                                    s.bqkvP[0:64, c:c + 1], None, op0=ALU.add)
                    step(v.tensor_scalar(s.kst[:, c, :], bank[64:128, :],
                                         s.bqkvP[64:128, c:c + 1], None, op0=ALU.add),
                         f"qkve{m}_{c}")
                else:
                    cv = c - 16
                    step(v.tensor_scalar(s.vP[:, :, 8 * cv:8 * cv + 8],
                                         bank[:, :].rearrange("p (sl ti) -> p sl ti", ti=8),
                                         s.bqkvP[:, c:c + 1], None, op0=ALU.add),
                         f"qkve{m}_{c}")
            if SUB < 3 or 30 <= SUB <= 31:
                continue
            for b in range(NBLK):
                blk = NBLK * m + b
                v.wait_ge(s.sch, SC_P1 * m + 9 + 8 * (b + 1))
                step(v.reciprocal(s.recip[:, 8 * b:8 * b + 8], s.sums[:, 8 * b:8 * b + 8]))
                for h4 in range(2):
                    sl = slice(4 * h4, 4 * h4 + 4)
                    step(v.scalar_tensor_tensor(
                        s.phat[:, sl, :], s.phat[:, sl, :], 1.0,
                        s.recip[:, 8 * b + 4 * h4:8 * b + 4 * h4 + 4, None].to_broadcast([128, 4, 128]),
                        op0=ALU.mult, op1=ALU.mult))
                ev[f"pn{blk}"] = vc[0]
                if SUB < 4:
                    continue
                v.wait_ge(s.s_pnt, blk + 1)
                step(v.tensor_copy(s.pnT[:, :, :], s.ps_tr[:, :, :]), f"pnte{blk}")
                v.wait_ge(s.s_vst, blk + 1)
                step(v.tensor_copy(s.vstk[:, :, :], s.ps_vs[:, :, :]), f"vste{blk}")
                if SUB < 5:
                    continue
                v.wait_ge(s.s_av, blk + 1)
                otv = s.oT[:, :, 64 * b:64 * b + 64].rearrange("p c (j t) -> p c j t", t=8)
                v.tensor_copy(otv[0:64], s.ps_s[2 * (b % 2)][0:64, :].rearrange(
                    "p (j hh t) -> p hh j t", hh=8, t=8))
                step(v.tensor_copy(otv[64:128], s.ps_s[2 * (b % 2) + 1][64:128, :].rearrange(
                    "p (j hh t) -> p hh j t", hh=8, t=8)), f"ote{blk}")
            if SUB < 6:
                continue
            for tile in range(NTILE):
                v.wait_ge(s.s_op, NTILE * m + tile + 1)
                xt_ = s.x2full[:, NTILE * m + tile, :]
                v.tensor_tensor(xt_[:, 0:512], xt_[:, 0:512], s.ps_qkv[0][:, :], op=ALU.add)
                step(v.tensor_tensor(xt_[:, 512:1024], xt_[:, 512:1024],
                                     s.ps_qkv[1][:, :], op=ALU.add), f"x2_{m}_{tile}")
        ev["p1"] = vc[0]
        if PH == 1:
            return
        v.wait_ge(s.s_wl, 16 * NLD1)
        for tl in range(11):
            if tl < 8:
                N, w_, lo_, hi_, st_ = II, s.shwordsA[:, tl, :], s.shloA[:, tl, :], \
                    s.shhiA[:, tl, :], s.shsctA[:, tl, :]
            else:
                N, w_, lo_, hi_, st_ = D, s.shwordsB[:, tl - 8, :], s.shloB[:, tl - 8, :], \
                    s.shhiB[:, tl - 8, :], s.shsctB[:, tl - 8, :]
            _emit_decode_tile(v, step, s, w_, [t[:, 0:N] for t in s.dtmp], lo_, hi_, st_,
                              SC_P1 * NMEGA + 8 * tl, f"ds{tl}")
        ev["shdec"] = vc[0]
        for m in range(NMEGA):
            x2 = s.x2full[:, NTILE * m:NTILE * m + NTILE, :]
            scb = SC_P2B + SC_P2 * m
            if m == 0:
                v.wait_ge(s.s_az, 16 * 17)
            v.wait_ge(s.sch, (SC_P2B + SC_P2 * (m - 1) + NTILE) if m > 0 else (SC_P2B - SC_P1 + NTILE))
            step(v.tensor_reduce(s.lnsc[:, :], x2[:, :, :], axis=AX.X, op=ALU.add, negate=True))
            step(v.tensor_scalar_mul(s.lnsc[:, :], s.lnsc[:, :], 1.0 / D), f"negmean2_{m}")
            v.wait_ge(s.sch, scb + NTILE)
            step(v.tensor_scalar(s.rstd[:, :], s.varsum[:, :], 1.0 / D, EPS, op0=ALU.mult, op1=ALU.add))
            step(v.reciprocal(s.rstd[:, :], s.rstd[:, :]), f"rstd2_{m}")
            v.wait_ge(s.sch, scb + NTILE + 1)
            step(v.tensor_tensor(s.bias2[:, :], s.lnsc[:, :], s.rstd[:, :], op=ALU.mult), f"bias22_{m}")
            for tile in range(NTILE):
                v.wait_ge(s.s_xt, NTILE * NMEGA + NTILE * m + tile + 1)
                step(v.tensor_copy(s.y2Tp[:, :, 128 * tile:128 * tile + 128], s.ps_tr[:, :, :]),
                     f"y2pe{m}_{tile}")
            v.wait_ge(s.s_lg, 6 * m + 1)
            step(v.tensor_copy(s.logT[:, :], s.ps_qkv[0][0:64, :]), f"logTe{m}")
            v.wait_ge(s.s_lg, 6 * m + 2)
            step(v.tensor_copy(s.logits[:, 4 * m:4 * m + 4, :], s.ps_tr[:, 0:4, 0:64]), f"loge{m}")
            for ic in range(6):
                v.wait_ge(s.s_sh, 14 * m + ic + 1)
                step(v.tensor_copy(s.hshU[:, ic, :], s.ps_s[1][:, :]), f"shupe{m}_{ic}")
                v.wait_ge(s.sch, scb + 9 + ic + 1)
                step(v.tensor_tensor(s.hshT[:, ic, :], s.hshT[:, ic, :], s.hshU[:, ic, :],
                                     op=ALU.mult), f"shmul{m}_{ic}")
            for dc in range(8):
                v.wait_ge(s.s_sh, 14 * m + 6 + dc + 1)
                pbank = s.ps_s[2] if dc % 2 == 0 else s.ps_s[3]
                step(v.tensor_copy(s.shTs[:, dc, :], pbank[:, :]), f"shdne{m}_{dc}")
            for tile in range(NTILE):
                v.wait_ge(s.s_lg, 6 * m + 2 + tile + 1)
                if DBG and tile == 0:
                    v.wait_ge(s.s_dbg, 16 * (m + 1))
                x2t = s.x2full[:, NTILE * m + tile, :]
                step(v.tensor_tensor(x2t[:, :], x2t[:, :],
                                     s.ps_tr[:, :, :].rearrange("p a b -> p (a b)"),
                                     op=ALU.add), f"acc{m}_{tile}")
        ev["p2"] = vc[0]
        if PH == 2:
            return
        v.wait_ge(s.s_sh, 14 * NMEGA)
        v.wait_ge(s.s_lg, 6 * NMEGA)
        v.wait_ge(s.s_wl, 16 * NLD0)
        v.memset(s.onesc[:, :], 1.0)
        v.memset(s.onesf[:, :], 1.0)
        step(v.memset(s.srcM[:, :, :], 0.0), "p3init")
        step(v.scalar_tensor_tensor(s.ltb[:, :, :],
                                    s.eiob[:, None, :].to_broadcast([128, NTA, E]), -1e-6,
                                    s.logits[:, :, :], op0=ALU.mult, op1=ALU.add))
        for t_ in range(NTA):
            step(v.max(out=s.mx8[:, t_, :], in_=s.ltb[:, t_, :]))
        ev["mx8"] = vc[0]
        for t_ in range(NTA):
            v.tensor_scalar(s.mask01[:, t_, :], s.ltb[:, t_, :], s.mx8[:, t_, 3:4], None,
                            op0=ALU.is_ge)
        step(v.tensor_scalar(s.nmx[:, :], s.mx8[:, :, 0], -1.0, None, op0=ALU.mult), "mask01")
        v.wait_ge(s.sch, SC_P3B + NTA)
        step(v.tensor_tensor(s.wfull[:, :, :], s.wfull[:, :, :], s.mask01[:, :, :], op=ALU.mult), "e2")
        step(v.tensor_reduce(s.s4[:, :], s.wfull[:, :, :], axis=AX.X, op=ALU.add))
        step(v.reciprocal(s.s4[:, :], s.s4[:, :]), "s4")
        for t_ in range(NTA):
            step(v.tensor_scalar(s.wfull[:, t_, :], s.wfull[:, t_, :], s.s4[:, t_:t_ + 1], None,
                                 op0=ALU.mult), "wfull" if t_ == NTA - 1 else None)
        v.wait_ge(s.s_cnt, 1)
        step(v.tensor_copy(s.countrow[:, :], s.ps_s[0][0:1, 0:E]), "cntev")
        v.wait_ge(s.s_cnt, 2)
        step(v.tensor_copy(s.baserow[:, :], s.ps_s[1][0:1, 0:E]), "basev")
        v.wait_ge(s.s_cnt, 3)
        step(v.tensor_copy(s.rbsb[:, :], s.ps_s[3][0:NTA, 0:E]), "rbev")
        step(v.tensor_scalar(s.rbsb[:, :], s.rbsb[:, :], -1.0, CAP * 1.0,
                             op0=ALU.mult, op1=ALU.add), "capm")
        v.wait_ge(s.s_rb, 16 * 2)
        for t_ in range(NTA):
            v.wait_ge(s.s_cnt, 3 + t_ + 1)
            pb = s.ps_s[t_ % 2]
            step(v.tensor_tensor(s.keptf[:, t_, :], pb[0:128, 0:E], s.capall[:, t_, :],
                                 op=ALU.is_lt), f"kept{t_}")
            step(v.scalar_tensor_tensor(s.posR[:, t_, :], pb[0:128, 0:E], 1.0,
                                        s.evecb[:, :], op0=ALU.mult, op1=ALU.add))
            step(v.tensor_tensor(s.posR[:, t_, :], s.posR[:, t_, :], s.capall[:, t_, :],
                                 op=ALU.subtract))
            step(v.scalar_tensor_tensor(s.posR2[:, t_, :], pb[0:128, 0:E], 1.0,
                                        s.evec2b[:, :], op0=ALU.mult, op1=ALU.add))
            step(v.tensor_tensor(s.posR2[:, t_, :], s.posR2[:, t_, :], s.capall[:, t_, :],
                                 op=ALU.subtract), f"posr{t_}")
        step(v.tensor_tensor(s.wfull[:, :, :], s.wfull[:, :, :], s.keptf[:, :, :], op=ALU.mult), "wm")
        step(v.tensor_tensor(s.keptf[:, :, :], s.keptf[:, :, :], s.mask01[:, :, :], op=ALU.mult))
        step(v.tensor_tensor(s.posR[:, :, :], s.posR[:, :, :], s.keptf[:, :, :], op=ALU.mult))
        step(v.tensor_scalar(s.mask01[:, :, :], s.keptf[:, :, :], -1.0, 1.0,
                             op0=ALU.mult, op1=ALU.add))
        step(v.tensor_tensor(s.posR2[:, :, :], s.posR2[:, :, :], s.keptf[:, :, :], op=ALU.mult))
        step(v.tensor_tensor(s.capall[:, :, :], s.mask01[:, :, :],
                             s.etb[:, None, :].to_broadcast([128, NTA, E]), op=ALU.mult))
        step(v.tensor_tensor(s.posR[:, :, :], s.posR[:, :, :], s.capall[:, :, :], op=ALU.add))
        step(v.tensor_tensor(s.capall[:, :, :], s.mask01[:, :, :],
                             s.etb2[:, None, :].to_broadcast([128, NTA, E]), op=ALU.mult))
        step(v.tensor_tensor(s.posR2[:, :, :], s.posR2[:, :, :], s.capall[:, :, :], op=ALU.add),
             "posm")
        for t_ in range(NTA):
            step(v.max(out=s.w8[:, t_, :], in_=s.wfull[:, t_, :]))
            step(v.max_index(out=s.i8[:, t_, :], in_max=s.w8[:, t_, :], in_values=s.wfull[:, t_, :]))
        step(v.tensor_copy(s.i8f[:, :, :], s.i8[:, :, :]), "i8fd")
        for k in range(4):
            step(v.tensor_tensor(s.keptf[:, :, :], s.eiob[:, None, :].to_broadcast([128, NTA, E]),
                                 s.i8f[:, :, k:k + 1].to_broadcast([128, NTA, E]), op=ALU.is_equal))
            step(v.tensor_tensor(s.capall[:, :, :], s.posR[:, :, :], s.keptf[:, :, :], op=ALU.mult))
            step(v.tensor_reduce(s.Rkall[:, k, :], s.capall[:, :, :], axis=AX.X, op=ALU.add))
            step(v.tensor_tensor(s.capall[:, :, :], s.posR2[:, :, :], s.keptf[:, :, :], op=ALU.mult))
            step(v.tensor_reduce(s.RkCall[:, k, :], s.capall[:, :, :], axis=AX.X, op=ALU.add))
            step(v.tensor_tensor(s.capall[:, :, :], s.mask01[:, :, :], s.keptf[:, :, :],
                                 op=ALU.mult))
            step(v.tensor_reduce(s.Rtmp1[:, :], s.capall[:, :, :], axis=AX.X, op=ALU.add))
            step(v.tensor_scalar(s.Rtmp2[:, :], s.Rtmp1[:, :], -1.0, 1.0,
                                 op0=ALU.mult, op1=ALU.add))
            step(v.tensor_tensor(s.RkSf[:, k, :], s.Rkall[:, k, :], s.Rtmp2[:, :], op=ALU.mult))
            step(v.tensor_scalar(s.Rtmp2[:, :], s.tlv[:, :], 1.0, 1.0 * (MREC + 4096 * k),
                                 op0=ALU.mult, op1=ALU.add))
            step(v.tensor_tensor(s.Rtmp2[:, :], s.Rtmp2[:, :], s.Rtmp1[:, :], op=ALU.mult))
            step(v.tensor_tensor(s.RkSf[:, k, :], s.RkSf[:, k, :], s.Rtmp2[:, :], op=ALU.add))
        step(v.tensor_copy(s.RkC32[:, :], s.RkCall[:, :, :].rearrange("p k t -> p (k t)")), "rkc")
        step(v.tensor_copy(s.RkS32[:, :], s.RkSf[:, :, :].rearrange("p k t -> p (k t)")), "rks")
        srar = s.srcM.rearrange("p a e -> p (a e)")
        for k in range(4):
            step(v.tensor_copy(srar[:, 4 * 32 * k:4 * 32 * (k + 1):4], s.tvecs[:, :]))
            step(v.tensor_copy(srar[:, 4 * 32 * k + 1:4 * 32 * (k + 1):4],
                               s.w8[:, :, k:k + 1].rearrange("p t o -> p (t o)")),
                 "srcm" if k == 3 else None)
        for q in range(1, 8):
            v.wait_ge(s.s_m, 16 * (2 * q + 2))
            step(v.tensor_tensor(s.macc[:, :], s.macc[:, :], s.mtmp[(q - 1) % 2][:, :],
                                 op=ALU.add))
            step(v.tensor_tensor(s.macct[:, :], s.macct[:, :], s.mtmpt[(q - 1) % 2][:, :],
                                 op=ALU.add), f"mrg{q}")
        v.wait_ge(s.s_m, 16 * 19)
        step(v.tensor_copy(s.gidx32[:, :], s.gtf[:, :]), "gidxc")
        for ct in range(16):
            v.wait_ge(s.s_ex, ct + 1)
            step(v.tensor_copy(s.bufT[:, :, 128 * ct:128 * ct + 128], s.ps_tr[:, :, :]),
                 f"btev{ct}")
        for e in range(ELOC):
            v.wait_ge(s.s_eW, 16 * NLDE * (e + 1))
            if e >= 1:
                v.wait_ge(s.s_ex, 16 + 16 * e)
            for tl in range(11):
                if tl < 8:
                    N, w_, lo_, hi_, st_ = II, s.ewordsA[:, tl, :], s.eloA[:, tl, :], \
                        s.ehiA[:, tl, :], s.esctA[:, tl, :]
                else:
                    N, w_, lo_, hi_, st_ = D, s.ewordsB[:, tl - 8, :], s.eloB[:, tl - 8, :], \
                        s.ehiB[:, tl - 8, :], s.esctB[:, tl - 8, :]
                _emit_decode_tile(v, step, s, w_, [t[:, 0:N] for t in s.dtmp4], lo_, hi_, st_,
                                  SC_P4B + SC_PE * e + 8 * tl, f"de{e}_{tl}")
            ev[f"edec{e}"] = vc[0]
            exb = 16 + 16 * e
            for ic in range(6):
                v.wait_ge(s.s_ex, exb + ic + 1)
                step(v.tensor_copy(s.hup[:, ic, :], s.ps_s[1][0:128, 0:MT]), f"hup{e}_{ic}")
                v.wait_ge(s.sch, SC_P4B + SC_PE * e + 11 * 8 + ic + 1)
                step(v.tensor_tensor(s.hT[:, ic, :], s.hT[:, ic, :], s.hup[:, ic, :],
                                     op=ALU.mult), f"hmul{e}_{ic}")
            for dc in range(8):
                v.wait_ge(s.s_ex, exb + 6 + dc + 1)
                pbank = s.ps_s[2] if dc % 2 == 0 else s.ps_s[3]
                step(v.tensor_copy(s.eoTs[:, dc, :], pbank[0:128, 0:MT]), f"eoe{e}_{dc}")
            v.wait_ge(s.s_m, 16 * 20)
            if e > 0:
                v.wait_ge(s.s_eod, 16 * e)
            for grp in range(2):
                v.wait_ge(s.s_ex, exb + 14 + grp + 1)
                step(v.tensor_scalar(s.eor[:, grp, :].rearrange("p (dc x) -> p dc x", dc=8),
                                     s.ps_tr[:, :, :], s.galn[:, 2 * e + grp:2 * e + grp + 1],
                                     None, op0=ALU.mult), f"eor{e}_{grp}")
        ev["p4"] = vc[0]
        for mt in range(NTA):
            for k in range(4):
                v.wait_ge(s.s_cb[k], 16 * (mt + 1))
            if mt >= 2:
                v.wait_ge(s.s_fin, 16 * (mt - 1))
            cb = s.cbuf[mt % 2]
            of = s.outf[mt % 2]
            step(v.tensor_tensor(of[:, :], cb[:, 0, :], cb[:, 1, :], op=ALU.add))
            step(v.tensor_tensor(of[:, :], of[:, :], cb[:, 2, :], op=ALU.add))
            step(v.tensor_tensor(of[:, :], of[:, :], cb[:, 3, :], op=ALU.add))
            step(v.tensor_tensor(of[:, :], of[:, :], s.x2full[:, mt, :], op=ALU.add),
                 f"fin{mt}")
        ev["end"] = vc[0]

    def emit_scalar(sc):
        scc = [0]

        def sstep(ins):
            scc[0] += 1
            ins.then_inc(s.sch, 1)
        for m in range(min(MLIM, NMEGA)):
            x = s.x2full[:, NTILE * m:NTILE * m + NTILE, :]
            sc.wait_ge(s.s_x, 16 * (m + 1))
            sc.wait_ge(s.s_xt, NTILE * m)
            if m > 0:
                sc.wait_ge(s.s_op, NTILE * m)
            sc.wait_ge(s.vch, ev[f"negmean{m}"])
            for t_ in range(NTILE):
                sstep(sc.activation(out=s.xln[:, t_, :], in_=x[:, t_, :], func=AF.Square,
                                    bias=s.lnsc[:, t_:t_ + 1], scale=1.0,
                                    accum_out=s.varsum[:, t_:t_ + 1]))
            sc.wait_ge(s.vch, ev[f"rstd{m}"])
            sstep(sc.activation(out=s.rstd[:, :], in_=s.rstd[:, :], func=AF.Sqrt))
            sc.wait_ge(s.vch, ev[f"bias2{m}"])
            for t_ in range(NTILE):
                sstep(sc.activation(out=s.xln[:, t_, :], in_=x[:, t_, :], func=AF.Identity,
                                    bias=s.bias2[:, t_:t_ + 1], scale=s.rstd[:, t_:t_ + 1]))
            if SUB < 3 or 30 <= SUB <= 31:
                continue
            for b in range(NBLK):
                blk = NBLK * m + b
                sc.wait_ge(s.s_S, blk + 1)
                if blk > 0:
                    sc.wait_ge(s.vch, ev[f"pn{blk - 1}"])
                for j in range(8):
                    bank = s.ps_s[2 * (b % 2) + j // 4]
                    sstep(sc.activation(out=s.phat[:, j, :],
                                        in_=bank[:, 128 * (j % 4):128 * (j % 4) + 128],
                                        func=AF.Exp, scale=0.125,
                                        accum_out=s.sums[:, 8 * b + j:8 * b + j + 1]))
        if PH == 1:
            return
        for tl in range(11):
            if tl < 8:
                N, lo_, hi_ = II, s.shloA[:, tl, :], s.shhiA[:, tl, :]
            else:
                N, lo_, hi_ = D, s.shloB[:, tl - 8, :], s.shhiB[:, tl - 8, :]
            _emit_decode_tile_sc(sc, sstep, s, ev, [t[:, 0:N] for t in s.dtmp],
                                 lo_, hi_, f"ds{tl}")
        for m in range(NMEGA):
            x2 = s.x2full[:, NTILE * m:NTILE * m + NTILE, :]
            sc.wait_ge(s.s_xt, NTILE * NMEGA + NTILE * m)
            sc.wait_ge(s.s_sc2, 16 * m)
            sc.wait_ge(s.vch, ev[f"negmean2_{m}"])
            for t_ in range(NTILE):
                sstep(sc.activation(out=s.y2[:, t_, :], in_=x2[:, t_, :], func=AF.Square,
                                    bias=s.lnsc[:, t_:t_ + 1], scale=1.0,
                                    accum_out=s.varsum[:, t_:t_ + 1]))
            sc.wait_ge(s.vch, ev[f"rstd2_{m}"])
            sstep(sc.activation(out=s.rstd[:, :], in_=s.rstd[:, :], func=AF.Sqrt))
            sc.wait_ge(s.vch, ev[f"bias22_{m}"])
            for t_ in range(NTILE):
                sstep(sc.activation(out=s.y2[:, t_, :], in_=x2[:, t_, :], func=AF.Identity,
                                    bias=s.bias2[:, t_:t_ + 1], scale=s.rstd[:, t_:t_ + 1]))
            for ic in range(6):
                sc.wait_ge(s.s_sh, 14 * m + ic + 1)
                sstep(sc.activation(out=s.hshT[:, ic, :], in_=s.ps_s[0][:, :], func=AF.Silu))
        if PH == 2:
            return
        sc.wait_ge(s.vch, ev["mask01"])
        for t_ in range(NTA):
            sstep(sc.activation(out=s.wfull[:, t_, :], in_=s.ltb[:, t_, :], func=AF.Exp,
                                bias=s.nmx[:, t_:t_ + 1], scale=1.0))
        for e in range(ELOC):
            for tl in range(11):
                if tl < 8:
                    N, lo_, hi_ = II, s.eloA[:, tl, :], s.ehiA[:, tl, :]
                else:
                    N, lo_, hi_ = D, s.eloB[:, tl - 8, :], s.ehiB[:, tl - 8, :]
                _emit_decode_tile_sc(sc, sstep, s, ev, [t[:, 0:N] for t in s.dtmp4],
                                     lo_, hi_, f"de{e}_{tl}")
            for ic in range(6):
                sc.wait_ge(s.s_ex, 16 + 16 * e + ic + 1)
                sstep(sc.activation(out=s.hT[:, ic, :], in_=s.ps_s[0][0:128, 0:MT], func=AF.Silu))

    def emit_tensor(t):
        t.wait_ge(s.s_wl, 16 * NLD0A)
        for m in range(min(MLIM, NMEGA)):
            for tile in range(NTILE):
                t.wait_ge(s.sch, SC_P1 * m + NTILE + 1 + tile + 1)
                if tile > 0:
                    t.wait_ge(s.vch, ev[f"xte{m}_{tile - 1}"])
                elif m > 0:
                    t.wait_ge(s.vch, ev[f"xte{m - 1}_{NTILE - 1}"])
                    t.wait_ge(s.vch, ev[f"pnte{NBLK * m - 1}"])
                ins = None
                for c in range(NCH):
                    ins = t.matmul(s.ps_tr[:, c, :], s.xln[:, tile, 128 * c:128 * c + 128],
                                   s.ident[:, :], is_transpose=True, start=(c == 0), stop=(c == NCH - 1))
                ins.then_inc(s.s_xt, 1)
            if SUB < 2:
                continue
            for c in range(QKVCH):
                t.wait_ge(s.vch, ev[f"xte{m}_{NTILE - 1}"])
                prevc = QKVCH * m + c - 2
                if prevc >= 0:
                    mm_, cc_ = divmod(prevc, QKVCH)
                    t.wait_ge(s.vch, ev[f"qkve{mm_}_{cc_}"])
                if m > 0 and c < 2:
                    t.wait_ge(s.vch, ev[f"x2_{m - 1}_{NTILE - 1}"])
                pb = s.ps_qkv[c % 2]
                ins = None
                for k in range(NCH):
                    ins = t.matmul(pb[:, :], s.wqkv[:, k, 128 * c:128 * c + 128], s.xlnT[:, k, :],
                                   start=(k == 0), stop=(k == NCH - 1))
                ins.then_inc(s.s_qkv, 1)
            if SUB < 3 or SUB == 30:
                continue
            for b in range(NBLK):
                blk = NBLK * m + b
                t.wait_ge(s.s_lay, 16 * LAY * (m + 1))
                if blk >= 2 and SUB < 30:
                    pm, pb_ = divmod(blk - 2, NBLK)
                    t.wait_ge(s.sch, SC_P1 * pm + 9 + 8 * (pb_ + 1))
                    if SUB >= 5:
                        t.wait_ge(s.vch, ev[f"ote{blk - 2}"])
                bank0 = s.ps_s[2 * (b % 2)]
                bank1 = s.ps_s[2 * (b % 2) + 1]
                ins = None
                for half4 in range(2):
                    bank = bank0 if half4 == 0 else bank1
                    for j4 in range(4):
                        t.matmul(bank[:, 128 * j4:128 * j4 + 128], s.maskU[:, :], s.maskV[:, :],
                                 start=(j4 == 0), stop=False)
                    for j4 in range(4):
                        slab = 8 * b + 4 * half4 + j4
                        ins = t.matmul(bank[:, 128 * j4:128 * j4 + 128], s.qP[:, slab, :],
                                       s.kL2[:, :, 8 * slab:8 * slab + 8],
                                       start=False, stop=(j4 == 3))
                ins.then_inc(s.s_S, 1)
                if SUB < 4 or 30 <= SUB <= 31:
                    continue
                t.wait_ge(s.vch, ev[f"pn{blk}"])
                if blk > 0:
                    t.wait_ge(s.vch, ev[f"pnte{blk - 1}"])
                ins = None
                for j in range(8):
                    ins = t.matmul(s.ps_tr[:, j, :], s.phat[:, j, :], s.ident[:, :],
                                   is_transpose=True, start=(j == 0), stop=(j == 7))
                ins.then_inc(s.s_pnt, 1)
                if blk > 0:
                    t.wait_ge(s.vch, ev[f"vste{blk - 1}"])
                t.wait_ge(s.vch, ev[f"qkve{m}_{QKVCH - 1}"])
                for j in range(8):
                    t.matmul(s.ps_vs[0:64, j, :], s.vP[0:64, 8 * b + j, :],
                             s.ident[0:64, 0:64],
                             is_transpose=True, start=(j == 0), stop=(j == 7))
                ins = None
                for j in range(8):
                    ins = t.matmul(s.ps_vs[64:128, j, :], s.vP[64:128, 8 * b + j, :],
                                   s.ident[64:128, 64:128],
                                   is_transpose=True, start=(j == 0), stop=(j == 7))
                ins.then_inc(s.s_vst, 1)
                if SUB < 5:
                    continue
                t.wait_ge(s.vch, ev[f"pnte{blk}"])
                t.wait_ge(s.vch, ev[f"vste{blk}"])
                t.wait_ge(s.sch, SC_P1 * m + 9 + 8 * (b + 1))
                ob_e = s.ps_s[2 * (b % 2)]
                ob_o = s.ps_s[2 * (b % 2) + 1]
                ins = None
                for j in range(8):
                    for par in range(2):
                        ob = ob_e if par == 0 else ob_o
                        out_ap = ob[64 * par:64 * par + 64, 64 * j:64 * j + 64]
                        ins = t.matmul(out_ap, s.vstk[:, j, :],
                                       s.pnT[:, j, 64 * par:64 * par + 64],
                                       start=(j == 0), stop=(j == 7),
                                       tile_position=(0, 64 * par))
                ins.then_inc(s.s_av, 1)
            if SUB < 6:
                continue
            for tile in range(NTILE):
                t.wait_ge(s.vch, ev[f"ote{NBLK * (m + 1) - 1}"])
                if tile >= 1:
                    t.wait_ge(s.vch, ev[f"x2_{m}_{tile - 1}"])
                elif m > 0:
                    t.wait_ge(s.vch, ev[f"x2_{m - 1}_{NTILE - 1}"])
                for c in range(NCH):
                    lh = s.oT[:, c, 128 * tile:128 * tile + 128]
                    t.matmul(s.ps_qkv[0][:, :], lh, s.wout[:, c, 0:512],
                             start=(c == 0), stop=False)
                    t.matmul(s.ps_qkv[1][:, :], lh, s.wout[:, c, 512:1024],
                             start=(c == 0), stop=False)
                t.matmul(s.ps_qkv[0][:, :], s.ones[0:1, :], s.bout[0:1, 0:512],
                         start=False, stop=True)
                t.matmul(s.ps_qkv[1][:, :], s.ones[0:1, :], s.bout[0:1, 512:1024],
                         start=False, stop=True).then_inc(s.s_op, 1)
        if PH == 1:
            return
        for m in range(NMEGA):
            scb = SC_P2B + SC_P2 * m
            for tile in range(NTILE):
                t.wait_ge(s.sch, scb + NTILE + 1 + tile + 1)
                if tile > 0:
                    t.wait_ge(s.vch, ev[f"y2pe{m}_{tile - 1}"])
                elif m > 0:
                    t.wait_ge(s.vch, ev[f"y2pe{m - 1}_{NTILE - 1}"])
                    t.wait_ge(s.vch, ev[f"acc{m - 1}_{NTILE - 1}"])
                else:
                    t.wait_ge(s.vch, ev[f"pnte{NBLK * NMEGA - 1}"])
                ins = None
                for sl8 in range(NCH):
                    kap, par = sl8 // 2, sl8 % 2
                    dsel = s.y2[:, tile, 256 * kap + par:256 * kap + 256:2]
                    ins = t.matmul(s.ps_tr[:, sl8, :], dsel, s.ident[:, :],
                                   is_transpose=True, start=(sl8 == 0), stop=(sl8 == NCH - 1))
                ins.then_inc(s.s_xt, 1)
            t.wait_ge(s.vch, ev[f"y2pe{m}_{NTILE - 1}"])
            if m > 0:
                t.wait_ge(s.vch, ev[f"logTe{m - 1}"])
            else:
                t.wait_ge(s.vch, ev[f"x2_{NMEGA - 1}_{NTILE - 1}"])
            ins = None
            for k in range(NCH):
                ins = t.matmul(s.ps_qkv[0][0:64, :], s.gw[:, k, :], s.y2Tp[:, k, :],
                               start=(k == 0), stop=(k == NCH - 1))
            ins.then_inc(s.s_lg, 1)
            t.wait_ge(s.vch, ev[f"logTe{m}"])
            if m > 0:
                t.wait_ge(s.vch, ev[f"loge{m - 1}"])
            t.wait_ge(s.vch, ev[f"y2pe{m}_{NTILE - 1}"])
            ins = None
            for tile in range(NTILE):
                ins = t.matmul(s.ps_tr[:, tile, 0:64], s.logT[0:64, 128 * tile:128 * tile + 128],
                               s.ident[0:64, 0:64], is_transpose=True,
                               start=(tile == 0), stop=(tile == NTILE - 1))
            ins.then_inc(s.s_lg, 1)
            t.wait_ge(s.vch, ev["shdec"])
            t.wait_ge(s.vch, ev[f"y2pe{m}_{NTILE - 1}"])
            for ic in range(6):
                p2, k2 = ic % 2, ic // 2
                isel = slice(256 * k2 + p2, 256 * k2 + 256, 2)
                if ic > 0:
                    t.wait_ge(s.sch, SC_P2B + SC_P2 * m + 9 + ic)
                    t.wait_ge(s.vch, ev[f"shupe{m}_{ic - 1}"])
                elif m > 0:
                    t.wait_ge(s.sch, SC_P2B + SC_P2 * (m - 1) + 15)
                    t.wait_ge(s.vch, ev[f"shupe{m - 1}_{5}"])
                ins = None
                for sl8 in range(8):
                    kap, par = sl8 // 2, sl8 % 2
                    wt = s.shloA if par == 0 else s.shhiA
                    t.matmul(s.ps_s[0][:, :], wt[:, kap, isel], s.y2Tp[:, sl8, :],
                             start=(sl8 == 0), stop=(sl8 == 7))
                    ins = t.matmul(s.ps_s[1][:, :], wt[:, 4 + kap, isel], s.y2Tp[:, sl8, :],
                                   start=(sl8 == 0), stop=(sl8 == 7))
                ins.then_inc(s.s_sh, 1)
            for dc in range(8):
                pm_prev = 14 * m + 6 + dc - 2
                if pm_prev >= 14 * m + 6:
                    t.wait_ge(s.vch, ev[f"shdne{m}_{dc - 2}"])
                elif m > 0 and dc < 2:
                    t.wait_ge(s.vch, ev[f"shdne{m - 1}_{6 + dc}"])
                t.wait_ge(s.vch, ev[f"shmul{m}_{5}"]) if dc == 0 else None
                pbank = s.ps_s[2] if dc % 2 == 0 else s.ps_s[3]
                ins = None
                for sl6 in range(6):
                    p2, k2 = sl6 % 2, sl6 // 2
                    wt = s.shloB if p2 == 0 else s.shhiB
                    ins = t.matmul(pbank[:, :], wt[:, k2, 128 * dc:128 * dc + 128],
                                   s.hshT[:, sl6, :], start=(sl6 == 0), stop=(sl6 == 5))
                ins.then_inc(s.s_sh, 1)
            for tile in range(NTILE):
                t.wait_ge(s.vch, ev[f"shdne{m}_{7}"])
                t.wait_ge(s.vch, ev[f"loge{m}"])
                if tile > 0:
                    t.wait_ge(s.vch, ev[f"acc{m}_{tile - 1}"])
                ins = None
                for dc in range(8):
                    ins = t.matmul(s.ps_tr[:, dc, :], s.shTs[:, dc, 128 * tile:128 * tile + 128],
                                   s.ident[:, :], is_transpose=True,
                                   start=(dc == 0), stop=(dc == 7))
                ins.then_inc(s.s_lg, 1)
        if PH == 2:
            return
        t.wait_ge(s.vch, ev["mask01"])
        t.wait_ge(s.vch, ev[f"acc{NMEGA - 1}_{NTILE - 1}"])
        t.wait_ge(s.sch, SC_P2B + SC_P2 * (NMEGA - 1) + 15)
        t.wait_ge(s.vch, ev[f"shupe{NMEGA - 1}_5"])
        t.wait_ge(s.vch, ev[f"shdne{NMEGA - 1}_6"])
        t.wait_ge(s.vch, ev[f"shdne{NMEGA - 1}_7"])
        ins = None
        for t_ in range(NTA):
            ins = t.matmul(s.ps_s[0][0:1, 0:E], s.onesf[:, :], s.mask01[:, t_, :],
                           start=(t_ == 0), stop=(t_ == NTA - 1))
        ins.then_inc(s.s_cnt, 1)
        t.wait_ge(s.s_gth, 16 * 2)
        t.matmul(s.ps_s[1][0:1, 0:E], s.stepm[:, :], s.agc[:, :],
                 start=True, stop=True).then_inc(s.s_cnt, 1)
        t.wait_ge(s.vch, ev["basev"])
        t.wait_ge(s.s_wl, 16 * NLD0)
        for t_ in range(NTA):
            t.matmul(s.ps_s[3][0:NTA, 0:E], s.su32sb[:, t_, :], s.mask01[:, t_, :],
                     start=(t_ == 0), stop=False)
        t.matmul(s.ps_s[3][0:NTA, 0:E], s.onesc[0:1, :], s.baserow[0:1, :],
                 start=False, stop=True).then_inc(s.s_cnt, 1)
        t.wait_ge(s.vch, ev["cntev"])
        for t_ in range(NTA):
            if t_ >= 2:
                t.wait_ge(s.vch, ev[f"posr{t_ - 2}"])
            pb = s.ps_s[t_ % 2]
            t.matmul(pb[0:128, 0:E], s.lstr[:, :], s.mask01[:, t_, :],
                     start=True, stop=True).then_inc(s.s_cnt, 1)
        t.wait_ge(s.s_gth, 16 * 18)
        t.wait_ge(s.vch, ev["srcm"])
        for ct in range(16):
            if ct >= 1:
                t.wait_ge(s.vch, ev[f"btev{ct - 1}"])
            ins = None
            for slab in range(8):
                kap, par = slab // 2, slab % 2
                ins = t.matmul(s.ps_tr[:, slab, :],
                               s.buf[:, ct, 256 * kap + par:256 * kap + 256:2],
                               s.ident[:, :], is_transpose=True,
                               start=(slab == 0), stop=(slab == 7))
            ins.then_inc(s.s_ex, 1)
        for e in range(ELOC):
            exb = 16 + 16 * e
            t.wait_ge(s.vch, ev[f"edec{e}"])
            t.wait_ge(s.vch, ev[f"btev{15}"])
            for ic in range(6):
                p2, k2 = ic % 2, ic // 2
                isel = slice(256 * k2 + p2, 256 * k2 + 256, 2)
                if e > 0 or ic > 0:
                    pe_, pic = divmod(6 * e + ic - 1, 6)
                    t.wait_ge(s.sch, SC_P4B + SC_PE * pe_ + 11 * 8 + pic + 1)
                    t.wait_ge(s.vch, ev[f"hup{pe_}_{pic}"])
                ins = None
                for slab in range(8):
                    kap, par = slab // 2, slab % 2
                    wt = s.eloA if par == 0 else s.ehiA
                    t.matmul(s.ps_s[0][0:128, 0:MT], wt[:, kap, isel],
                             s.bufT[:, slab, MT * e:MT * e + MT],
                             start=(slab == 0), stop=(slab == 7))
                    ins = t.matmul(s.ps_s[1][0:128, 0:MT], wt[:, 4 + kap, isel],
                                   s.bufT[:, slab, MT * e:MT * e + MT],
                                   start=(slab == 0), stop=(slab == 7))
                ins.then_inc(s.s_ex, 1)
            for dc in range(8):
                if e > 0 or dc >= 2:
                    pe_, pdc = divmod(8 * e + dc - 2, 8)
                    t.wait_ge(s.vch, ev[f"eoe{pe_}_{pdc}"])
                if dc == 0:
                    t.wait_ge(s.vch, ev[f"hmul{e}_{5}"])
                pbank = s.ps_s[2] if dc % 2 == 0 else s.ps_s[3]
                ins = None
                for sl6 in range(6):
                    p2, k2 = sl6 % 2, sl6 // 2
                    wt = s.eloB if p2 == 0 else s.ehiB
                    ins = t.matmul(pbank[0:128, 0:MT], wt[:, k2, 128 * dc:128 * dc + 128],
                                   s.hT[:, sl6, :], start=(sl6 == 0), stop=(sl6 == 5))
                ins.then_inc(s.s_ex, 1)
            for grp in range(2):
                t.wait_ge(s.vch, ev[f"eoe{e}_{7}"])
                if e > 0 or grp > 0:
                    t.wait_ge(s.vch, ev[f"eor{e - 1 if grp == 0 else e}_{1 if grp == 0 else 0}"])
                ins = None
                for dc in range(8):
                    ins = t.matmul(s.ps_tr[:, dc, :], s.eoTs[:, dc, 128 * grp:128 * grp + 128],
                                   s.ident[:, :], is_transpose=True,
                                   start=(dc == 0), stop=(dc == 7))
                ins.then_inc(s.s_ex, 1)


    def emit_gpsimd(g):
        W = s.s_wl
        for k in range(NCH):
            g.dma_start(out=s.wqkv[:, k, :], in_=DI["wq"][128 * k:128 * k + 128, :]).then_inc(W, 16)
            g.dma_start(out=s.wout[:, k, :], in_=DI["wo"][128 * k:128 * k + 128, :]).then_inc(W, 16)
        g.dma_start(out=s.bqkvP[:, :], in_=DI["bq"][:, :]).then_inc(W, 16)
        g.dma_start(out=s.bout[:, :], in_=DI["bo"][None, :]).then_inc(W, 16)
        g.dma_start(out=s.ident[:, :], in_=DI["idn"][:, :]).then_inc(W, 16)
        g.dma_start(out=s.maskU[:, :], in_=DI["mU"][:, :]).then_inc(W, 16)
        g.dma_start(out=s.maskV[:, :], in_=DI["mV"][:, :]).then_inc(W, 16)
        g.dma_start(out=s.stepm[:, :], in_=DI["stepmask"][:, :]).then_inc(W, 16)
        g.dma_start(out=s.shardi[:, :], in_=DI["coreid"][None, :].broadcast_to([128, 1])).then_inc(W, 16)
        for m in range(min(MLIM, NMEGA)):
            t0 = m * MEGA
            xv = DI["x"][t0:t0 + MEGA, :].rearrange("(a p) d -> p a d", p=128)
            g.dma_start(out=s.x2full[:, NTILE * m:NTILE * m + NTILE, :], in_=xv).then_inc(s.s_x, 16)
            if m > 0:
                g.wait_ge(s.s_S, NBLK * m)
            if SUB < 3:
                continue
            L = s.s_lay
            lb = 16 * LAY * m
            g.dma_start(out=s.kL2[:, :, :],
                        in_=DI["peT"][:, None, t0:t0 + MEGA].broadcast_to(
                            [64, 16, MEGA])).then_inc(L, 16)
            g.wait_ge(L, lb + 16)
            g.wait_ge(s.vch, ev[f"qkve{m}_{15}"])
            for c16 in range(16):
                g.dma_start(out=s.kL2[:, c16, :], in_=s.kst[:, c16, :],
                            accum_op=ALU.add).then_inc(L, 16)
        if PH == 1:
            if SUB >= 30:
                if SUB == 31:
                    g.wait_ge(s.s_S, NBLK * min(MLIM, NMEGA))
                else:
                    g.wait_ge(s.s_lay, 16 * LAY * min(MLIM, NMEGA))
                subev = None
            else:
                subev = {1: "xte{}_3", 2: "qkve{}_23", 3: "pn##", 4: "vste##", 5: "ote##",
                         6: "x2_{}_3"}[min(SUB, 6)]
            for m in range(min(MLIM, NMEGA)):
                if subev is None:
                    pass
                elif "##" in subev:
                    g.wait_ge(s.vch, ev[subev.replace("##", str(NBLK * (m + 1) - 1))])
                else:
                    g.wait_ge(s.vch, ev[subev.format(m)])
                if m > 0:
                    g.wait_ge(s.s_fin, 16 * m)
                ovx = out_d[MEGA * m:MEGA * m + MEGA, :].rearrange("(a p) d -> p a d", p=128)
                g.dma_start(out=ovx,
                            in_=s.x2full[:, NTILE * m:NTILE * m + NTILE, :]).then_inc(s.s_fin, 16)
            g.wait_ge(s.s_fin, 16 * min(MLIM, NMEGA))
            return
        g.wait_ge(s.s_qkv, QKVCH * NMEGA)
        g.wait_ge(s.s_op, NTILE * NMEGA)
        g.dma_start(out=s.gw[:, :, :], in_=DI["gwT"].rearrange("(k p) e -> p k e", p=128)).then_inc(W, 16)
        g.dma_start(out=s.shwordsA[:, 0:4, :],
                    in_=DI["shg"].rearrange("(k p) i -> p k i", p=128)).then_inc(W, 16)
        g.dma_start(out=s.shwordsA[:, 4:8, :],
                    in_=DI["shu"].rearrange("(k p) i -> p k i", p=128)).then_inc(W, 16)
        g.dma_start(out=s.shwordsB[:, :, :],
                    in_=DI["shd"].rearrange("(k p) i -> p k i", p=128)).then_inc(W, 16)
        for k in range(4):
            for h in range(2):
                g.dma_start(out=s.shsctA[64 * h:64 * h + 64, k, :],
                            in_=DI["scA"][k, h, 16, None, :].broadcast_to([64, II])).then_inc(W, 16)
                g.dma_start(out=s.shsctA[64 * h:64 * h + 64, 4 + k, :],
                            in_=DI["scA"][k, h, 17, None, :].broadcast_to([64, II])).then_inc(W, 16)
        for k in range(3):
            for h in range(2):
                g.dma_start(out=s.shsctB[64 * h:64 * h + 64, k, :],
                            in_=DI["scB"][k, h, 8, None, :].broadcast_to([64, D])).then_inc(W, 16)

        def eload(e):
            g.dma_start(out=s.ewordsA[:, 0:4, :],
                        in_=DI["gup"][e].rearrange("(k p) i -> p k i", p=128)).then_inc(s.s_eW, 16)
            g.dma_start(out=s.ewordsA[:, 4:8, :],
                        in_=DI["uup"][e].rearrange("(k p) i -> p k i", p=128)).then_inc(s.s_eW, 16)
            g.dma_start(out=s.ewordsB[:, :, :],
                        in_=DI["dnp"][e].rearrange("(k p) i -> p k i", p=128)).then_inc(s.s_eW, 16)
            for k in range(4):
                for h in range(2):
                    g.dma_start(out=s.esctA[64 * h:64 * h + 64, k, :],
                                in_=DI["scA"][k, h, 2 * e, None, :].broadcast_to([64, II])).then_inc(s.s_eW, 16)
                    g.dma_start(out=s.esctA[64 * h:64 * h + 64, 4 + k, :],
                                in_=DI["scA"][k, h, 2 * e + 1, None, :].broadcast_to([64, II])).then_inc(s.s_eW, 16)
            for k in range(3):
                for h in range(2):
                    g.dma_start(out=s.esctB[64 * h:64 * h + 64, k, :],
                                in_=DI["scB"][k, h, e, None, :].broadcast_to([64, D])).then_inc(s.s_eW, 16)
        for m in range(NMEGA):
            g.wait_ge(s.s_sc2, 16 * (m + 1))
            sl = slice(MEGA * D * m, MEGA * D * (m + 1))
            ob2 = 8 * MEGA * D * m
            g.collective_compute("AllGather", ALU.bypass,
                                 replica_groups=[[0, 1, 2, 3, 4, 5, 6, 7]],
                                 ins=[ag2_in[sl]],
                                 outs=[ag2_out[ob2:ob2 + 8 * MEGA * D]]).then_inc(s.s_ag2, 1)
        g.wait_ge(s.vch, ev[f"shdne{NMEGA - 1}_{7}"])
        g.wait_ge(s.s_lg, 6 * NMEGA)
        g.dma_start(out=s.lstr[:, :], in_=DI["lstrict"][:, :]).then_inc(W, 16)
        g.dma_start(out=s.su32sb[:, :, :],
                    in_=DI["su32"].rearrange("(o a) e -> o a e", o=1).broadcast_to(
                        [128, NTA, NTA])).then_inc(W, 16)
        g.dma_start(out=s.evecb[:, :], in_=DI["evec"][None, :].broadcast_to([128, E])).then_inc(W, 16)
        g.dma_start(out=s.etb[:, :], in_=DI["etrash"][None, :].broadcast_to([128, E])).then_inc(W, 16)
        g.dma_start(out=s.eiob[:, :], in_=DI["eiota"][None, :].broadcast_to([128, E])).then_inc(W, 16)
        g.dma_start(out=s.tvecs[:, :], in_=DI["tvec"][:, :]).then_inc(W, 16)
        g.dma_start(out=s.tlv[:, :], in_=DI["tloc"][:, :]).then_inc(W, 16)
        g.dma_start(out=s.evec2b[:, :], in_=DI["evec2"][None, :].broadcast_to([128, E])).then_inc(W, 16)
        g.dma_start(out=s.etb2[:, :], in_=DI["etrash2"][None, :].broadcast_to([128, E])).then_inc(W, 16)
        if PH == 2:
            for m in range(NMEGA):
                g.wait_ge(s.vch, ev[f"acc{m}_{NTILE - 1}"])
                if m > 0:
                    g.wait_ge(s.s_fin, 16 * m)
                ovx = out_d[MEGA * m:MEGA * m + MEGA, :].rearrange("(a p) d -> p a d", p=128)
                g.dma_start(out=ovx,
                            in_=s.x2full[:, NTILE * m:NTILE * m + NTILE, :]).then_inc(s.s_fin, 16)
            g.wait_ge(s.s_fin, 16 * NMEGA)
            return
        g.wait_ge(s.vch, ev["cntev"])
        g.dma_start(out=ag1_in[None, :], in_=s.countrow[0:1, :]).then_inc(s.s_gth, 16)
        g.wait_ge(s.s_gth, 16)
        g.collective_compute("AllGather", ALU.bypass, replica_groups=[[0, 1, 2, 3, 4, 5, 6, 7]],
                             ins=[ag1_in[:]], outs=[ag1_out[:, :]]).then_inc(s.s_ag, 1)
        g.wait_ge(s.s_ag, 1)
        g.dma_start(out=s.agc[:, :], in_=ag1_out[:, :]).then_inc(s.s_gth, 16)
        g.wait_ge(s.vch, ev["capm"])
        g.dma_start(out=rb_d.rearrange("(a e) -> a e", e=E), in_=s.rbsb[:, :]).then_inc(s.s_rb, 16)
        g.wait_ge(s.s_rb, 16)
        g.dma_start(out=s.capall.rearrange("p a e -> p (a e)"),
                    in_=rb_d[None, :].broadcast_to([128, NTA * E])).then_inc(s.s_rb, 16)
        g.wait_ge(s.vch, ev["rks"])
        g.wait_ge(s.vch, ev["srcm"])
        g.wait_ge(s.s_az, 16 * 17)
        for j in range(128):
            if j >= 8:
                g.wait_ge(s.s_ms[j % 8], 16 * (j // 8))
            mv = m_sub[j % 8].rearrange("(r c) -> r c", c=4)
            g.indirect_dma_start(
                out=mv, out_offset=bass.IndirectOffsetOnAxis(ap=s.RkS32[:, j:j + 1], axis=0),
                in_=s.srcM[:, j, :], in_offset=None).then_inc(s.s_ms[j % 8], 16)
        for q in range(8):
            g.wait_ge(s.s_ms[q], 16 * 16)
            if q >= 1:
                g.wait_ge(s.s_m, 16 * 2 * q)
            if q >= 3:
                g.wait_ge(s.vch, ev[f"mrg{q - 2}"])
            dstm = s.macc if q == 0 else s.mtmp[(q - 1) % 2]
            dstt = s.macct if q == 0 else s.mtmpt[(q - 1) % 2]
            g.dma_start(out=dstm[:, :],
                        in_=m_sub[q][0:65536].rearrange("(p x) -> p x", p=128)).then_inc(s.s_m, 16)
            g.dma_start(out=dstt[:, :],
                        in_=m_sub[q][65536:MREC * 4].rearrange("(r c) -> r c", c=4)).then_inc(s.s_m, 16)
        g.wait_ge(s.vch, ev["mrg7"])
        g.wait_ge(s.s_m, 16 * 16)
        g.dma_start(out=m_in[0:65536].rearrange("(p x) -> p x", p=128),
                    in_=s.macc[:, :]).then_inc(s.s_m, 16)
        g.wait_ge(s.s_m, 16 * 17)
        g.dma_start(out=m_in[65536:MREC * 4].rearrange("(r c) -> r c", c=4),
                    in_=s.macct[:, :]).then_inc(s.s_m, 16)
        g.wait_ge(s.s_m, 16 * 18)
        g.collective_compute("ReduceScatter", ALU.add, replica_groups=[[0, 1, 2, 3, 4, 5, 6, 7]],
                             ins=[m_in[0:MREC * 4]], outs=[m_my[:]]).then_inc(s.s_ag, 1)
        g.wait_ge(s.s_ag, 2)
        g.collective_compute("AllGather", ALU.bypass, replica_groups=[[0, 1, 2, 3, 4, 5, 6, 7]],
                             ins=[m_my[:]], outs=[m_all[:, :]]).then_inc(s.s_ag, 1)
        mm4 = m_my.rearrange("(r c) -> r c", c=4)
        g.dma_start(out=s.gtf[:, :],
                    in_=mm4[0:NROWS, 0:1].rearrange("(ct p) o -> p (ct o)", p=128),
                    allow_slow_non_contiguous=True).then_inc(s.s_m, 16)
        g.wait_ge(s.s_m, 16 * 19)
        g.dma_start(out=s.galn[:, :],
                    in_=mm4[0:NROWS, 1:2].rearrange("(ct p) o -> p (ct o)", p=128),
                    allow_slow_non_contiguous=True).then_inc(s.s_m, 16)
        g.wait_ge(s.vch, ev["gidxc"])
        g.wait_ge(s.s_ag2, NMEGA)
        ag2v = ag2_out.rearrange("(t d) -> t d", d=D)
        for ct in range(16):
            g.indirect_dma_start(
                out=s.buf[:, ct, :], out_offset=None, in_=ag2v,
                in_offset=bass.IndirectOffsetOnAxis(
                    ap=s.gidx32[:, ct:ct + 1], axis=0)).then_inc(s.s_gth, 16)
        for e in range(ELOC):
            if e > 0:
                g.wait_ge(s.vch, ev[f"edec{e - 1}"])
            eload(e)
        g.wait_ge(s.s_az, 16 * 17)
        for c4 in range(4):
            g.wait_ge(s.s_eod, 16 * 2 * (c4 + 1))
            lo4 = 512 * D * c4
            hi4 = (512 * D * (c4 + 1)) if c4 < 3 else (MROW * D)
            ob = 8 * 512 * D * c4
            g.collective_compute("AllGather", ALU.bypass,
                                 replica_groups=[[0, 1, 2, 3, 4, 5, 6, 7]],
                                 ins=[ag4_in[lo4:hi4]],
                                 outs=[ag4_out[ob:ob + 8 * (hi4 - lo4)]]).then_inc(s.s_ag4, 1)
        g.wait_ge(s.s_ag4, 4)
        ag4f = ag4_out.rearrange("(r d) -> r d", d=D)
        for mt in range(NTA):
            if mt >= 2:
                g.wait_ge(s.vch, ev[f"fin{mt - 2}"])
            for k in range(4):
                if mt >= 1:
                    g.wait_ge(s.s_cb[k], 16 * mt)
                g.indirect_dma_start(
                    out=s.cbuf[mt % 2][:, k, :], out_offset=None, in_=ag4f,
                    in_offset=bass.IndirectOffsetOnAxis(
                        ap=s.RkC32[:, NTA * k + mt:NTA * k + mt + 1],
                        axis=0)).then_inc(s.s_cb[k], 16)
        if DBG:
            g.dma_start(out=mmy_d[:], in_=m_my[:]).then_inc(s.s_dbg2, 16)
            g.wait_ge(s.s_dbg2, 16)
        g.wait_ge(s.s_fin, 16 * NTA)

    def emit_sync(sy):
        if PH == 1:
            return
        sy.wait_ge(s.vch, ev["init0"])
        for q in range(8):
            sy.dma_start(out=m_sub[q][0:65536].rearrange("(p a c) -> p a c", p=128, c=E),
                         in_=s.zf[:, None, :].broadcast_to([128, 8, E])).then_inc(s.s_az, 16)
            sy.dma_start(out=m_sub[q][65536:MREC * 4].rearrange("(r c) -> r c", c=4),
                         in_=s.zf[0:(MREC * 4 - 65536) // 4, 0:4]).then_inc(s.s_az, 16)
        ag4v = ag4_in.rearrange("(c d) -> c d", d=D)
        sy.dma_start(out=ag4v[NROWS:MROW, :], in_=s.zt[0:1, :]).then_inc(s.s_az, 16)
        if DBG:
            for m in range(NMEGA):
                sy.wait_ge(s.vch, ev[f"x2_{m}_{NTILE - 1}"])
                sy.wait_ge(s.s_dbg, 16 * m)
                xp = x2p1_d[MEGA * m:MEGA * m + MEGA, :].rearrange("(a p) d -> p a d", p=128)
                sy.dma_start(out=xp,
                             in_=s.x2full[:, NTILE * m:NTILE * m + NTILE, :]).then_inc(s.s_dbg, 16)
        ag2vw = ag2_in.rearrange("(t d) -> t d", d=D)
        for m in range(NMEGA):
            sy.wait_ge(s.sch, SC_P2B + SC_P2 * m + 9)
            sy.wait_ge(s.s_sc2, 16 * m)
            sy.dma_start(out=ag2vw[MEGA * m:MEGA * m + MEGA, :].rearrange("(a p) d -> p a d", p=128),
                         in_=s.y2[:, :, :]).then_inc(s.s_sc2, 16)
        if PH == 2:
            return
        for e in range(ELOC):
            sy.wait_ge(s.vch, ev[f"eor{e}_1"])
            sy.wait_ge(s.s_eod, 16 * e)
            sy.dma_start(out=ag4v[MT * e:MT * e + MT, :].rearrange("(f p) d -> p f d", p=128),
                         in_=s.eor[:, :, :]).then_inc(s.s_eod, 16)
        for mt in range(NTA):
            sy.wait_ge(s.vch, ev[f"fin{mt}"])
            sy.wait_ge(s.s_fin, 16 * mt)
            ov = out_d[128 * mt:128 * mt + 128, :]
            sy.dma_start(out=ov, in_=s.outf[mt % 2][:, :]).then_inc(s.s_fin, 16)
        sy.wait_ge(s.s_fin, 16 * NTA)
        if DBG:
            dv = dbg_d
            sy.wait_ge(s.s_dbg, 16 * NMEGA)
            sy.dma_start(out=dbgb_d[:, 0:2048],
                         in_=s.logits[:, :, :].rearrange("p a e -> p (a e)")).then_inc(s.s_dbg, 16)
            sy.wait_ge(s.s_dbg, 16 * (NMEGA + 1))
            sy.dma_start(out=dv[:, 2048:2064], in_=s.galn[:, :]).then_inc(s.s_dbg, 16)
            sy.wait_ge(s.s_dbg, 16 * (NMEGA + 2))
            sy.dma_start(out=dv[:, 2304:2320], in_=s.galn[:, :]).then_inc(s.s_dbg, 16)
            sy.wait_ge(s.s_dbg, 16 * (NMEGA + 3))
            sy.dma_start(out=dbgu_d[:, 0:128],
                         in_=s.RkC32[:, :]).then_inc(s.s_dbg, 16)
            sy.wait_ge(s.s_dbg, 16 * (NMEGA + 4))
            sy.dma_start(out=dv[:, 2688:2704], in_=s.galn[:, :]).then_inc(s.s_dbg, 16)
            sy.wait_ge(s.s_dbg, 16 * (NMEGA + 5))
            sy.dma_start(out=dbgu_d[:, 128:144], in_=s.gidx32[:, :]).then_inc(s.s_dbg, 16)
            sy.wait_ge(s.s_dbg, 16 * (NMEGA + 6))

    with nc.Block() as block:
        @block.vector
        def _(v):
            emit_vector(v)

        @block.scalar
        def _(sc):
            emit_scalar(sc)

        @block.tensor
        def _(t):
            emit_tensor(t)

        @block.gpsimd
        def _(g):
            emit_gpsimd(g)

        @block.sync
        def _(sy):
            emit_sync(sy)

    stk.close()
    return nc


_NC_CACHE = {}


def kernel(**inputs):
    if "nc" not in _NC_CACHE:
        nc_ = build_nc()
        lov.lower_extended_insts(nc_)
        _NC_CACHE["nc"] = nc_
    nc = _NC_CACHE["nc"]
    in_maps = []
    names = [t[0] for t in _IN_SPECS]
    for c in range(NCORE):
        H = _host_prep(c, inputs)
        in_maps.append({k: np.ascontiguousarray(H[k]) for k in names})
    res = run_bass_kernel_spmd(nc, in_maps, core_ids=list(range(NCORE)))
    outs = [np.asarray(res.results[c]["out"], dtype=np.float32) for c in range(NCORE)]
    return np.concatenate(outs, axis=0).reshape(2, 16384, 1024)


# version 12
# speedup vs baseline: 1.0311x; 1.0311x over previous
import numpy as np
import ml_dtypes
from contextlib import ExitStack
import concourse.bass as bass
import concourse.mybir as mybir
import concourse.library_overlay as lov
from concourse.bass_utils import run_bass_kernel_spmd

F32 = mybir.dt.float32
BF16 = mybir.dt.bfloat16
I16 = mybir.dt.int16
I32 = mybir.dt.int32
U16 = mybir.dt.uint16
U32 = mybir.dt.uint32
AF = mybir.ActivationFunctionType
ALU = mybir.AluOpType
AX = mybir.AxisListType

NCORE = 8
TC = 4096
D = 1024
NCH = 8
QKVCH = 24
MEGA = 512
NMEGA = TC // MEGA
NTILE = 4
NTA = TC // 128
NBLK = 8
NSLAB = 64
EPS = 1e-5
E = 64
ELOC = 8
APS = 5
CAP = 256
MT = 256
NROWS = ELOC * MT
MFD = 10368
II = 768
TB = 32768
LAY = 17

SC_P1 = 73
SC_P2 = 15
SC_P2B = SC_P1 * NMEGA
SC_P3B = SC_P2B + SC_P2 * NMEGA
SC_P4B = SC_P3B + NTA
SC_PE = 6
MROW = ELOC * CAP + 1
MREC = 8 * MROW
NREC = 4 * TC
NLD0A = 23
NLD1 = 49
NLD0 = 58
DBG = False
PH = 99
MLIM = 99
SUB = 99
NLDE = 25


def _host_prep(core, inp):
    H = {}
    t0 = core * TC
    perm = np.zeros(3072, np.int64)
    for c in range(16):
        u = 2 * (c % 8) + c // 8
        for p in range(128):
            perm[128 * c + p] = 192 * u + p
    for cv in range(8):
        c = 16 + cv
        for p in range(128):
            g = 2 * cv + (1 if p >= 64 else 0)
            perm[128 * c + p] = 192 * g + 128 + (p % 64)
    H["wq"] = np.ascontiguousarray(inp["L0_attn_w_qkv"][:, perm])
    H["bq"] = np.ascontiguousarray(inp["L0_attn_b_qkv"][perm].reshape(24, 128).T)
    H["wo"] = np.ascontiguousarray(inp["L0_attn_w_out"])
    H["bo"] = inp["L0_attn_b_out"]
    H["su32"] = np.ascontiguousarray(
        (np.arange(NTA)[:, None] < np.arange(NTA)[None, :]).astype(np.float32))
    pos = (t0 + np.arange(TC)) % 16384
    div = np.exp(np.arange(0, 64, 2, dtype=np.float32) * (-np.log(10000.0) / 64.0))
    ang = pos[:, None].astype(np.float32) * div[None, :]
    pe = np.zeros((TC, 64), np.float32)
    pe[:, 0::2] = np.sin(ang)
    pe[:, 1::2] = np.cos(ang)
    H["peT"] = np.ascontiguousarray(pe.T)
    R = 1e9
    mU = np.zeros((9, 128), np.float32)
    mV = np.zeros((9, 128), np.float32)
    mU[0, :] = -R
    mV[0, :] = R
    for tau in range(8):
        for hp in range(2):
            for hh in range(8):
                mU[1 + tau, hp * 64 + hh * 8 + tau] = R
        for g in range(16):
            mV[1 + tau, g * 8 + tau] = R
    H["mU"], H["mV"] = mU, mV
    H["idn"] = np.eye(128, dtype=np.float32)
    H["lstrict"] = np.ascontiguousarray(
        (np.arange(128)[:, None] < np.arange(128)[None, :]).astype(np.float32))
    porder = np.zeros(D, np.int64)
    for sl8 in range(8):
        kap, par = sl8 // 2, sl8 % 2
        porder[128 * sl8:128 * sl8 + 128] = 256 * kap + par + 2 * np.arange(128)
    H["gwT"] = np.ascontiguousarray(inp["L0_ffn_gate_w"].T[porder])
    es = slice(core * ELOC, core * ELOC + ELOC)
    H["gup"] = np.ascontiguousarray(inp["L0_ffn_gate_packed"][es].transpose(0, 2, 1))
    H["uup"] = np.ascontiguousarray(inp["L0_ffn_up_packed"][es].transpose(0, 2, 1))
    H["dnp"] = np.ascontiguousarray(inp["L0_ffn_down_packed"][es].transpose(0, 2, 1))
    H["shg"] = np.ascontiguousarray(inp["L0_ffn_sh_gate_packed"].T)
    H["shu"] = np.ascontiguousarray(inp["L0_ffn_sh_up_packed"].T)
    H["shd"] = np.ascontiguousarray(inp["L0_ffn_sh_down_packed"].T)
    gs = inp["L0_ffn_gate_scales"][es]
    us = inp["L0_ffn_up_scales"][es]
    ds = inp["L0_ffn_down_scales"][es]
    scA = np.zeros((4, 2, 18, II), np.float32)
    scB = np.zeros((3, 2, 9, D), np.float32)
    for k in range(4):
        for h in range(2):
            for e in range(ELOC):
                scA[k, h, 2 * e + 0] = gs[e, :, 2 * k + h]
                scA[k, h, 2 * e + 1] = us[e, :, 2 * k + h]
            scA[k, h, 16] = inp["L0_ffn_sh_gate_scales"][:, 2 * k + h]
            scA[k, h, 17] = inp["L0_ffn_sh_up_scales"][:, 2 * k + h]
    for k in range(3):
        for h in range(2):
            for e in range(ELOC):
                scB[k, h, e] = ds[e, :, 2 * k + h]
            scB[k, h, 8] = inp["L0_ffn_sh_down_scales"][:, 2 * k + h]
    H["scA"], H["scB"] = scA * 0.5, scB * 0.5
    sm = np.zeros((8, 1), np.float32)
    sm[:core, 0] = 1.0
    H["stepmask"] = sm
    H["coreid"] = np.array([core], np.int32)
    ee = np.arange(E)
    H["evec"] = (MROW * (ee // ELOC) + 256 * (ee % ELOC) + CAP).astype(np.float32)
    H["etrash"] = (MROW * (ee // ELOC) + CAP * ELOC).astype(np.float32)
    H["eiota"] = ee.astype(np.float32)
    tt = (t0 + 128 * np.arange(NTA)[None, :] + np.arange(128)[:, None]).astype(np.int64)
    H["tvec"] = (4096 * ((tt % 4096) // 512) + 512 * (tt // 4096)
                 + (tt % 512)).astype(np.float32)
    H["tloc"] = (128.0 * np.arange(NTA)[None, :]
                 + np.arange(128)[:, None]).astype(np.float32)
    cc_ = (ee % ELOC) // 2
    rr_ = ee // ELOC
    rows_c = np.where(cc_ < 3, 512, 513)
    cbase = np.where(cc_ < 3, 4096 * cc_, 12288)
    H["evec2"] = (cbase + rows_c * rr_ + 256 * ((ee % ELOC) % 2) + CAP).astype(np.float32)
    H["etrash2"] = (12288.0 + 513 * rr_ + 512).astype(np.float32)
    H["x"] = np.ascontiguousarray(inp["L0_attn_x"].reshape(TB, D)[t0:t0 + TC])
    return H


_IN_SPECS = [
    ("x", [TC, D], F32), ("wq", [D, 3072], F32), ("bq", [128, 24], F32),
    ("wo", [D, D], F32), ("bo", [D], F32), ("su32", [NTA, NTA], F32),
    ("peT", [64, TC], F32),
    ("mU", [9, 128], F32), ("mV", [9, 128], F32), ("idn", [128, 128], F32),
    ("lstrict", [128, 128], F32), ("gwT", [D, E], F32),
    ("gup", [ELOC, 512, II], I32), ("uup", [ELOC, 512, II], I32),
    ("dnp", [ELOC, 384, D], I32), ("shg", [512, II], I32), ("shu", [512, II], I32),
    ("shd", [384, D], I32), ("scA", [4, 2, 18, II], F32), ("scB", [3, 2, 9, D], F32),
    ("stepmask", [8, 1], F32), ("coreid", [1], I32),
    ("evec", [E], F32), ("etrash", [E], F32), ("eiota", [E], F32),
    ("tvec", [128, NTA], F32), ("tloc", [128, NTA], F32),
    ("evec2", [E], F32), ("etrash2", [E], F32),
]


def _emit_decode_tile(v, step, w, tmps, lo, hi, st):
    a_l, a_h, ts_, t0f, t1f, af = tmps
    v.tensor_scalar(a_l, w, 7, None, op0=ALU.bitwise_and)
    v.tensor_scalar(a_h, w, 4, 7, op0=ALU.logical_shift_right, op1=ALU.bitwise_and)
    step(v.tensor_scalar(ts_, w, 3, 1, op0=ALU.logical_shift_right, op1=ALU.bitwise_and))
    step(v.tensor_scalar(t0f, ts_, -2.0, 1.0, op0=ALU.mult, op1=ALU.add))
    step(v.tensor_scalar(ts_, w, 7, 1, op0=ALU.logical_shift_right, op1=ALU.bitwise_and))
    v.tensor_scalar(t1f, ts_, -2.0, 1.0, op0=ALU.mult, op1=ALU.add)
    v.tensor_scalar(lo, a_l, 2.0, -4.0, op0=ALU.mult, op1=ALU.add)
    step(v.tensor_scalar(af, a_l, 4.0, -16.0, op0=ALU.mult, op1=ALU.add))
    step(v.tensor_tensor(lo, lo, af, op=ALU.max))
    step(v.tensor_scalar(af, a_l, 1.0, None, op0=ALU.mult))
    step(v.tensor_tensor(lo, lo, af, op=ALU.max))
    v.tensor_tensor(lo, lo, t0f, op=ALU.mult)
    v.tensor_scalar(hi, a_h, 2.0, -4.0, op0=ALU.mult, op1=ALU.add)
    step(v.tensor_scalar(af, a_h, 4.0, -16.0, op0=ALU.mult, op1=ALU.add))
    step(v.tensor_tensor(hi, hi, af, op=ALU.max))
    step(v.tensor_scalar(af, a_h, 1.0, None, op0=ALU.mult))
    step(v.tensor_tensor(hi, hi, af, op=ALU.max))
    step(v.tensor_tensor(hi, hi, t1f, op=ALU.mult))
    v.tensor_tensor(lo, lo, st, op=ALU.mult)
    step(v.tensor_tensor(hi, hi, st, op=ALU.mult))


def build_nc():
    nc = bass.Bass("TRN2", target_bir_lowering=False, debug=False, num_swdge_queues=4)
    d = nc.dram_tensor
    DI = {}
    for name, shape, dt_ in _IN_SPECS:
        DI[name] = d(name, shape, dt_, kind="ExternalInput").ap()
    out_d = d("out", [TC, D], F32, kind="ExternalOutput").ap()
    if DBG:
        x2p1_d = d("x2p1", [TC, D], BF16, kind="ExternalOutput").ap()
        dbg_d = d("dbg", [128, 4096], F32, kind="ExternalOutput").ap()
        mmy_d = d("mmy", [MROW * 4], F32, kind="ExternalOutput").ap()
        dbgb_d = d("dbgb", [128, 2048], BF16, kind="ExternalOutput").ap()
        dbgu_d = d("dbgu", [128, 144], U32, kind="ExternalOutput").ap()

    ag1_in = d("ag1_in", [E], F32, kind="Internal").ap()
    ag1_out = d("ag1_out", [8, E], F32, kind="Internal").ap()
    ag2_in = d("ag2_in", [TC * D], BF16, kind="Internal").ap()
    ag2_out = d("ag2_out", [8 * TC * D], BF16, kind="Internal").ap()
    ag4_in = d("ag4_in", [MROW * D], BF16, kind="Internal").ap()
    ag4_out = d("ag4_out", [(3 * 8 * 512 + 8 * 513) * D], BF16, kind="Internal").ap()
    rb_d = d("rb_d", [NTA * E], F32, kind="Internal").ap()
    m_in = d("m_in", [(MREC + NREC) * 4], F32, kind="Internal").ap()
    m_sub = [d(f"m_sub{q}", [(MREC + NREC) * 4], F32, kind="Internal").ap()
             for q in range(8)]
    m_my = d("m_my", [MROW * 4], F32, kind="Internal").ap()
    m_all = d("m_all", [8, MROW * 4], F32, kind="Internal").ap()

    stk = ExitStack()
    sb = lambda n, s_, dt_: stk.enter_context(nc.sbuf_tensor(n, s_, dt_))
    ps = lambda n, s_, dt_: stk.enter_context(nc.psum_tensor(n, s_, dt_))
    sem = lambda n: stk.enter_context(nc.semaphore(n))

    class S:
        pass
    s = S()
    s.x2full = sb("x2full", [128, NTA, D], BF16)
    s.ident = sb("ident", [128, 128], BF16)
    s.lnsc = sb("lnsc", [128, NTILE], F32)
    s.varsum = sb("varsum", [128, NTILE], F32)
    s.rstd = sb("rstd", [128, NTILE], F32)
    s.bias2 = sb("bias2", [128, NTILE], F32)
    s.logits = sb("logits", [128, NTA, E], BF16)
    s.stepm = sb("stepm", [8, 1], F32)
    s.shardi = sb("shardi", [128, 1], I32)
    s.nb4 = sb("nb4", [128, 1], F32)
    s.galn = sb("galn", [128, 16], F32)
    s.gidx32 = sb("gidx32", [128, 16], U32)
    s.RkC32 = sb("RkC32", [128, 4 * NTA], U32)
    s.zf = sb("zf", [128, E], F32)
    s.zt = s.logits.rearrange("p a e -> p (a e)")[:, 0:D]
    g1 = ExitStack()
    sb1 = lambda n, s_, dt_: g1.enter_context(nc.sbuf_tensor(n, s_, dt_))
    s.wqkv = sb1("wqkv", [128, NCH, 3072], BF16)
    s.bqkvP = sb1("bqkvP", [128, QKVCH], F32)
    s.bout = sb1("bout", [1, D], BF16)
    s.ones = sb1("ones", [1, 128], BF16)
    s.maskU = sb1("maskU", [9, 128], BF16)
    s.maskV = sb1("maskV", [9, 128], BF16)
    s.sums = sb1("sums", [128, NSLAB], F32)
    s.recip = sb1("recip", [128, 64], F32)
    s.wout = sb1("wout", [128, NCH, D], BF16)
    s.xot = sb1("xot", [128, 4096], BF16)
    s.xlnT = sb1("xlnT", [128, NCH, MEGA], BF16)
    s.qPb = sb1("qPb", [128, NSLAB * 128], BF16)
    s.qP = s.qPb[0:64, :].rearrange("p (sl m) -> p sl m", m=128)
    s.kst = s.qPb[64:128, :].rearrange("p (c t) -> p c t", t=MEGA)
    s.kL2 = sb1("kL2", [64, 16, MEGA], BF16)
    s.vP = sb1("vP", [128, NSLAB, 64], BF16)
    s.phat = sb1("phat", [128, NBLK, 128], BF16)
    s.pnT = sb1("pnT", [128, NBLK, 128], BF16)
    s.vstk = sb1("vstk", [128, NBLK, 64], BF16)
    s.xln = s.xot.rearrange("p (a d) -> p a d", a=NTILE)
    s.oT = s.xot.rearrange("p (c x) -> p c x", c=NCH)
    g1.close()
    g2 = ExitStack()
    sb2 = lambda n, s_, dt_: g2.enter_context(nc.sbuf_tensor(n, s_, dt_))
    s.gw = sb2("gw", [128, NCH, E], BF16)
    s.dtmpblk = sb2("dtmpblk", [128, 6 * D], BF16)
    s.dtmp = ([s.dtmpblk[:, i * D:(i + 1) * D].bitcast(I16) for i in range(3)]
              + [s.dtmpblk[:, i * D:(i + 1) * D] for i in range(3, 6)])
    s.shwordsA = sb2("shwordsA", [128, 8, II], I16)
    s.shwordsB = sb2("shwordsB", [128, 3, D], I16)
    s.shsctA = sb2("shsctA", [128, 8, II], BF16)
    s.shsctB = sb2("shsctB", [128, 3, D], BF16)
    s.shloA = sb2("shloA", [128, 8, II], BF16)
    s.shloB = sb2("shloB", [128, 3, D], BF16)
    s.shhiA = sb2("shhiA", [128, 8, II], BF16)
    s.shhiB = sb2("shhiB", [128, 3, D], BF16)
    s.y2 = sb2("y2", [128, NTILE, D], BF16)
    s.y2Tp = sb2("y2Tp", [128, NCH, MEGA], BF16)
    s.hshT = sb2("hshT", [128, 6, MEGA], BF16)
    s.hshU = sb2("hshU", [128, 6, MEGA], BF16)
    s.shTs = sb2("shTs", [128, 8, MEGA], BF16)
    s.logT = sb2("logT", [64, MEGA], BF16)
    g2.close()
    g3 = ExitStack()
    sb3 = lambda n, s_, dt_: g3.enter_context(nc.sbuf_tensor(n, s_, dt_))
    s.srcM = sb3("srcM", [128, 128, 4], F32)
    s.capall = sb3("capall", [128, NTA, E], F32)
    s.su32sb = sb3("su32sb", [128, NTA, NTA], BF16)
    s.posR = sb3("posR", [128, NTA, E], F32)
    s.mask01 = sb3("mask01", [128, NTA, E], BF16)
    s.wfull = sb3("wfull", [128, NTA, E], F32)
    s.ltb = sb3("ltb", [128, NTA, E], F32)
    s.keptf = sb3("keptf", [128, NTA, E], BF16)
    s.mx8 = sb3("mx8", [128, NTA, 8], F32)
    s.nmx = sb3("nmx", [128, NTA], F32)
    s.s4 = sb3("s4", [128, NTA], F32)
    s.w8 = sb3("w8", [128, NTA, 8], F32)
    s.i8 = sb3("i8", [128, NTA, 8], U32)
    s.i8f = sb3("i8f", [128, NTA, 8], F32)
    s.gtf = sb3("gtf", [128, 16], F32)
    s.macc = sb3("macc", [128, 512], F32)
    s.mtmp = [sb3(f"mtmp{i}", [128, 512], F32) for i in range(2)]
    s.macct = sb3("macct", [8, 4], F32)
    s.mtmpt = [sb3(f"mtmpt{i}", [8, 4], F32) for i in range(2)]
    s.RkSf = sb3("RkSf", [128, 4, NTA], F32)
    s.RkS32 = sb3("RkS32", [128, 4 * NTA], U32)
    s.Rtmp1 = sb3("Rtmp1", [128, NTA], F32)
    s.Rtmp2 = sb3("Rtmp2", [128, NTA], F32)
    s.tlv = sb3("tlv", [128, NTA], F32)
    s.tvecs = sb3("tvecs", [128, NTA], F32)
    s.posR2 = sb3("posR2", [128, NTA, E], F32)
    s.evec2b = sb3("evec2b", [128, E], F32)
    s.etb2 = sb3("etb2", [128, E], F32)
    s.RkCall = sb3("RkCall", [128, 4, NTA], F32)
    s.evecb = sb3("evecb", [128, E], F32)
    s.etb = sb3("etb", [128, E], F32)
    s.eiob = sb3("eiob", [128, E], F32)
    s.lstr = sb3("lstr", [128, 128], BF16)
    s.rbsb = sb3("rbsb", [NTA, E], F32)
    s.onesc = sb3("onesc", [1, NTA], F32)
    s.onesf = sb3("onesf", [128, 1], BF16)
    s.Rkall = sb3("Rkall", [128, 4, NTA], F32)
    s.baserow = sb3("baserow", [1, E], F32)
    s.countrow = sb3("countrow", [1, E], F32)
    s.agc = sb3("agc", [8, E], F32)
    g3.close()
    g4 = ExitStack()
    sb4 = lambda n, s_, dt_: g4.enter_context(nc.sbuf_tensor(n, s_, dt_))
    s.g4big = sb4("g4big", [128, 16 * D], BF16)
    s.buf = s.g4big.rearrange("p (a d) -> p a d", d=D)
    s.eloA = s.g4big[:, 0:8 * II].rearrange("p (a d) -> p a d", d=II)
    s.eloB = s.g4big[:, 8 * II:8 * II + 3 * D].rearrange("p (a d) -> p a d", d=D)
    EOF_ = 8 * II + 3 * D
    s.dtmp4 = ([s.g4big[:, EOF_ + D * i:EOF_ + D * (i + 1)].bitcast(I16) for i in range(3)]
               + [s.g4big[:, EOF_ + D * i:EOF_ + D * (i + 1)] for i in range(3, 6)])
    s.bufT = sb4("bufT", [128, 8, NROWS], BF16)
    s.ewordsA = sb4("ewordsA", [128, 8, II], I16)
    s.ewordsB = sb4("ewordsB", [128, 3, D], I16)
    s.esctA = sb4("esctA", [128, 8, II], BF16)
    s.esctB = sb4("esctB", [128, 3, D], BF16)
    s.ehiA = sb4("ehiA", [128, 8, II], BF16)
    s.ehiB = sb4("ehiB", [128, 3, D], BF16)
    s.hT = sb4("hT", [128, 6, MT], BF16)
    s.hup = sb4("hup", [128, 6, MT], BF16)
    s.eoTs = sb4("eoTs", [128, 8, MT], BF16)
    s.eor = sb4("eor", [128, 2, D], BF16)
    g4.close()
    g5 = ExitStack()
    sb5 = lambda n, s_, dt_: g5.enter_context(nc.sbuf_tensor(n, s_, dt_))
    s.cbuf = [sb5(f"cbuf{i}", [128, 4, D], BF16) for i in range(4)]
    s.outf = [sb5(f"outf{i}", [128, D], F32) for i in range(2)]
    g5.close()
    s.ps_qkv = [ps("ps_qkv0", [128, 512], F32), ps("ps_qkv1", [128, 512], F32)]
    s.ps_s = [ps(f"ps_s{i}", [128, 512], F32) for i in range(4)]
    s.ps_tr = ps("ps_tr", [128, 8, 128], BF16)
    s.ps_vs = ps("ps_vs", [128, 8, 64], BF16)

    s.s_wl = sem("s_wl")
    s.s_x = sem("s_x")
    s.s_lay = sem("s_lay")
    s.s_xt = sem("s_xt")
    s.s_y2p = sem("s_y2p")
    s.s_qkv = sem("s_qkv")
    s.s_S = sem("s_S")
    s.s_pnt = sem("s_pnt")
    s.s_vst = sem("s_vst")
    s.s_av = sem("s_av")
    s.s_op = sem("s_op")
    s.s_lg = sem("s_lg")
    s.s_sh = sem("s_sh")
    s.s_cnt = sem("s_cnt")
    s.s_ag = sem("s_ag")
    s.s_gth = sem("s_gth")
    s.s_eW = sem("s_eW")
    s.s_ex = sem("s_ex")
    s.s_sc2 = sem("s_sc2")
    s.s_m = sem("s_m")
    s.s_eod = sem("s_eod")
    s.s_cb = [sem(f"s_cb{k}") for k in range(4)]
    s.s_ms = [sem(f"s_ms{q}") for q in range(8)]
    s.s_dbg = sem("s_dbg")
    s.s_ag2 = sem("s_ag2")
    s.s_ag4 = sem("s_ag4")
    s.s_dbg2 = sem("s_dbg2")
    s.s_az = sem("s_az")
    s.s_fin = sem("s_fin")
    s.s_rb = sem("s_rb")
    s.vch = sem("vch")
    s.sch = sem("sch")
    vc = [0]
    ev = {}

    def emit_vector(v):
        def step(ins, name=None):
            vc[0] += 1
            ins.then_inc(s.vch, 1)
            v.wait_ge(s.vch, vc[0])
            if name:
                ev[name] = vc[0]
        v.memset(s.ones[:, :], 1.0)
        v.memset(s.zt[:, :], 0.0)
        step(v.memset(s.zf[:, :], 0.0), "init0")
        for m in range(min(MLIM, NMEGA)):
            x = s.x2full[:, NTILE * m:NTILE * m + NTILE, :]
            v.wait_ge(s.s_x, 16 * (m + 1))
            if m > 0:
                v.wait_ge(s.sch, SC_P1 * (m - 1) + NTILE)
            step(v.tensor_reduce(s.lnsc[:, :], x[:, :, :], axis=AX.X, op=ALU.add, negate=True))
            step(v.tensor_scalar_mul(s.lnsc[:, :], s.lnsc[:, :], 1.0 / D), f"negmean{m}")
            v.wait_ge(s.sch, SC_P1 * m + NTILE)
            step(v.tensor_scalar(s.rstd[:, :], s.varsum[:, :], 1.0 / D, EPS, op0=ALU.mult, op1=ALU.add))
            step(v.reciprocal(s.rstd[:, :], s.rstd[:, :]), f"rstd{m}")
            v.wait_ge(s.sch, SC_P1 * m + NTILE + 1)
            step(v.tensor_tensor(s.bias2[:, :], s.lnsc[:, :], s.rstd[:, :], op=ALU.mult), f"bias2{m}")
            for tile in range(NTILE):
                v.wait_ge(s.s_xt, NTILE * m + tile + 1)
                step(v.tensor_copy(s.xlnT[:, :, 128 * tile:128 * tile + 128], s.ps_tr[:, :, :]),
                     f"xte{m}_{tile}")
            if SUB < 2:
                continue
            if m > 0:
                v.wait_ge(s.s_S, NBLK * m)
                v.wait_ge(s.s_vst, NBLK * m)
                v.wait_ge(s.s_lay, 16 * LAY * m)
            for c in range(QKVCH):
                v.wait_ge(s.s_qkv, QKVCH * m + c + 1)
                bank = s.ps_qkv[c % 2]
                if c < 16:
                    m0 = 64 * (c // 8) + 8 * (c % 8)
                    v.tensor_scalar(s.qP[:, :, m0:m0 + 8],
                                    bank[0:64, :].rearrange("p (sl ti) -> p sl ti", ti=8),
                                    s.bqkvP[0:64, c:c + 1], None, op0=ALU.add)
                    step(v.tensor_scalar(s.kst[:, c, :], bank[64:128, :],
                                         s.bqkvP[64:128, c:c + 1], None, op0=ALU.add),
                         f"qkve{m}_{c}")
                else:
                    cv = c - 16
                    step(v.tensor_scalar(s.vP[:, :, 8 * cv:8 * cv + 8],
                                         bank[:, :].rearrange("p (sl ti) -> p sl ti", ti=8),
                                         s.bqkvP[:, c:c + 1], None, op0=ALU.add),
                         f"qkve{m}_{c}")
            if SUB < 3 or 30 <= SUB <= 31:
                continue
            for b in range(NBLK):
                blk = NBLK * m + b
                v.wait_ge(s.sch, SC_P1 * m + 9 + 8 * (b + 1))
                step(v.reciprocal(s.recip[:, 8 * b:8 * b + 8], s.sums[:, 8 * b:8 * b + 8]))
                for h4 in range(2):
                    sl = slice(4 * h4, 4 * h4 + 4)
                    step(v.scalar_tensor_tensor(
                        s.phat[:, sl, :], s.phat[:, sl, :], 1.0,
                        s.recip[:, 8 * b + 4 * h4:8 * b + 4 * h4 + 4, None].to_broadcast([128, 4, 128]),
                        op0=ALU.mult, op1=ALU.mult))
                ev[f"pn{blk}"] = vc[0]
                if SUB < 4:
                    continue
                v.wait_ge(s.s_pnt, blk + 1)
                step(v.tensor_copy(s.pnT[:, :, :], s.ps_tr[:, :, :]), f"pnte{blk}")
                v.wait_ge(s.s_vst, blk + 1)
                step(v.tensor_copy(s.vstk[:, :, :], s.ps_vs[:, :, :]), f"vste{blk}")
                if SUB < 5:
                    continue
                v.wait_ge(s.s_av, blk + 1)
                otv = s.oT[:, :, 64 * b:64 * b + 64].rearrange("p c (j t) -> p c j t", t=8)
                v.tensor_copy(otv[0:64], s.ps_s[2 * (b % 2)][0:64, :].rearrange(
                    "p (j hh t) -> p hh j t", hh=8, t=8))
                step(v.tensor_copy(otv[64:128], s.ps_s[2 * (b % 2) + 1][64:128, :].rearrange(
                    "p (j hh t) -> p hh j t", hh=8, t=8)), f"ote{blk}")
            if SUB < 6:
                continue
            for tile in range(NTILE):
                v.wait_ge(s.s_op, NTILE * m + tile + 1)
                xt_ = s.x2full[:, NTILE * m + tile, :]
                v.tensor_tensor(xt_[:, 0:512], xt_[:, 0:512], s.ps_qkv[0][:, :], op=ALU.add)
                step(v.tensor_tensor(xt_[:, 512:1024], xt_[:, 512:1024],
                                     s.ps_qkv[1][:, :], op=ALU.add), f"x2_{m}_{tile}")
        ev["p1"] = vc[0]
        if PH == 1:
            return
        v.wait_ge(s.s_wl, 16 * NLD1)
        for tl in range(11):
            if tl < 8:
                N, w_, lo_, hi_, st_ = II, s.shwordsA[:, tl, :], s.shloA[:, tl, :], \
                    s.shhiA[:, tl, :], s.shsctA[:, tl, :]
            else:
                N, w_, lo_, hi_, st_ = D, s.shwordsB[:, tl - 8, :], s.shloB[:, tl - 8, :], \
                    s.shhiB[:, tl - 8, :], s.shsctB[:, tl - 8, :]
            _emit_decode_tile(v, step, w_, [t[:, 0:N] for t in s.dtmp], lo_, hi_, st_)
        ev["shdec"] = vc[0]
        for m in range(NMEGA):
            x2 = s.x2full[:, NTILE * m:NTILE * m + NTILE, :]
            scb = SC_P2B + SC_P2 * m
            if m == 0:
                v.wait_ge(s.s_az, 16 * 17)
            v.wait_ge(s.sch, (SC_P2B + SC_P2 * (m - 1) + NTILE) if m > 0 else (SC_P2B - SC_P1 + NTILE))
            step(v.tensor_reduce(s.lnsc[:, :], x2[:, :, :], axis=AX.X, op=ALU.add, negate=True))
            step(v.tensor_scalar_mul(s.lnsc[:, :], s.lnsc[:, :], 1.0 / D), f"negmean2_{m}")
            v.wait_ge(s.sch, scb + NTILE)
            step(v.tensor_scalar(s.rstd[:, :], s.varsum[:, :], 1.0 / D, EPS, op0=ALU.mult, op1=ALU.add))
            step(v.reciprocal(s.rstd[:, :], s.rstd[:, :]), f"rstd2_{m}")
            v.wait_ge(s.sch, scb + NTILE + 1)
            step(v.tensor_tensor(s.bias2[:, :], s.lnsc[:, :], s.rstd[:, :], op=ALU.mult), f"bias22_{m}")
            for tile in range(NTILE):
                v.wait_ge(s.s_xt, NTILE * NMEGA + NTILE * m + tile + 1)
                step(v.tensor_copy(s.y2Tp[:, :, 128 * tile:128 * tile + 128], s.ps_tr[:, :, :]),
                     f"y2pe{m}_{tile}")
            v.wait_ge(s.s_lg, 6 * m + 1)
            step(v.tensor_copy(s.logT[:, :], s.ps_qkv[0][0:64, :]), f"logTe{m}")
            v.wait_ge(s.s_lg, 6 * m + 2)
            step(v.tensor_copy(s.logits[:, 4 * m:4 * m + 4, :], s.ps_tr[:, 0:4, 0:64]), f"loge{m}")
            for ic in range(6):
                v.wait_ge(s.s_sh, 14 * m + ic + 1)
                step(v.tensor_copy(s.hshU[:, ic, :], s.ps_s[1][:, :]), f"shupe{m}_{ic}")
                v.wait_ge(s.sch, scb + 9 + ic + 1)
                step(v.tensor_tensor(s.hshT[:, ic, :], s.hshT[:, ic, :], s.hshU[:, ic, :],
                                     op=ALU.mult), f"shmul{m}_{ic}")
            for dc in range(8):
                v.wait_ge(s.s_sh, 14 * m + 6 + dc + 1)
                pbank = s.ps_s[2] if dc % 2 == 0 else s.ps_s[3]
                step(v.tensor_copy(s.shTs[:, dc, :], pbank[:, :]), f"shdne{m}_{dc}")
            for tile in range(NTILE):
                v.wait_ge(s.s_lg, 6 * m + 2 + tile + 1)
                if DBG and tile == 0:
                    v.wait_ge(s.s_dbg, 16 * (m + 1))
                x2t = s.x2full[:, NTILE * m + tile, :]
                step(v.tensor_tensor(x2t[:, :], x2t[:, :],
                                     s.ps_tr[:, :, :].rearrange("p a b -> p (a b)"),
                                     op=ALU.add), f"acc{m}_{tile}")
        ev["p2"] = vc[0]
        if PH == 2:
            return
        v.wait_ge(s.s_sh, 14 * NMEGA)
        v.wait_ge(s.s_lg, 6 * NMEGA)
        v.wait_ge(s.s_wl, 16 * NLD0)
        v.memset(s.onesc[:, :], 1.0)
        v.memset(s.onesf[:, :], 1.0)
        step(v.memset(s.srcM[:, :, :], 0.0), "p3init")
        step(v.scalar_tensor_tensor(s.ltb[:, :, :],
                                    s.eiob[:, None, :].to_broadcast([128, NTA, E]), -1e-7,
                                    s.logits[:, :, :], op0=ALU.mult, op1=ALU.add))
        for t_ in range(NTA):
            step(v.max(out=s.mx8[:, t_, :], in_=s.ltb[:, t_, :]))
        ev["mx8"] = vc[0]
        for t_ in range(NTA):
            v.tensor_scalar(s.mask01[:, t_, :], s.ltb[:, t_, :], s.mx8[:, t_, 3:4], None,
                            op0=ALU.is_ge)
        step(v.tensor_scalar(s.nmx[:, :], s.mx8[:, :, 0], -1.0, None, op0=ALU.mult), "mask01")
        v.wait_ge(s.sch, SC_P3B + NTA)
        step(v.tensor_tensor(s.wfull[:, :, :], s.wfull[:, :, :], s.mask01[:, :, :], op=ALU.mult), "e2")
        step(v.tensor_reduce(s.s4[:, :], s.wfull[:, :, :], axis=AX.X, op=ALU.add))
        step(v.reciprocal(s.s4[:, :], s.s4[:, :]), "s4")
        for t_ in range(NTA):
            step(v.tensor_scalar(s.wfull[:, t_, :], s.wfull[:, t_, :], s.s4[:, t_:t_ + 1], None,
                                 op0=ALU.mult), "wfull" if t_ == NTA - 1 else None)
        v.wait_ge(s.s_cnt, 1)
        step(v.tensor_copy(s.countrow[:, :], s.ps_s[0][0:1, 0:E]), "cntev")
        v.wait_ge(s.s_cnt, 2)
        step(v.tensor_copy(s.baserow[:, :], s.ps_s[1][0:1, 0:E]), "basev")
        v.wait_ge(s.s_cnt, 3)
        step(v.tensor_copy(s.rbsb[:, :], s.ps_s[3][0:NTA, 0:E]), "rbev")
        step(v.tensor_scalar(s.rbsb[:, :], s.rbsb[:, :], -1.0, CAP * 1.0,
                             op0=ALU.mult, op1=ALU.add), "capm")
        v.wait_ge(s.s_rb, 16 * 2)
        for t_ in range(NTA):
            v.wait_ge(s.s_cnt, 3 + t_ + 1)
            pb = s.ps_s[t_ % 2]
            step(v.tensor_tensor(s.keptf[:, t_, :], pb[0:128, 0:E], s.capall[:, t_, :],
                                 op=ALU.is_lt), f"kept{t_}")
            step(v.scalar_tensor_tensor(s.posR[:, t_, :], pb[0:128, 0:E], 1.0,
                                        s.evecb[:, :], op0=ALU.mult, op1=ALU.add))
            step(v.tensor_tensor(s.posR[:, t_, :], s.posR[:, t_, :], s.capall[:, t_, :],
                                 op=ALU.subtract))
            step(v.scalar_tensor_tensor(s.posR2[:, t_, :], pb[0:128, 0:E], 1.0,
                                        s.evec2b[:, :], op0=ALU.mult, op1=ALU.add))
            step(v.tensor_tensor(s.posR2[:, t_, :], s.posR2[:, t_, :], s.capall[:, t_, :],
                                 op=ALU.subtract), f"posr{t_}")
        step(v.tensor_tensor(s.wfull[:, :, :], s.wfull[:, :, :], s.keptf[:, :, :], op=ALU.mult), "wm")
        step(v.tensor_tensor(s.keptf[:, :, :], s.keptf[:, :, :], s.mask01[:, :, :], op=ALU.mult))
        step(v.tensor_tensor(s.posR[:, :, :], s.posR[:, :, :], s.keptf[:, :, :], op=ALU.mult))
        step(v.tensor_scalar(s.mask01[:, :, :], s.keptf[:, :, :], -1.0, 1.0,
                             op0=ALU.mult, op1=ALU.add))
        step(v.tensor_tensor(s.posR2[:, :, :], s.posR2[:, :, :], s.keptf[:, :, :], op=ALU.mult))
        step(v.tensor_tensor(s.capall[:, :, :], s.mask01[:, :, :],
                             s.etb[:, None, :].to_broadcast([128, NTA, E]), op=ALU.mult))
        step(v.tensor_tensor(s.posR[:, :, :], s.posR[:, :, :], s.capall[:, :, :], op=ALU.add))
        step(v.tensor_tensor(s.capall[:, :, :], s.mask01[:, :, :],
                             s.etb2[:, None, :].to_broadcast([128, NTA, E]), op=ALU.mult))
        step(v.tensor_tensor(s.posR2[:, :, :], s.posR2[:, :, :], s.capall[:, :, :], op=ALU.add),
             "posm")
        for t_ in range(NTA):
            step(v.max(out=s.w8[:, t_, :], in_=s.wfull[:, t_, :]))
            step(v.max_index(out=s.i8[:, t_, :], in_max=s.w8[:, t_, :], in_values=s.wfull[:, t_, :]))
        step(v.tensor_copy(s.i8f[:, :, :], s.i8[:, :, :]), "i8fd")
        for k in range(4):
            step(v.tensor_tensor(s.keptf[:, :, :], s.eiob[:, None, :].to_broadcast([128, NTA, E]),
                                 s.i8f[:, :, k:k + 1].to_broadcast([128, NTA, E]), op=ALU.is_equal))
            step(v.tensor_tensor(s.capall[:, :, :], s.posR[:, :, :], s.keptf[:, :, :], op=ALU.mult))
            step(v.tensor_reduce(s.Rkall[:, k, :], s.capall[:, :, :], axis=AX.X, op=ALU.add))
            step(v.tensor_tensor(s.capall[:, :, :], s.posR2[:, :, :], s.keptf[:, :, :], op=ALU.mult))
            step(v.tensor_reduce(s.RkCall[:, k, :], s.capall[:, :, :], axis=AX.X, op=ALU.add))
            step(v.tensor_tensor(s.capall[:, :, :], s.mask01[:, :, :], s.keptf[:, :, :],
                                 op=ALU.mult))
            step(v.tensor_reduce(s.Rtmp1[:, :], s.capall[:, :, :], axis=AX.X, op=ALU.add))
            step(v.tensor_scalar(s.Rtmp2[:, :], s.Rtmp1[:, :], -1.0, 1.0,
                                 op0=ALU.mult, op1=ALU.add))
            step(v.tensor_tensor(s.RkSf[:, k, :], s.Rkall[:, k, :], s.Rtmp2[:, :], op=ALU.mult))
            step(v.tensor_scalar(s.Rtmp2[:, :], s.tlv[:, :], 1.0, 1.0 * (MREC + 4096 * k),
                                 op0=ALU.mult, op1=ALU.add))
            step(v.tensor_tensor(s.Rtmp2[:, :], s.Rtmp2[:, :], s.Rtmp1[:, :], op=ALU.mult))
            step(v.tensor_tensor(s.RkSf[:, k, :], s.RkSf[:, k, :], s.Rtmp2[:, :], op=ALU.add))
        step(v.tensor_copy(s.RkC32[:, :], s.RkCall[:, :, :].rearrange("p k t -> p (k t)")), "rkc")
        step(v.tensor_copy(s.RkS32[:, :], s.RkSf[:, :, :].rearrange("p k t -> p (k t)")), "rks")
        srar = s.srcM.rearrange("p a e -> p (a e)")
        for k in range(4):
            step(v.tensor_copy(srar[:, 4 * 32 * k:4 * 32 * (k + 1):4], s.tvecs[:, :]))
            step(v.tensor_copy(srar[:, 4 * 32 * k + 1:4 * 32 * (k + 1):4],
                               s.w8[:, :, k:k + 1].rearrange("p t o -> p (t o)")),
                 "srcm" if k == 3 else None)
        for q in range(1, 8):
            v.wait_ge(s.s_m, 16 * (2 * q + 2))
            step(v.tensor_tensor(s.macc[:, :], s.macc[:, :], s.mtmp[(q - 1) % 2][:, :],
                                 op=ALU.add))
            step(v.tensor_tensor(s.macct[:, :], s.macct[:, :], s.mtmpt[(q - 1) % 2][:, :],
                                 op=ALU.add), f"mrg{q}")
        v.wait_ge(s.s_m, 16 * 19)
        step(v.tensor_copy(s.gidx32[:, :], s.gtf[:, :]), "gidxc")
        for ct in range(16):
            v.wait_ge(s.s_ex, ct + 1)
            step(v.tensor_copy(s.bufT[:, :, 128 * ct:128 * ct + 128], s.ps_tr[:, :, :]),
                 f"btev{ct}")
        for e in range(ELOC):
            v.wait_ge(s.s_eW, 16 * NLDE * (e + 1))
            if e >= 1:
                v.wait_ge(s.s_ex, 16 + 16 * e)
            for tl in range(11):
                if tl < 8:
                    N, w_, lo_, hi_, st_ = II, s.ewordsA[:, tl, :], s.eloA[:, tl, :], \
                        s.ehiA[:, tl, :], s.esctA[:, tl, :]
                else:
                    N, w_, lo_, hi_, st_ = D, s.ewordsB[:, tl - 8, :], s.eloB[:, tl - 8, :], \
                        s.ehiB[:, tl - 8, :], s.esctB[:, tl - 8, :]
                _emit_decode_tile(v, step, w_, [t[:, 0:N] for t in s.dtmp4], lo_, hi_, st_)
            ev[f"edec{e}"] = vc[0]
            exb = 16 + 16 * e
            for ic in range(6):
                v.wait_ge(s.s_ex, exb + ic + 1)
                step(v.tensor_copy(s.hup[:, ic, :], s.ps_s[1][0:128, 0:MT]), f"hup{e}_{ic}")
                v.wait_ge(s.sch, SC_P4B + SC_PE * e + ic + 1)
                step(v.tensor_tensor(s.hT[:, ic, :], s.hT[:, ic, :], s.hup[:, ic, :],
                                     op=ALU.mult), f"hmul{e}_{ic}")
            for dc in range(8):
                v.wait_ge(s.s_ex, exb + 6 + dc + 1)
                pbank = s.ps_s[2] if dc % 2 == 0 else s.ps_s[3]
                step(v.tensor_copy(s.eoTs[:, dc, :], pbank[0:128, 0:MT]), f"eoe{e}_{dc}")
            v.wait_ge(s.s_m, 16 * 20)
            if e > 0:
                v.wait_ge(s.s_eod, 16 * e)
            for grp in range(2):
                v.wait_ge(s.s_ex, exb + 14 + grp + 1)
                step(v.tensor_scalar(s.eor[:, grp, :].rearrange("p (dc x) -> p dc x", dc=8),
                                     s.ps_tr[:, :, :], s.galn[:, 2 * e + grp:2 * e + grp + 1],
                                     None, op0=ALU.mult), f"eor{e}_{grp}")
        ev["p4"] = vc[0]
        for mt in range(NTA):
            for k in range(4):
                v.wait_ge(s.s_cb[k], 16 * (mt + 1))
            if mt >= 2:
                v.wait_ge(s.s_fin, 16 * (mt - 1))
            cb = s.cbuf[mt % 4]
            of = s.outf[mt % 2]
            step(v.tensor_tensor(of[:, :], cb[:, 0, :], cb[:, 1, :], op=ALU.add))
            step(v.tensor_tensor(of[:, :], of[:, :], cb[:, 2, :], op=ALU.add))
            step(v.tensor_tensor(of[:, :], of[:, :], cb[:, 3, :], op=ALU.add))
            step(v.tensor_tensor(of[:, :], of[:, :], s.x2full[:, mt, :], op=ALU.add),
                 f"fin{mt}")
        ev["end"] = vc[0]

    def emit_scalar(sc):
        scc = [0]

        def sstep(ins):
            scc[0] += 1
            ins.then_inc(s.sch, 1)
        for m in range(min(MLIM, NMEGA)):
            x = s.x2full[:, NTILE * m:NTILE * m + NTILE, :]
            sc.wait_ge(s.s_x, 16 * (m + 1))
            sc.wait_ge(s.s_xt, NTILE * m)
            if m > 0:
                sc.wait_ge(s.s_op, NTILE * m)
            sc.wait_ge(s.vch, ev[f"negmean{m}"])
            for t_ in range(NTILE):
                sstep(sc.activation(out=s.xln[:, t_, :], in_=x[:, t_, :], func=AF.Square,
                                    bias=s.lnsc[:, t_:t_ + 1], scale=1.0,
                                    accum_out=s.varsum[:, t_:t_ + 1]))
            sc.wait_ge(s.vch, ev[f"rstd{m}"])
            sstep(sc.activation(out=s.rstd[:, :], in_=s.rstd[:, :], func=AF.Sqrt))
            sc.wait_ge(s.vch, ev[f"bias2{m}"])
            for t_ in range(NTILE):
                sstep(sc.activation(out=s.xln[:, t_, :], in_=x[:, t_, :], func=AF.Identity,
                                    bias=s.bias2[:, t_:t_ + 1], scale=s.rstd[:, t_:t_ + 1]))
            if SUB < 3 or 30 <= SUB <= 31:
                continue
            for b in range(NBLK):
                blk = NBLK * m + b
                sc.wait_ge(s.s_S, blk + 1)
                if blk > 0:
                    sc.wait_ge(s.vch, ev[f"pn{blk - 1}"])
                for j in range(8):
                    bank = s.ps_s[2 * (b % 2) + j // 4]
                    sstep(sc.activation(out=s.phat[:, j, :],
                                        in_=bank[:, 128 * (j % 4):128 * (j % 4) + 128],
                                        func=AF.Exp, scale=0.125,
                                        accum_out=s.sums[:, 8 * b + j:8 * b + j + 1]))
        if PH == 1:
            return
        for m in range(NMEGA):
            x2 = s.x2full[:, NTILE * m:NTILE * m + NTILE, :]
            sc.wait_ge(s.s_xt, NTILE * NMEGA + NTILE * m)
            sc.wait_ge(s.s_sc2, 16 * m)
            sc.wait_ge(s.vch, ev[f"negmean2_{m}"])
            for t_ in range(NTILE):
                sstep(sc.activation(out=s.y2[:, t_, :], in_=x2[:, t_, :], func=AF.Square,
                                    bias=s.lnsc[:, t_:t_ + 1], scale=1.0,
                                    accum_out=s.varsum[:, t_:t_ + 1]))
            sc.wait_ge(s.vch, ev[f"rstd2_{m}"])
            sstep(sc.activation(out=s.rstd[:, :], in_=s.rstd[:, :], func=AF.Sqrt))
            sc.wait_ge(s.vch, ev[f"bias22_{m}"])
            for t_ in range(NTILE):
                sstep(sc.activation(out=s.y2[:, t_, :], in_=x2[:, t_, :], func=AF.Identity,
                                    bias=s.bias2[:, t_:t_ + 1], scale=s.rstd[:, t_:t_ + 1]))
            for ic in range(6):
                sc.wait_ge(s.s_sh, 14 * m + ic + 1)
                sstep(sc.activation(out=s.hshT[:, ic, :], in_=s.ps_s[0][:, :], func=AF.Silu))
        if PH == 2:
            return
        sc.wait_ge(s.vch, ev["mask01"])
        for t_ in range(NTA):
            sstep(sc.activation(out=s.wfull[:, t_, :], in_=s.ltb[:, t_, :], func=AF.Exp,
                                bias=s.nmx[:, t_:t_ + 1], scale=1.0))
        for e in range(ELOC):
            for ic in range(6):
                sc.wait_ge(s.s_ex, 16 + 16 * e + ic + 1)
                sstep(sc.activation(out=s.hT[:, ic, :], in_=s.ps_s[0][0:128, 0:MT], func=AF.Silu))

    def emit_tensor(t):
        t.wait_ge(s.s_wl, 16 * NLD0A)
        for m in range(min(MLIM, NMEGA)):
            for tile in range(NTILE):
                t.wait_ge(s.sch, SC_P1 * m + NTILE + 1 + tile + 1)
                if tile > 0:
                    t.wait_ge(s.vch, ev[f"xte{m}_{tile - 1}"])
                elif m > 0:
                    t.wait_ge(s.vch, ev[f"xte{m - 1}_{NTILE - 1}"])
                    t.wait_ge(s.vch, ev[f"pnte{NBLK * m - 1}"])
                ins = None
                for c in range(NCH):
                    ins = t.matmul(s.ps_tr[:, c, :], s.xln[:, tile, 128 * c:128 * c + 128],
                                   s.ident[:, :], is_transpose=True, start=(c == 0), stop=(c == NCH - 1))
                ins.then_inc(s.s_xt, 1)
            if SUB < 2:
                continue
            for c in range(QKVCH):
                t.wait_ge(s.vch, ev[f"xte{m}_{NTILE - 1}"])
                prevc = QKVCH * m + c - 2
                if prevc >= 0:
                    mm_, cc_ = divmod(prevc, QKVCH)
                    t.wait_ge(s.vch, ev[f"qkve{mm_}_{cc_}"])
                if m > 0 and c < 2:
                    t.wait_ge(s.vch, ev[f"x2_{m - 1}_{NTILE - 1}"])
                pb = s.ps_qkv[c % 2]
                ins = None
                for k in range(NCH):
                    ins = t.matmul(pb[:, :], s.wqkv[:, k, 128 * c:128 * c + 128], s.xlnT[:, k, :],
                                   start=(k == 0), stop=(k == NCH - 1))
                ins.then_inc(s.s_qkv, 1)
            if SUB < 3 or SUB == 30:
                continue
            for b in range(NBLK):
                blk = NBLK * m + b
                t.wait_ge(s.s_lay, 16 * LAY * (m + 1))
                if blk >= 2 and SUB < 30:
                    pm, pb_ = divmod(blk - 2, NBLK)
                    t.wait_ge(s.sch, SC_P1 * pm + 9 + 8 * (pb_ + 1))
                    if SUB >= 5:
                        t.wait_ge(s.vch, ev[f"ote{blk - 2}"])
                bank0 = s.ps_s[2 * (b % 2)]
                bank1 = s.ps_s[2 * (b % 2) + 1]
                ins = None
                for half4 in range(2):
                    bank = bank0 if half4 == 0 else bank1
                    for j4 in range(4):
                        t.matmul(bank[:, 128 * j4:128 * j4 + 128], s.maskU[:, :], s.maskV[:, :],
                                 start=(j4 == 0), stop=False)
                    for j4 in range(4):
                        slab = 8 * b + 4 * half4 + j4
                        ins = t.matmul(bank[:, 128 * j4:128 * j4 + 128], s.qP[:, slab, :],
                                       s.kL2[:, :, 8 * slab:8 * slab + 8],
                                       start=False, stop=(j4 == 3))
                ins.then_inc(s.s_S, 1)
                if SUB < 4 or 30 <= SUB <= 31:
                    continue
                t.wait_ge(s.vch, ev[f"pn{blk}"])
                if blk > 0:
                    t.wait_ge(s.vch, ev[f"pnte{blk - 1}"])
                ins = None
                for j in range(8):
                    ins = t.matmul(s.ps_tr[:, j, :], s.phat[:, j, :], s.ident[:, :],
                                   is_transpose=True, start=(j == 0), stop=(j == 7))
                ins.then_inc(s.s_pnt, 1)
                if blk > 0:
                    t.wait_ge(s.vch, ev[f"vste{blk - 1}"])
                t.wait_ge(s.vch, ev[f"qkve{m}_{QKVCH - 1}"])
                for j in range(8):
                    t.matmul(s.ps_vs[0:64, j, :], s.vP[0:64, 8 * b + j, :],
                             s.ident[0:64, 0:64],
                             is_transpose=True, start=(j == 0), stop=(j == 7))
                ins = None
                for j in range(8):
                    ins = t.matmul(s.ps_vs[64:128, j, :], s.vP[64:128, 8 * b + j, :],
                                   s.ident[64:128, 64:128],
                                   is_transpose=True, start=(j == 0), stop=(j == 7))
                ins.then_inc(s.s_vst, 1)
                if SUB < 5:
                    continue
                t.wait_ge(s.vch, ev[f"pnte{blk}"])
                t.wait_ge(s.vch, ev[f"vste{blk}"])
                t.wait_ge(s.sch, SC_P1 * m + 9 + 8 * (b + 1))
                ob_e = s.ps_s[2 * (b % 2)]
                ob_o = s.ps_s[2 * (b % 2) + 1]
                ins = None
                for j in range(8):
                    for par in range(2):
                        ob = ob_e if par == 0 else ob_o
                        out_ap = ob[64 * par:64 * par + 64, 64 * j:64 * j + 64]
                        ins = t.matmul(out_ap, s.vstk[:, j, :],
                                       s.pnT[:, j, 64 * par:64 * par + 64],
                                       start=(j == 0), stop=(j == 7),
                                       tile_position=(0, 64 * par))
                ins.then_inc(s.s_av, 1)
            if SUB < 6:
                continue
            for tile in range(NTILE):
                t.wait_ge(s.vch, ev[f"ote{NBLK * (m + 1) - 1}"])
                if tile >= 1:
                    t.wait_ge(s.vch, ev[f"x2_{m}_{tile - 1}"])
                elif m > 0:
                    t.wait_ge(s.vch, ev[f"x2_{m - 1}_{NTILE - 1}"])
                for c in range(NCH):
                    lh = s.oT[:, c, 128 * tile:128 * tile + 128]
                    t.matmul(s.ps_qkv[0][:, :], lh, s.wout[:, c, 0:512],
                             start=(c == 0), stop=False)
                    t.matmul(s.ps_qkv[1][:, :], lh, s.wout[:, c, 512:1024],
                             start=(c == 0), stop=False)
                t.matmul(s.ps_qkv[0][:, :], s.ones[0:1, :], s.bout[0:1, 0:512],
                         start=False, stop=True)
                t.matmul(s.ps_qkv[1][:, :], s.ones[0:1, :], s.bout[0:1, 512:1024],
                         start=False, stop=True).then_inc(s.s_op, 1)
        if PH == 1:
            return
        for m in range(NMEGA):
            scb = SC_P2B + SC_P2 * m
            for tile in range(NTILE):
                t.wait_ge(s.sch, scb + NTILE + 1 + tile + 1)
                if tile > 0:
                    t.wait_ge(s.vch, ev[f"y2pe{m}_{tile - 1}"])
                elif m > 0:
                    t.wait_ge(s.vch, ev[f"y2pe{m - 1}_{NTILE - 1}"])
                    t.wait_ge(s.vch, ev[f"acc{m - 1}_{NTILE - 1}"])
                else:
                    t.wait_ge(s.vch, ev[f"pnte{NBLK * NMEGA - 1}"])
                ins = None
                for sl8 in range(NCH):
                    kap, par = sl8 // 2, sl8 % 2
                    dsel = s.y2[:, tile, 256 * kap + par:256 * kap + 256:2]
                    ins = t.matmul(s.ps_tr[:, sl8, :], dsel, s.ident[:, :],
                                   is_transpose=True, start=(sl8 == 0), stop=(sl8 == NCH - 1))
                ins.then_inc(s.s_xt, 1)
            t.wait_ge(s.vch, ev[f"y2pe{m}_{NTILE - 1}"])
            if m > 0:
                t.wait_ge(s.vch, ev[f"logTe{m - 1}"])
            else:
                t.wait_ge(s.vch, ev[f"x2_{NMEGA - 1}_{NTILE - 1}"])
            ins = None
            for k in range(NCH):
                ins = t.matmul(s.ps_qkv[0][0:64, :], s.gw[:, k, :], s.y2Tp[:, k, :],
                               start=(k == 0), stop=(k == NCH - 1))
            ins.then_inc(s.s_lg, 1)
            t.wait_ge(s.vch, ev[f"logTe{m}"])
            if m > 0:
                t.wait_ge(s.vch, ev[f"loge{m - 1}"])
            t.wait_ge(s.vch, ev[f"y2pe{m}_{NTILE - 1}"])
            ins = None
            for tile in range(NTILE):
                ins = t.matmul(s.ps_tr[:, tile, 0:64], s.logT[0:64, 128 * tile:128 * tile + 128],
                               s.ident[0:64, 0:64], is_transpose=True,
                               start=(tile == 0), stop=(tile == NTILE - 1))
            ins.then_inc(s.s_lg, 1)
            t.wait_ge(s.vch, ev["shdec"])
            t.wait_ge(s.vch, ev[f"y2pe{m}_{NTILE - 1}"])
            for ic in range(6):
                p2, k2 = ic % 2, ic // 2
                isel = slice(256 * k2 + p2, 256 * k2 + 256, 2)
                if ic > 0:
                    t.wait_ge(s.sch, SC_P2B + SC_P2 * m + 9 + ic)
                    t.wait_ge(s.vch, ev[f"shupe{m}_{ic - 1}"])
                elif m > 0:
                    t.wait_ge(s.sch, SC_P2B + SC_P2 * (m - 1) + 15)
                    t.wait_ge(s.vch, ev[f"shupe{m - 1}_{5}"])
                ins = None
                for sl8 in range(8):
                    kap, par = sl8 // 2, sl8 % 2
                    wt = s.shloA if par == 0 else s.shhiA
                    t.matmul(s.ps_s[0][:, :], wt[:, kap, isel], s.y2Tp[:, sl8, :],
                             start=(sl8 == 0), stop=(sl8 == 7))
                    ins = t.matmul(s.ps_s[1][:, :], wt[:, 4 + kap, isel], s.y2Tp[:, sl8, :],
                                   start=(sl8 == 0), stop=(sl8 == 7))
                ins.then_inc(s.s_sh, 1)
            for dc in range(8):
                pm_prev = 14 * m + 6 + dc - 2
                if pm_prev >= 14 * m + 6:
                    t.wait_ge(s.vch, ev[f"shdne{m}_{dc - 2}"])
                elif m > 0 and dc < 2:
                    t.wait_ge(s.vch, ev[f"shdne{m - 1}_{6 + dc}"])
                t.wait_ge(s.vch, ev[f"shmul{m}_{5}"]) if dc == 0 else None
                pbank = s.ps_s[2] if dc % 2 == 0 else s.ps_s[3]
                ins = None
                for sl6 in range(6):
                    p2, k2 = sl6 % 2, sl6 // 2
                    wt = s.shloB if p2 == 0 else s.shhiB
                    ins = t.matmul(pbank[:, :], wt[:, k2, 128 * dc:128 * dc + 128],
                                   s.hshT[:, sl6, :], start=(sl6 == 0), stop=(sl6 == 5))
                ins.then_inc(s.s_sh, 1)
            for tile in range(NTILE):
                t.wait_ge(s.vch, ev[f"shdne{m}_{7}"])
                t.wait_ge(s.vch, ev[f"loge{m}"])
                if tile > 0:
                    t.wait_ge(s.vch, ev[f"acc{m}_{tile - 1}"])
                ins = None
                for dc in range(8):
                    ins = t.matmul(s.ps_tr[:, dc, :], s.shTs[:, dc, 128 * tile:128 * tile + 128],
                                   s.ident[:, :], is_transpose=True,
                                   start=(dc == 0), stop=(dc == 7))
                ins.then_inc(s.s_lg, 1)
        if PH == 2:
            return
        t.wait_ge(s.vch, ev["mask01"])
        t.wait_ge(s.vch, ev[f"acc{NMEGA - 1}_{NTILE - 1}"])
        t.wait_ge(s.sch, SC_P2B + SC_P2 * (NMEGA - 1) + 15)
        t.wait_ge(s.vch, ev[f"shupe{NMEGA - 1}_5"])
        t.wait_ge(s.vch, ev[f"shdne{NMEGA - 1}_6"])
        t.wait_ge(s.vch, ev[f"shdne{NMEGA - 1}_7"])
        ins = None
        for t_ in range(NTA):
            ins = t.matmul(s.ps_s[0][0:1, 0:E], s.onesf[:, :], s.mask01[:, t_, :],
                           start=(t_ == 0), stop=(t_ == NTA - 1))
        ins.then_inc(s.s_cnt, 1)
        t.wait_ge(s.s_gth, 16 * 2)
        t.matmul(s.ps_s[1][0:1, 0:E], s.stepm[:, :], s.agc[:, :],
                 start=True, stop=True).then_inc(s.s_cnt, 1)
        t.wait_ge(s.vch, ev["basev"])
        t.wait_ge(s.s_wl, 16 * NLD0)
        for t_ in range(NTA):
            t.matmul(s.ps_s[3][0:NTA, 0:E], s.su32sb[:, t_, :], s.mask01[:, t_, :],
                     start=(t_ == 0), stop=False)
        t.matmul(s.ps_s[3][0:NTA, 0:E], s.onesc[0:1, :], s.baserow[0:1, :],
                 start=False, stop=True).then_inc(s.s_cnt, 1)
        t.wait_ge(s.vch, ev["cntev"])
        for t_ in range(NTA):
            if t_ >= 2:
                t.wait_ge(s.vch, ev[f"posr{t_ - 2}"])
            pb = s.ps_s[t_ % 2]
            t.matmul(pb[0:128, 0:E], s.lstr[:, :], s.mask01[:, t_, :],
                     start=True, stop=True).then_inc(s.s_cnt, 1)
        t.wait_ge(s.s_gth, 16 * 18)
        t.wait_ge(s.vch, ev["srcm"])
        for ct in range(16):
            if ct >= 1:
                t.wait_ge(s.vch, ev[f"btev{ct - 1}"])
            ins = None
            for slab in range(8):
                kap, par = slab // 2, slab % 2
                ins = t.matmul(s.ps_tr[:, slab, :],
                               s.buf[:, ct, 256 * kap + par:256 * kap + 256:2],
                               s.ident[:, :], is_transpose=True,
                               start=(slab == 0), stop=(slab == 7))
            ins.then_inc(s.s_ex, 1)
        for e in range(ELOC):
            exb = 16 + 16 * e
            t.wait_ge(s.vch, ev[f"edec{e}"])
            t.wait_ge(s.vch, ev[f"btev{15}"])
            for ic in range(6):
                p2, k2 = ic % 2, ic // 2
                isel = slice(256 * k2 + p2, 256 * k2 + 256, 2)
                if e > 0 or ic > 0:
                    pe_, pic = divmod(6 * e + ic - 1, 6)
                    t.wait_ge(s.sch, SC_P4B + SC_PE * pe_ + pic + 1)
                    t.wait_ge(s.vch, ev[f"hup{pe_}_{pic}"])
                ins = None
                for slab in range(8):
                    kap, par = slab // 2, slab % 2
                    wt = s.eloA if par == 0 else s.ehiA
                    t.matmul(s.ps_s[0][0:128, 0:MT], wt[:, kap, isel],
                             s.bufT[:, slab, MT * e:MT * e + MT],
                             start=(slab == 0), stop=(slab == 7))
                    ins = t.matmul(s.ps_s[1][0:128, 0:MT], wt[:, 4 + kap, isel],
                                   s.bufT[:, slab, MT * e:MT * e + MT],
                                   start=(slab == 0), stop=(slab == 7))
                ins.then_inc(s.s_ex, 1)
            for dc in range(8):
                if e > 0 or dc >= 2:
                    pe_, pdc = divmod(8 * e + dc - 2, 8)
                    t.wait_ge(s.vch, ev[f"eoe{pe_}_{pdc}"])
                if dc == 0:
                    t.wait_ge(s.vch, ev[f"hmul{e}_{5}"])
                pbank = s.ps_s[2] if dc % 2 == 0 else s.ps_s[3]
                ins = None
                for sl6 in range(6):
                    p2, k2 = sl6 % 2, sl6 // 2
                    wt = s.eloB if p2 == 0 else s.ehiB
                    ins = t.matmul(pbank[0:128, 0:MT], wt[:, k2, 128 * dc:128 * dc + 128],
                                   s.hT[:, sl6, :], start=(sl6 == 0), stop=(sl6 == 5))
                ins.then_inc(s.s_ex, 1)
            for grp in range(2):
                t.wait_ge(s.vch, ev[f"eoe{e}_{7}"])
                if e > 0 or grp > 0:
                    t.wait_ge(s.vch, ev[f"eor{e - 1 if grp == 0 else e}_{1 if grp == 0 else 0}"])
                ins = None
                for dc in range(8):
                    ins = t.matmul(s.ps_tr[:, dc, :], s.eoTs[:, dc, 128 * grp:128 * grp + 128],
                                   s.ident[:, :], is_transpose=True,
                                   start=(dc == 0), stop=(dc == 7))
                ins.then_inc(s.s_ex, 1)


    def emit_gpsimd(g):
        W = s.s_wl
        for k in range(NCH):
            g.dma_start(out=s.wqkv[:, k, :], in_=DI["wq"][128 * k:128 * k + 128, :]).then_inc(W, 16)
            g.dma_start(out=s.wout[:, k, :], in_=DI["wo"][128 * k:128 * k + 128, :]).then_inc(W, 16)
        g.dma_start(out=s.bqkvP[:, :], in_=DI["bq"][:, :]).then_inc(W, 16)
        g.dma_start(out=s.bout[:, :], in_=DI["bo"][None, :]).then_inc(W, 16)
        g.dma_start(out=s.ident[:, :], in_=DI["idn"][:, :]).then_inc(W, 16)
        g.dma_start(out=s.maskU[:, :], in_=DI["mU"][:, :]).then_inc(W, 16)
        g.dma_start(out=s.maskV[:, :], in_=DI["mV"][:, :]).then_inc(W, 16)
        g.dma_start(out=s.stepm[:, :], in_=DI["stepmask"][:, :]).then_inc(W, 16)
        g.dma_start(out=s.shardi[:, :], in_=DI["coreid"][None, :].broadcast_to([128, 1])).then_inc(W, 16)
        for m in range(min(MLIM, NMEGA)):
            t0 = m * MEGA
            xv = DI["x"][t0:t0 + MEGA, :].rearrange("(a p) d -> p a d", p=128)
            g.dma_start(out=s.x2full[:, NTILE * m:NTILE * m + NTILE, :], in_=xv).then_inc(s.s_x, 16)
            if m > 0:
                g.wait_ge(s.s_S, NBLK * m)
            if SUB < 3:
                continue
            L = s.s_lay
            lb = 16 * LAY * m
            g.dma_start(out=s.kL2[:, :, :],
                        in_=DI["peT"][:, None, t0:t0 + MEGA].broadcast_to(
                            [64, 16, MEGA])).then_inc(L, 16)
            g.wait_ge(L, lb + 16)
            g.wait_ge(s.vch, ev[f"qkve{m}_{15}"])
            for c16 in range(16):
                g.dma_start(out=s.kL2[:, c16, :], in_=s.kst[:, c16, :],
                            accum_op=ALU.add).then_inc(L, 16)
        if PH == 1:
            if SUB >= 30:
                if SUB == 31:
                    g.wait_ge(s.s_S, NBLK * min(MLIM, NMEGA))
                else:
                    g.wait_ge(s.s_lay, 16 * LAY * min(MLIM, NMEGA))
                subev = None
            else:
                subev = {1: "xte{}_3", 2: "qkve{}_23", 3: "pn##", 4: "vste##", 5: "ote##",
                         6: "x2_{}_3"}[min(SUB, 6)]
            for m in range(min(MLIM, NMEGA)):
                if subev is None:
                    pass
                elif "##" in subev:
                    g.wait_ge(s.vch, ev[subev.replace("##", str(NBLK * (m + 1) - 1))])
                else:
                    g.wait_ge(s.vch, ev[subev.format(m)])
                if m > 0:
                    g.wait_ge(s.s_fin, 16 * m)
                ovx = out_d[MEGA * m:MEGA * m + MEGA, :].rearrange("(a p) d -> p a d", p=128)
                g.dma_start(out=ovx,
                            in_=s.x2full[:, NTILE * m:NTILE * m + NTILE, :]).then_inc(s.s_fin, 16)
            g.wait_ge(s.s_fin, 16 * min(MLIM, NMEGA))
            return
        g.wait_ge(s.s_qkv, QKVCH * NMEGA)
        g.wait_ge(s.s_op, NTILE * NMEGA)
        g.dma_start(out=s.gw[:, :, :], in_=DI["gwT"].rearrange("(k p) e -> p k e", p=128)).then_inc(W, 16)
        g.dma_start(out=s.shwordsA[:, 0:4, :],
                    in_=DI["shg"].rearrange("(k p) i -> p k i", p=128)).then_inc(W, 16)
        g.dma_start(out=s.shwordsA[:, 4:8, :],
                    in_=DI["shu"].rearrange("(k p) i -> p k i", p=128)).then_inc(W, 16)
        g.dma_start(out=s.shwordsB[:, :, :],
                    in_=DI["shd"].rearrange("(k p) i -> p k i", p=128)).then_inc(W, 16)
        for k in range(4):
            for h in range(2):
                g.dma_start(out=s.shsctA[64 * h:64 * h + 64, k, :],
                            in_=DI["scA"][k, h, 16, None, :].broadcast_to([64, II])).then_inc(W, 16)
                g.dma_start(out=s.shsctA[64 * h:64 * h + 64, 4 + k, :],
                            in_=DI["scA"][k, h, 17, None, :].broadcast_to([64, II])).then_inc(W, 16)
        for k in range(3):
            for h in range(2):
                g.dma_start(out=s.shsctB[64 * h:64 * h + 64, k, :],
                            in_=DI["scB"][k, h, 8, None, :].broadcast_to([64, D])).then_inc(W, 16)

        def eload(e):
            g.dma_start(out=s.ewordsA[:, 0:4, :],
                        in_=DI["gup"][e].rearrange("(k p) i -> p k i", p=128)).then_inc(s.s_eW, 16)
            g.dma_start(out=s.ewordsA[:, 4:8, :],
                        in_=DI["uup"][e].rearrange("(k p) i -> p k i", p=128)).then_inc(s.s_eW, 16)
            g.dma_start(out=s.ewordsB[:, :, :],
                        in_=DI["dnp"][e].rearrange("(k p) i -> p k i", p=128)).then_inc(s.s_eW, 16)
            for k in range(4):
                for h in range(2):
                    g.dma_start(out=s.esctA[64 * h:64 * h + 64, k, :],
                                in_=DI["scA"][k, h, 2 * e, None, :].broadcast_to([64, II])).then_inc(s.s_eW, 16)
                    g.dma_start(out=s.esctA[64 * h:64 * h + 64, 4 + k, :],
                                in_=DI["scA"][k, h, 2 * e + 1, None, :].broadcast_to([64, II])).then_inc(s.s_eW, 16)
            for k in range(3):
                for h in range(2):
                    g.dma_start(out=s.esctB[64 * h:64 * h + 64, k, :],
                                in_=DI["scB"][k, h, e, None, :].broadcast_to([64, D])).then_inc(s.s_eW, 16)
        for m in range(NMEGA):
            g.wait_ge(s.s_sc2, 16 * (m + 1))
            sl = slice(MEGA * D * m, MEGA * D * (m + 1))
            ob2 = 8 * MEGA * D * m
            g.collective_compute("AllGather", ALU.bypass,
                                 replica_groups=[[0, 1, 2, 3, 4, 5, 6, 7]],
                                 ins=[ag2_in[sl]],
                                 outs=[ag2_out[ob2:ob2 + 8 * MEGA * D]]).then_inc(s.s_ag2, 1)
        g.wait_ge(s.vch, ev[f"shdne{NMEGA - 1}_{7}"])
        g.wait_ge(s.s_lg, 6 * NMEGA)
        g.dma_start(out=s.lstr[:, :], in_=DI["lstrict"][:, :]).then_inc(W, 16)
        g.dma_start(out=s.su32sb[:, :, :],
                    in_=DI["su32"].rearrange("(o a) e -> o a e", o=1).broadcast_to(
                        [128, NTA, NTA])).then_inc(W, 16)
        g.dma_start(out=s.evecb[:, :], in_=DI["evec"][None, :].broadcast_to([128, E])).then_inc(W, 16)
        g.dma_start(out=s.etb[:, :], in_=DI["etrash"][None, :].broadcast_to([128, E])).then_inc(W, 16)
        g.dma_start(out=s.eiob[:, :], in_=DI["eiota"][None, :].broadcast_to([128, E])).then_inc(W, 16)
        g.dma_start(out=s.tvecs[:, :], in_=DI["tvec"][:, :]).then_inc(W, 16)
        g.dma_start(out=s.tlv[:, :], in_=DI["tloc"][:, :]).then_inc(W, 16)
        g.dma_start(out=s.evec2b[:, :], in_=DI["evec2"][None, :].broadcast_to([128, E])).then_inc(W, 16)
        g.dma_start(out=s.etb2[:, :], in_=DI["etrash2"][None, :].broadcast_to([128, E])).then_inc(W, 16)
        if PH == 2:
            for m in range(NMEGA):
                g.wait_ge(s.vch, ev[f"acc{m}_{NTILE - 1}"])
                if m > 0:
                    g.wait_ge(s.s_fin, 16 * m)
                ovx = out_d[MEGA * m:MEGA * m + MEGA, :].rearrange("(a p) d -> p a d", p=128)
                g.dma_start(out=ovx,
                            in_=s.x2full[:, NTILE * m:NTILE * m + NTILE, :]).then_inc(s.s_fin, 16)
            g.wait_ge(s.s_fin, 16 * NMEGA)
            return
        g.wait_ge(s.vch, ev["cntev"])
        g.dma_start(out=ag1_in[None, :], in_=s.countrow[0:1, :]).then_inc(s.s_gth, 16)
        g.wait_ge(s.s_gth, 16)
        g.collective_compute("AllGather", ALU.bypass, replica_groups=[[0, 1, 2, 3, 4, 5, 6, 7]],
                             ins=[ag1_in[:]], outs=[ag1_out[:, :]]).then_inc(s.s_ag, 1)
        g.wait_ge(s.s_ag, 1)
        g.dma_start(out=s.agc[:, :], in_=ag1_out[:, :]).then_inc(s.s_gth, 16)
        g.wait_ge(s.vch, ev["capm"])
        g.dma_start(out=rb_d.rearrange("(a e) -> a e", e=E), in_=s.rbsb[:, :]).then_inc(s.s_rb, 16)
        g.wait_ge(s.s_rb, 16)
        g.dma_start(out=s.capall.rearrange("p a e -> p (a e)"),
                    in_=rb_d[None, :].broadcast_to([128, NTA * E])).then_inc(s.s_rb, 16)
        g.wait_ge(s.vch, ev["rks"])
        g.wait_ge(s.vch, ev["srcm"])
        g.wait_ge(s.s_az, 16 * 17)
        for j in range(128):
            if j >= 8:
                g.wait_ge(s.s_ms[j % 8], 16 * (j // 8))
            mv = m_sub[j % 8].rearrange("(r c) -> r c", c=4)
            g.indirect_dma_start(
                out=mv, out_offset=bass.IndirectOffsetOnAxis(ap=s.RkS32[:, j:j + 1], axis=0),
                in_=s.srcM[:, j, :], in_offset=None).then_inc(s.s_ms[j % 8], 16)
        for q in range(8):
            g.wait_ge(s.s_ms[q], 16 * 16)
            if q >= 1:
                g.wait_ge(s.s_m, 16 * 2 * q)
            if q >= 3:
                g.wait_ge(s.vch, ev[f"mrg{q - 2}"])
            dstm = s.macc if q == 0 else s.mtmp[(q - 1) % 2]
            dstt = s.macct if q == 0 else s.mtmpt[(q - 1) % 2]
            g.dma_start(out=dstm[:, :],
                        in_=m_sub[q][0:65536].rearrange("(p x) -> p x", p=128)).then_inc(s.s_m, 16)
            g.dma_start(out=dstt[:, :],
                        in_=m_sub[q][65536:MREC * 4].rearrange("(r c) -> r c", c=4)).then_inc(s.s_m, 16)
        g.wait_ge(s.vch, ev["mrg7"])
        g.wait_ge(s.s_m, 16 * 16)
        g.dma_start(out=m_in[0:65536].rearrange("(p x) -> p x", p=128),
                    in_=s.macc[:, :]).then_inc(s.s_m, 16)
        g.wait_ge(s.s_m, 16 * 17)
        g.dma_start(out=m_in[65536:MREC * 4].rearrange("(r c) -> r c", c=4),
                    in_=s.macct[:, :]).then_inc(s.s_m, 16)
        g.wait_ge(s.s_m, 16 * 18)
        g.collective_compute("ReduceScatter", ALU.add, replica_groups=[[0, 1, 2, 3, 4, 5, 6, 7]],
                             ins=[m_in[0:MREC * 4]], outs=[m_my[:]]).then_inc(s.s_ag, 1)
        g.wait_ge(s.s_ag, 2)
        g.collective_compute("AllGather", ALU.bypass, replica_groups=[[0, 1, 2, 3, 4, 5, 6, 7]],
                             ins=[m_my[:]], outs=[m_all[:, :]]).then_inc(s.s_ag, 1)
        mm4 = m_my.rearrange("(r c) -> r c", c=4)
        g.dma_start(out=s.gtf[:, :],
                    in_=mm4[0:NROWS, 0:1].rearrange("(ct p) o -> p (ct o)", p=128),
                    allow_slow_non_contiguous=True).then_inc(s.s_m, 16)
        g.wait_ge(s.s_m, 16 * 19)
        g.dma_start(out=s.galn[:, :],
                    in_=mm4[0:NROWS, 1:2].rearrange("(ct p) o -> p (ct o)", p=128),
                    allow_slow_non_contiguous=True).then_inc(s.s_m, 16)
        g.wait_ge(s.vch, ev["gidxc"])
        g.wait_ge(s.s_ag2, NMEGA)
        ag2v = ag2_out.rearrange("(t d) -> t d", d=D)
        for ct in range(16):
            g.indirect_dma_start(
                out=s.buf[:, ct, :], out_offset=None, in_=ag2v,
                in_offset=bass.IndirectOffsetOnAxis(
                    ap=s.gidx32[:, ct:ct + 1], axis=0)).then_inc(s.s_gth, 16)
        for e in range(ELOC):
            if e > 0:
                g.wait_ge(s.vch, ev[f"edec{e - 1}"])
            eload(e)
        g.wait_ge(s.s_az, 16 * 17)
        for c4 in range(4):
            g.wait_ge(s.s_eod, 16 * 2 * (c4 + 1))
            lo4 = 512 * D * c4
            hi4 = (512 * D * (c4 + 1)) if c4 < 3 else (MROW * D)
            ob = 8 * 512 * D * c4
            g.collective_compute("AllGather", ALU.bypass,
                                 replica_groups=[[0, 1, 2, 3, 4, 5, 6, 7]],
                                 ins=[ag4_in[lo4:hi4]],
                                 outs=[ag4_out[ob:ob + 8 * (hi4 - lo4)]]).then_inc(s.s_ag4, 1)
        g.wait_ge(s.s_ag4, 4)
        ag4f = ag4_out.rearrange("(r d) -> r d", d=D)
        for mt in range(NTA):
            if mt >= 4:
                g.wait_ge(s.vch, ev[f"fin{mt - 4}"])
            for k in range(4):
                if mt >= 1:
                    g.wait_ge(s.s_cb[k], 16 * mt)
                g.indirect_dma_start(
                    out=s.cbuf[mt % 4][:, k, :], out_offset=None, in_=ag4f,
                    in_offset=bass.IndirectOffsetOnAxis(
                        ap=s.RkC32[:, NTA * k + mt:NTA * k + mt + 1],
                        axis=0)).then_inc(s.s_cb[k], 16)
        if DBG:
            g.dma_start(out=mmy_d[:], in_=m_my[:]).then_inc(s.s_dbg2, 16)
            g.wait_ge(s.s_dbg2, 16)
        g.wait_ge(s.s_fin, 16 * NTA)

    def emit_sync(sy):
        if PH == 1:
            return
        sy.wait_ge(s.vch, ev["init0"])
        for q in range(8):
            sy.dma_start(out=m_sub[q][0:65536].rearrange("(p a c) -> p a c", p=128, c=E),
                         in_=s.zf[:, None, :].broadcast_to([128, 8, E])).then_inc(s.s_az, 16)
            sy.dma_start(out=m_sub[q][65536:MREC * 4].rearrange("(r c) -> r c", c=4),
                         in_=s.zf[0:(MREC * 4 - 65536) // 4, 0:4]).then_inc(s.s_az, 16)
        ag4v = ag4_in.rearrange("(c d) -> c d", d=D)
        sy.dma_start(out=ag4v[NROWS:MROW, :], in_=s.zt[0:1, :]).then_inc(s.s_az, 16)
        if DBG:
            for m in range(NMEGA):
                sy.wait_ge(s.vch, ev[f"x2_{m}_{NTILE - 1}"])
                sy.wait_ge(s.s_dbg, 16 * m)
                xp = x2p1_d[MEGA * m:MEGA * m + MEGA, :].rearrange("(a p) d -> p a d", p=128)
                sy.dma_start(out=xp,
                             in_=s.x2full[:, NTILE * m:NTILE * m + NTILE, :]).then_inc(s.s_dbg, 16)
        ag2vw = ag2_in.rearrange("(t d) -> t d", d=D)
        for m in range(NMEGA):
            sy.wait_ge(s.sch, SC_P2B + SC_P2 * m + 9)
            sy.wait_ge(s.s_sc2, 16 * m)
            sy.dma_start(out=ag2vw[MEGA * m:MEGA * m + MEGA, :].rearrange("(a p) d -> p a d", p=128),
                         in_=s.y2[:, :, :]).then_inc(s.s_sc2, 16)
        if PH == 2:
            return
        for e in range(ELOC):
            sy.wait_ge(s.vch, ev[f"eor{e}_1"])
            sy.wait_ge(s.s_eod, 16 * e)
            sy.dma_start(out=ag4v[MT * e:MT * e + MT, :].rearrange("(f p) d -> p f d", p=128),
                         in_=s.eor[:, :, :]).then_inc(s.s_eod, 16)
        for mt in range(NTA):
            sy.wait_ge(s.vch, ev[f"fin{mt}"])
            sy.wait_ge(s.s_fin, 16 * mt)
            ov = out_d[128 * mt:128 * mt + 128, :]
            sy.dma_start(out=ov, in_=s.outf[mt % 2][:, :]).then_inc(s.s_fin, 16)
        sy.wait_ge(s.s_fin, 16 * NTA)
        if DBG:
            dv = dbg_d
            sy.wait_ge(s.s_dbg, 16 * NMEGA)
            sy.dma_start(out=dbgb_d[:, 0:2048],
                         in_=s.logits[:, :, :].rearrange("p a e -> p (a e)")).then_inc(s.s_dbg, 16)
            sy.wait_ge(s.s_dbg, 16 * (NMEGA + 1))
            sy.dma_start(out=dv[:, 2048:2064], in_=s.galn[:, :]).then_inc(s.s_dbg, 16)
            sy.wait_ge(s.s_dbg, 16 * (NMEGA + 2))
            sy.dma_start(out=dv[:, 2304:2320], in_=s.galn[:, :]).then_inc(s.s_dbg, 16)
            sy.wait_ge(s.s_dbg, 16 * (NMEGA + 3))
            sy.dma_start(out=dbgu_d[:, 0:128],
                         in_=s.RkC32[:, :]).then_inc(s.s_dbg, 16)
            sy.wait_ge(s.s_dbg, 16 * (NMEGA + 4))
            sy.dma_start(out=dv[:, 2688:2704], in_=s.galn[:, :]).then_inc(s.s_dbg, 16)
            sy.wait_ge(s.s_dbg, 16 * (NMEGA + 5))
            sy.dma_start(out=dbgu_d[:, 128:144], in_=s.gidx32[:, :]).then_inc(s.s_dbg, 16)
            sy.wait_ge(s.s_dbg, 16 * (NMEGA + 6))

    with nc.Block() as block:
        @block.vector
        def _(v):
            emit_vector(v)

        @block.scalar
        def _(sc):
            emit_scalar(sc)

        @block.tensor
        def _(t):
            emit_tensor(t)

        @block.gpsimd
        def _(g):
            emit_gpsimd(g)

        @block.sync
        def _(sy):
            emit_sync(sy)

    stk.close()
    return nc


_NC_CACHE = {}


def kernel(**inputs):
    if "nc" not in _NC_CACHE:
        nc_ = build_nc()
        lov.lower_extended_insts(nc_)
        _NC_CACHE["nc"] = nc_
    nc = _NC_CACHE["nc"]
    in_maps = []
    names = [t[0] for t in _IN_SPECS]
    for c in range(NCORE):
        H = _host_prep(c, inputs)
        in_maps.append({k: np.ascontiguousarray(H[k]) for k in names})
    res = run_bass_kernel_spmd(nc, in_maps, core_ids=list(range(NCORE)))
    outs = [np.asarray(res.results[c]["out"], dtype=np.float32) for c in range(NCORE)]
    return np.concatenate(outs, axis=0).reshape(2, 16384, 1024)


# version 13 (final)
# speedup vs baseline: 1.0477x; 1.0161x over previous
import numpy as np
import ml_dtypes
from contextlib import ExitStack
import concourse.bass as bass
import concourse.mybir as mybir
import concourse.library_overlay as lov
from concourse.bass_utils import run_bass_kernel_spmd

F32 = mybir.dt.float32
BF16 = mybir.dt.bfloat16
I16 = mybir.dt.int16
I32 = mybir.dt.int32
U16 = mybir.dt.uint16
U32 = mybir.dt.uint32
AF = mybir.ActivationFunctionType
ALU = mybir.AluOpType
AX = mybir.AxisListType

NCORE = 8
TC = 4096
D = 1024
NCH = 8
QKVCH = 24
MEGA = 512
NMEGA = TC // MEGA
NTILE = 4
NTA = TC // 128
NBLK = 8
NSLAB = 64
EPS = 1e-5
E = 64
ELOC = 8
APS = 5
CAP = 256
MT = 256
NROWS = ELOC * MT
MFD = 10368
II = 768
TB = 32768
LAY = 17

SC_P1 = 73
SC_P2 = 15
SC_P2B = SC_P1 * NMEGA
SC_P3B = SC_P2B + SC_P2 * NMEGA
SC_P4B = SC_P3B + NTA
SC_PE = 6
MROW = ELOC * CAP + 1
MREC = 8 * MROW
NREC = 4 * TC
NLD0A = 23
NLD1 = 49
NLD0 = 58
DBG = False
PH = 99
MLIM = 99
SUB = 99
NLDE = 25


def _host_prep(core, inp):
    H = {}
    t0 = core * TC
    perm = np.zeros(3072, np.int64)
    for c in range(16):
        u = 2 * (c % 8) + c // 8
        for p in range(128):
            perm[128 * c + p] = 192 * u + p
    for cv in range(8):
        c = 16 + cv
        for p in range(128):
            g = 2 * cv + (1 if p >= 64 else 0)
            perm[128 * c + p] = 192 * g + 128 + (p % 64)
    H["wq"] = np.ascontiguousarray(inp["L0_attn_w_qkv"][:, perm])
    H["bq"] = np.ascontiguousarray(inp["L0_attn_b_qkv"][perm].reshape(24, 128).T)
    H["wo"] = np.ascontiguousarray(inp["L0_attn_w_out"])
    H["bo"] = inp["L0_attn_b_out"]
    H["su32"] = np.ascontiguousarray(
        (np.arange(NTA)[:, None] < np.arange(NTA)[None, :]).astype(np.float32))
    pos = (t0 + np.arange(TC)) % 16384
    div = np.exp(np.arange(0, 64, 2, dtype=np.float32) * (-np.log(10000.0) / 64.0))
    ang = pos[:, None].astype(np.float32) * div[None, :]
    pe = np.zeros((TC, 64), np.float32)
    pe[:, 0::2] = np.sin(ang)
    pe[:, 1::2] = np.cos(ang)
    H["peT"] = np.ascontiguousarray(pe.T)
    R = 1e9
    mU = np.zeros((9, 128), np.float32)
    mV = np.zeros((9, 128), np.float32)
    mU[0, :] = -R
    mV[0, :] = R
    for tau in range(8):
        for hp in range(2):
            for hh in range(8):
                mU[1 + tau, hp * 64 + hh * 8 + tau] = R
        for g in range(16):
            mV[1 + tau, g * 8 + tau] = R
    H["mU"], H["mV"] = mU, mV
    H["idn"] = np.eye(128, dtype=np.float32)
    H["lstrict"] = np.ascontiguousarray(
        (np.arange(128)[:, None] < np.arange(128)[None, :]).astype(np.float32))
    porder = np.zeros(D, np.int64)
    for sl8 in range(8):
        kap, par = sl8 // 2, sl8 % 2
        porder[128 * sl8:128 * sl8 + 128] = 256 * kap + par + 2 * np.arange(128)
    H["gwT"] = np.ascontiguousarray(inp["L0_ffn_gate_w"].T[porder])
    es = slice(core * ELOC, core * ELOC + ELOC)
    H["gup"] = np.ascontiguousarray(inp["L0_ffn_gate_packed"][es].transpose(0, 2, 1))
    H["uup"] = np.ascontiguousarray(inp["L0_ffn_up_packed"][es].transpose(0, 2, 1))
    H["dnp"] = np.ascontiguousarray(inp["L0_ffn_down_packed"][es].transpose(0, 2, 1))
    H["shg"] = np.ascontiguousarray(inp["L0_ffn_sh_gate_packed"].T)
    H["shu"] = np.ascontiguousarray(inp["L0_ffn_sh_up_packed"].T)
    H["shd"] = np.ascontiguousarray(inp["L0_ffn_sh_down_packed"].T)
    gs = inp["L0_ffn_gate_scales"][es]
    us = inp["L0_ffn_up_scales"][es]
    ds = inp["L0_ffn_down_scales"][es]
    scA = np.zeros((4, 2, 18, II), np.float32)
    scB = np.zeros((3, 2, 9, D), np.float32)
    for k in range(4):
        for h in range(2):
            for e in range(ELOC):
                scA[k, h, 2 * e + 0] = gs[e, :, 2 * k + h]
                scA[k, h, 2 * e + 1] = us[e, :, 2 * k + h]
            scA[k, h, 16] = inp["L0_ffn_sh_gate_scales"][:, 2 * k + h]
            scA[k, h, 17] = inp["L0_ffn_sh_up_scales"][:, 2 * k + h]
    for k in range(3):
        for h in range(2):
            for e in range(ELOC):
                scB[k, h, e] = ds[e, :, 2 * k + h]
            scB[k, h, 8] = inp["L0_ffn_sh_down_scales"][:, 2 * k + h]
    H["scA"], H["scB"] = scA * 0.5, scB * 0.5
    sm = np.zeros((8, 1), np.float32)
    sm[:core, 0] = 1.0
    H["stepmask"] = sm
    H["coreid"] = np.array([core], np.int32)
    ee = np.arange(E)
    H["evec"] = (MROW * (ee // ELOC) + 256 * (ee % ELOC) + CAP).astype(np.float32)
    H["etrash"] = (MROW * (ee // ELOC) + CAP * ELOC).astype(np.float32)
    H["eiota"] = ee.astype(np.float32)
    tt = (t0 + 128 * np.arange(NTA)[None, :] + np.arange(128)[:, None]).astype(np.int64)
    H["tvec"] = (4096 * ((tt % 4096) // 512) + 512 * (tt // 4096)
                 + (tt % 512)).astype(np.float32)
    H["tloc"] = (128.0 * np.arange(NTA)[None, :]
                 + np.arange(128)[:, None]).astype(np.float32)
    cc_ = (ee % ELOC) // 2
    rr_ = ee // ELOC
    rows_c = np.where(cc_ < 3, 512, 513)
    cbase = np.where(cc_ < 3, 4096 * cc_, 12288)
    H["evec2"] = (cbase + rows_c * rr_ + 256 * ((ee % ELOC) % 2) + CAP).astype(np.float32)
    H["etrash2"] = (12288.0 + 513 * rr_ + 512).astype(np.float32)
    H["x"] = np.ascontiguousarray(inp["L0_attn_x"].reshape(TB, D)[t0:t0 + TC])
    return H


_IN_SPECS = [
    ("x", [TC, D], F32), ("wq", [D, 3072], F32), ("bq", [128, 24], F32),
    ("wo", [D, D], F32), ("bo", [D], F32), ("su32", [NTA, NTA], F32),
    ("peT", [64, TC], F32),
    ("mU", [9, 128], F32), ("mV", [9, 128], F32), ("idn", [128, 128], F32),
    ("lstrict", [128, 128], F32), ("gwT", [D, E], F32),
    ("gup", [ELOC, 512, II], I32), ("uup", [ELOC, 512, II], I32),
    ("dnp", [ELOC, 384, D], I32), ("shg", [512, II], I32), ("shu", [512, II], I32),
    ("shd", [384, D], I32), ("scA", [4, 2, 18, II], F32), ("scB", [3, 2, 9, D], F32),
    ("stepmask", [8, 1], F32), ("coreid", [1], I32),
    ("evec", [E], F32), ("etrash", [E], F32), ("eiota", [E], F32),
    ("tvec", [128, NTA], F32), ("tloc", [128, NTA], F32),
    ("evec2", [E], F32), ("etrash2", [E], F32),
]


def _emit_decode_tile(v, step, w, tmps, lo, hi, st):
    a_l, a_h, ts_, t0f, t1f, af = tmps
    v.tensor_scalar(a_l, w, 7, None, op0=ALU.bitwise_and)
    v.tensor_scalar(a_h, w, 4, 7, op0=ALU.logical_shift_right, op1=ALU.bitwise_and)
    step(v.tensor_scalar(ts_, w, 3, 1, op0=ALU.logical_shift_right, op1=ALU.bitwise_and))
    step(v.tensor_scalar(t0f, ts_, -2.0, 1.0, op0=ALU.mult, op1=ALU.add))
    step(v.tensor_scalar(ts_, w, 7, 1, op0=ALU.logical_shift_right, op1=ALU.bitwise_and))
    v.tensor_scalar(t1f, ts_, -2.0, 1.0, op0=ALU.mult, op1=ALU.add)
    v.tensor_scalar(lo, a_l, 2.0, -4.0, op0=ALU.mult, op1=ALU.add)
    step(v.tensor_scalar(af, a_l, 4.0, -16.0, op0=ALU.mult, op1=ALU.add))
    step(v.tensor_tensor(lo, lo, af, op=ALU.max))
    step(v.tensor_scalar(af, a_l, 1.0, None, op0=ALU.mult))
    step(v.tensor_tensor(lo, lo, af, op=ALU.max))
    v.tensor_tensor(lo, lo, t0f, op=ALU.mult)
    v.tensor_scalar(hi, a_h, 2.0, -4.0, op0=ALU.mult, op1=ALU.add)
    step(v.tensor_scalar(af, a_h, 4.0, -16.0, op0=ALU.mult, op1=ALU.add))
    step(v.tensor_tensor(hi, hi, af, op=ALU.max))
    step(v.tensor_scalar(af, a_h, 1.0, None, op0=ALU.mult))
    step(v.tensor_tensor(hi, hi, af, op=ALU.max))
    step(v.tensor_tensor(hi, hi, t1f, op=ALU.mult))
    v.tensor_tensor(lo, lo, st, op=ALU.mult)
    step(v.tensor_tensor(hi, hi, st, op=ALU.mult))


def build_nc():
    nc = bass.Bass("TRN2", target_bir_lowering=False, debug=False, num_swdge_queues=4)
    d = nc.dram_tensor
    DI = {}
    for name, shape, dt_ in _IN_SPECS:
        DI[name] = d(name, shape, dt_, kind="ExternalInput").ap()
    out_d = d("out", [TC, D], F32, kind="ExternalOutput").ap()
    if DBG:
        x2p1_d = d("x2p1", [TC, D], BF16, kind="ExternalOutput").ap()
        dbg_d = d("dbg", [128, 4096], F32, kind="ExternalOutput").ap()
        mmy_d = d("mmy", [MROW * 4], F32, kind="ExternalOutput").ap()
        dbgb_d = d("dbgb", [128, 2048], BF16, kind="ExternalOutput").ap()
        dbgu_d = d("dbgu", [128, 144], U32, kind="ExternalOutput").ap()

    ag1_in = d("ag1_in", [E], F32, kind="Internal").ap()
    ag1_out = d("ag1_out", [8, E], F32, kind="Internal").ap()
    ag2_in = d("ag2_in", [TC * D], BF16, kind="Internal").ap()
    ag2_out = d("ag2_out", [8 * TC * D], BF16, kind="Internal").ap()
    ag4_in = d("ag4_in", [MROW * D], BF16, kind="Internal").ap()
    ag4_out = d("ag4_out", [(3 * 8 * 512 + 8 * 513) * D], BF16, kind="Internal").ap()
    rb_d = d("rb_d", [NTA * E], F32, kind="Internal").ap()
    m_in = d("m_in", [(MREC + NREC) * 4], F32, kind="Internal").ap()
    m_sub = [d(f"m_sub{q}", [(MREC + NREC) * 4], F32, kind="Internal").ap()
             for q in range(8)]
    m_my = d("m_my", [MROW * 4], F32, kind="Internal").ap()
    m_all = d("m_all", [8, MROW * 4], F32, kind="Internal").ap()

    stk = ExitStack()
    sb = lambda n, s_, dt_: stk.enter_context(nc.sbuf_tensor(n, s_, dt_))
    ps = lambda n, s_, dt_: stk.enter_context(nc.psum_tensor(n, s_, dt_))
    sem = lambda n: stk.enter_context(nc.semaphore(n))

    class S:
        pass
    s = S()
    s.x2full = sb("x2full", [128, NTA, D], BF16)
    s.ident = sb("ident", [128, 128], BF16)
    s.lnsc = sb("lnsc", [128, NTILE], F32)
    s.varsum = sb("varsum", [128, NTILE], F32)
    s.rstd = sb("rstd", [128, NTILE], F32)
    s.bias2 = sb("bias2", [128, NTILE], F32)
    s.logits = sb("logits", [128, NTA, E], BF16)
    s.stepm = sb("stepm", [8, 1], F32)
    s.shardi = sb("shardi", [128, 1], I32)
    s.nb4 = sb("nb4", [128, 1], F32)
    s.galn = sb("galn", [128, 16], F32)
    s.gidx32 = sb("gidx32", [128, 16], U32)
    s.RkC32 = sb("RkC32", [128, 4 * NTA], U32)
    s.zf = sb("zf", [128, E], F32)
    s.zt = s.logits.rearrange("p a e -> p (a e)")[:, 0:D]
    g1 = ExitStack()
    sb1 = lambda n, s_, dt_: g1.enter_context(nc.sbuf_tensor(n, s_, dt_))
    s.wqkv = sb1("wqkv", [128, NCH, 3072], BF16)
    s.bqkvP = sb1("bqkvP", [128, QKVCH], F32)
    s.bout = sb1("bout", [1, D], BF16)
    s.ones = sb1("ones", [1, 128], BF16)
    s.maskU = sb1("maskU", [9, 128], BF16)
    s.maskV = sb1("maskV", [9, 128], BF16)
    s.sums = sb1("sums", [128, NSLAB], F32)
    s.recip = sb1("recip", [128, 64], F32)
    s.wout = sb1("wout", [128, NCH, D], BF16)
    s.xot = sb1("xot", [128, 4096], BF16)
    s.xlnT = sb1("xlnT", [128, NCH, MEGA], BF16)
    s.qPb = sb1("qPb", [128, NSLAB * 128], BF16)
    s.qP = s.qPb[0:64, :].rearrange("p (sl m) -> p sl m", m=128)
    s.kst = s.qPb[64:128, :].rearrange("p (c t) -> p c t", t=MEGA)
    s.kL2 = sb1("kL2", [64, 16, MEGA], BF16)
    s.vP = sb1("vP", [128, NSLAB, 64], BF16)
    s.phat = sb1("phat", [128, NBLK, 128], BF16)
    s.pnT = sb1("pnT", [128, NBLK, 128], BF16)
    s.vstk = sb1("vstk", [128, NBLK, 64], BF16)
    s.xln = s.xot.rearrange("p (a d) -> p a d", a=NTILE)
    s.oT = s.xot.rearrange("p (c x) -> p c x", c=NCH)
    g1.close()
    g2 = ExitStack()
    sb2 = lambda n, s_, dt_: g2.enter_context(nc.sbuf_tensor(n, s_, dt_))
    s.gw = sb2("gw", [128, NCH, E], BF16)
    s.dtmpblk = sb2("dtmpblk", [128, 6 * D], BF16)
    s.dtmp = ([s.dtmpblk[:, i * D:(i + 1) * D].bitcast(I16) for i in range(3)]
              + [s.dtmpblk[:, i * D:(i + 1) * D] for i in range(3, 6)])
    s.shwordsA = sb2("shwordsA", [128, 8, II], I16)
    s.shwordsB = sb2("shwordsB", [128, 3, D], I16)
    s.shsctA = sb2("shsctA", [128, 8, II], BF16)
    s.shsctB = sb2("shsctB", [128, 3, D], BF16)
    s.shloA = sb2("shloA", [128, 8, II], BF16)
    s.shloB = sb2("shloB", [128, 3, D], BF16)
    s.shhiA = sb2("shhiA", [128, 8, II], BF16)
    s.shhiB = sb2("shhiB", [128, 3, D], BF16)
    s.y2 = sb2("y2", [128, NTILE, D], BF16)
    s.y2Tp = sb2("y2Tp", [128, NCH, MEGA], BF16)
    s.hshT = sb2("hshT", [128, 6, MEGA], BF16)
    s.hshU = sb2("hshU", [128, 6, MEGA], BF16)
    s.shTs = sb2("shTs", [128, 8, MEGA], BF16)
    s.logT = sb2("logT", [64, MEGA], BF16)
    g2.close()
    g3 = ExitStack()
    sb3 = lambda n, s_, dt_: g3.enter_context(nc.sbuf_tensor(n, s_, dt_))
    s.srcM = sb3("srcM", [128, 128, 4], F32)
    s.capall = sb3("capall", [128, NTA, E], F32)
    s.su32sb = sb3("su32sb", [128, NTA, NTA], BF16)
    s.posR = sb3("posR", [128, NTA, E], F32)
    s.mask01 = sb3("mask01", [128, NTA, E], BF16)
    s.wfull = sb3("wfull", [128, NTA, E], F32)
    s.ltb = sb3("ltb", [128, NTA, E], F32)
    s.keptf = sb3("keptf", [128, NTA, E], BF16)
    s.mx8 = sb3("mx8", [128, NTA, 8], F32)
    s.nmx = sb3("nmx", [128, NTA], F32)
    s.s4 = sb3("s4", [128, NTA], F32)
    s.w8 = sb3("w8", [128, NTA, 8], F32)
    s.i8 = sb3("i8", [128, NTA, 8], U32)
    s.i8f = sb3("i8f", [128, NTA, 8], F32)
    s.gtf = sb3("gtf", [128, 16], F32)
    s.macc = sb3("macc", [128, 512], F32)
    s.mtmp = [sb3(f"mtmp{i}", [128, 512], F32) for i in range(2)]
    s.macct = sb3("macct", [8, 4], F32)
    s.mtmpt = [sb3(f"mtmpt{i}", [8, 4], F32) for i in range(2)]
    s.RkSf = sb3("RkSf", [128, 4, NTA], F32)
    s.RkS32 = sb3("RkS32", [128, 4 * NTA], U32)
    s.Rtmp1 = sb3("Rtmp1", [128, NTA], F32)
    s.Rtmp2 = sb3("Rtmp2", [128, NTA], F32)
    s.tlv = sb3("tlv", [128, NTA], F32)
    s.tvecs = sb3("tvecs", [128, NTA], F32)
    s.posR2 = sb3("posR2", [128, NTA, E], F32)
    s.evec2b = sb3("evec2b", [128, E], F32)
    s.etb2 = sb3("etb2", [128, E], F32)
    s.RkCall = sb3("RkCall", [128, 4, NTA], F32)
    s.evecb = sb3("evecb", [128, E], F32)
    s.etb = sb3("etb", [128, E], F32)
    s.eiob = sb3("eiob", [128, E], F32)
    s.lstr = sb3("lstr", [128, 128], BF16)
    s.rbsb = sb3("rbsb", [NTA, E], F32)
    s.onesc = sb3("onesc", [1, NTA], F32)
    s.onesf = sb3("onesf", [128, 1], BF16)
    s.Rkall = sb3("Rkall", [128, 4, NTA], F32)
    s.baserow = sb3("baserow", [1, E], F32)
    s.countrow = sb3("countrow", [1, E], F32)
    s.agc = sb3("agc", [8, E], F32)
    g3.close()
    g4 = ExitStack()
    sb4 = lambda n, s_, dt_: g4.enter_context(nc.sbuf_tensor(n, s_, dt_))
    s.g4big = sb4("g4big", [128, 16 * D], BF16)
    s.buf = s.g4big.rearrange("p (a d) -> p a d", d=D)
    s.eloA = s.g4big[:, 0:8 * II].rearrange("p (a d) -> p a d", d=II)
    s.eloB = s.g4big[:, 8 * II:8 * II + 3 * D].rearrange("p (a d) -> p a d", d=D)
    EOF_ = 8 * II + 3 * D
    s.dtmp4 = ([s.g4big[:, EOF_ + D * i:EOF_ + D * (i + 1)].bitcast(I16) for i in range(3)]
               + [s.g4big[:, EOF_ + D * i:EOF_ + D * (i + 1)] for i in range(3, 6)])
    s.bufT = sb4("bufT", [128, 8, NROWS], BF16)
    s.ewordsA = sb4("ewordsA", [128, 8, II], I16)
    s.ewordsB = sb4("ewordsB", [128, 3, D], I16)
    s.esctA = sb4("esctA", [128, 8, II], BF16)
    s.esctB = sb4("esctB", [128, 3, D], BF16)
    s.ehiA = sb4("ehiA", [128, 8, II], BF16)
    s.ehiB = sb4("ehiB", [128, 3, D], BF16)
    s.hT = sb4("hT", [128, 6, MT], BF16)
    s.hup = sb4("hup", [128, 6, MT], BF16)
    s.eoTs = sb4("eoTs", [128, 8, MT], BF16)
    s.eor = sb4("eor", [128, 2, D], BF16)
    g4.close()
    g5 = ExitStack()
    sb5 = lambda n, s_, dt_: g5.enter_context(nc.sbuf_tensor(n, s_, dt_))
    s.cbuf = [sb5(f"cbuf{i}", [128, 4, D], BF16) for i in range(4)]
    s.outf = [sb5(f"outf{i}", [128, D], F32) for i in range(2)]
    g5.close()
    s.ps_qkv = [ps("ps_qkv0", [128, 512], F32), ps("ps_qkv1", [128, 512], F32)]
    s.ps_s = [ps(f"ps_s{i}", [128, 512], F32) for i in range(4)]
    s.ps_tr = ps("ps_tr", [128, 8, 128], BF16)
    s.ps_vs = ps("ps_vs", [128, 8, 64], BF16)

    s.s_wl = sem("s_wl")
    s.s_x = sem("s_x")
    s.s_lay = sem("s_lay")
    s.s_xt = sem("s_xt")
    s.s_y2p = sem("s_y2p")
    s.s_qkv = sem("s_qkv")
    s.s_S = sem("s_S")
    s.s_pnt = sem("s_pnt")
    s.s_vst = sem("s_vst")
    s.s_av = sem("s_av")
    s.s_op = sem("s_op")
    s.s_lg = sem("s_lg")
    s.s_sh = sem("s_sh")
    s.s_cnt = sem("s_cnt")
    s.s_ag = sem("s_ag")
    s.s_gth = sem("s_gth")
    s.s_eW = sem("s_eW")
    s.s_ex = sem("s_ex")
    s.s_sc2 = sem("s_sc2")
    s.s_m = sem("s_m")
    s.s_eod = sem("s_eod")
    s.s_cb = [sem(f"s_cb{k}") for k in range(4)]
    s.s_ms = [sem(f"s_ms{q}") for q in range(8)]
    s.s_dbg = sem("s_dbg")
    s.s_ag2 = sem("s_ag2")
    s.s_ag4 = sem("s_ag4")
    s.s_dbg2 = sem("s_dbg2")
    s.s_az = sem("s_az")
    s.s_fin = sem("s_fin")
    s.s_rb = sem("s_rb")
    s.vch = sem("vch")
    s.sch = sem("sch")
    vc = [0]
    ev = {}

    def emit_vector(v):
        def step(ins, name=None):
            vc[0] += 1
            ins.then_inc(s.vch, 1)
            v.wait_ge(s.vch, vc[0])
            if name:
                ev[name] = vc[0]
        v.memset(s.ones[:, :], 1.0)
        v.memset(s.zt[:, :], 0.0)
        step(v.memset(s.zf[:, :], 0.0), "init0")
        for m in range(min(MLIM, NMEGA)):
            x = s.x2full[:, NTILE * m:NTILE * m + NTILE, :]
            v.wait_ge(s.s_x, 16 * (m + 1))
            if m > 0:
                v.wait_ge(s.sch, SC_P1 * (m - 1) + NTILE)
            step(v.tensor_reduce(s.lnsc[:, :], x[:, :, :], axis=AX.X, op=ALU.add, negate=True))
            step(v.tensor_scalar_mul(s.lnsc[:, :], s.lnsc[:, :], 1.0 / D), f"negmean{m}")
            v.wait_ge(s.sch, SC_P1 * m + NTILE)
            step(v.tensor_scalar(s.rstd[:, :], s.varsum[:, :], 1.0 / D, EPS, op0=ALU.mult, op1=ALU.add))
            step(v.reciprocal(s.rstd[:, :], s.rstd[:, :]), f"rstd{m}")
            v.wait_ge(s.sch, SC_P1 * m + NTILE + 1)
            step(v.tensor_tensor(s.bias2[:, :], s.lnsc[:, :], s.rstd[:, :], op=ALU.mult), f"bias2{m}")
            for tile in range(NTILE):
                v.wait_ge(s.s_xt, NTILE * m + tile + 1)
                step(v.tensor_copy(s.xlnT[:, :, 128 * tile:128 * tile + 128], s.ps_tr[:, :, :]),
                     f"xte{m}_{tile}")
            if SUB < 2:
                continue
            if m > 0:
                v.wait_ge(s.s_S, NBLK * m)
                v.wait_ge(s.s_vst, NBLK * m)
                v.wait_ge(s.s_lay, 16 * LAY * m)
            for c in range(QKVCH):
                v.wait_ge(s.s_qkv, QKVCH * m + c + 1)
                bank = s.ps_qkv[c % 2]
                if c < 16:
                    m0 = 64 * (c // 8) + 8 * (c % 8)
                    v.tensor_scalar(s.qP[:, :, m0:m0 + 8],
                                    bank[0:64, :].rearrange("p (sl ti) -> p sl ti", ti=8),
                                    s.bqkvP[0:64, c:c + 1], None, op0=ALU.add)
                    step(v.tensor_scalar(s.kst[:, c, :], bank[64:128, :],
                                         s.bqkvP[64:128, c:c + 1], None, op0=ALU.add),
                         f"qkve{m}_{c}")
                else:
                    cv = c - 16
                    step(v.tensor_scalar(s.vP[:, :, 8 * cv:8 * cv + 8],
                                         bank[:, :].rearrange("p (sl ti) -> p sl ti", ti=8),
                                         s.bqkvP[:, c:c + 1], None, op0=ALU.add),
                         f"qkve{m}_{c}")
            if SUB < 3 or 30 <= SUB <= 31:
                continue
            for b in range(NBLK):
                blk = NBLK * m + b
                v.wait_ge(s.sch, SC_P1 * m + 9 + 8 * (b + 1))
                step(v.reciprocal(s.recip[:, 8 * b:8 * b + 8], s.sums[:, 8 * b:8 * b + 8]))
                for h4 in range(2):
                    sl = slice(4 * h4, 4 * h4 + 4)
                    step(v.scalar_tensor_tensor(
                        s.phat[:, sl, :], s.phat[:, sl, :], 1.0,
                        s.recip[:, 8 * b + 4 * h4:8 * b + 4 * h4 + 4, None].to_broadcast([128, 4, 128]),
                        op0=ALU.mult, op1=ALU.mult))
                ev[f"pn{blk}"] = vc[0]
                if SUB < 4:
                    continue
                v.wait_ge(s.s_pnt, blk + 1)
                step(v.tensor_copy(s.pnT[:, :, :], s.ps_tr[:, :, :]), f"pnte{blk}")
                v.wait_ge(s.s_vst, blk + 1)
                step(v.tensor_copy(s.vstk[:, :, :], s.ps_vs[:, :, :]), f"vste{blk}")
                if SUB < 5:
                    continue
                v.wait_ge(s.s_av, blk + 1)
                otv = s.oT[:, :, 64 * b:64 * b + 64].rearrange("p c (j t) -> p c j t", t=8)
                v.tensor_copy(otv[0:64], s.ps_s[2 * (b % 2)][0:64, :].rearrange(
                    "p (j hh t) -> p hh j t", hh=8, t=8))
                step(v.tensor_copy(otv[64:128], s.ps_s[2 * (b % 2) + 1][64:128, :].rearrange(
                    "p (j hh t) -> p hh j t", hh=8, t=8)), f"ote{blk}")
            if SUB < 6:
                continue
            for tile in range(NTILE):
                v.wait_ge(s.s_op, NTILE * m + tile + 1)
                xt_ = s.x2full[:, NTILE * m + tile, :]
                v.tensor_tensor(xt_[:, 0:512], xt_[:, 0:512],
                                s.ps_s[2 * (tile % 2)][:, :], op=ALU.add)
                step(v.tensor_tensor(xt_[:, 512:1024], xt_[:, 512:1024],
                                     s.ps_s[2 * (tile % 2) + 1][:, :], op=ALU.add),
                     f"x2_{m}_{tile}")
        ev["p1"] = vc[0]
        if PH == 1:
            return
        v.wait_ge(s.s_wl, 16 * NLD1)
        for tl in range(11):
            if tl < 8:
                N, w_, lo_, hi_, st_ = II, s.shwordsA[:, tl, :], s.shloA[:, tl, :], \
                    s.shhiA[:, tl, :], s.shsctA[:, tl, :]
            else:
                N, w_, lo_, hi_, st_ = D, s.shwordsB[:, tl - 8, :], s.shloB[:, tl - 8, :], \
                    s.shhiB[:, tl - 8, :], s.shsctB[:, tl - 8, :]
            _emit_decode_tile(v, step, w_, [t[:, 0:N] for t in s.dtmp], lo_, hi_, st_)
        ev["shdec"] = vc[0]
        for m in range(NMEGA):
            x2 = s.x2full[:, NTILE * m:NTILE * m + NTILE, :]
            scb = SC_P2B + SC_P2 * m
            if m == 0:
                v.wait_ge(s.s_az, 16 * 17)
            v.wait_ge(s.sch, (SC_P2B + SC_P2 * (m - 1) + NTILE) if m > 0 else (SC_P2B - SC_P1 + NTILE))
            step(v.tensor_reduce(s.lnsc[:, :], x2[:, :, :], axis=AX.X, op=ALU.add, negate=True))
            step(v.tensor_scalar_mul(s.lnsc[:, :], s.lnsc[:, :], 1.0 / D), f"negmean2_{m}")
            v.wait_ge(s.sch, scb + NTILE)
            step(v.tensor_scalar(s.rstd[:, :], s.varsum[:, :], 1.0 / D, EPS, op0=ALU.mult, op1=ALU.add))
            step(v.reciprocal(s.rstd[:, :], s.rstd[:, :]), f"rstd2_{m}")
            v.wait_ge(s.sch, scb + NTILE + 1)
            step(v.tensor_tensor(s.bias2[:, :], s.lnsc[:, :], s.rstd[:, :], op=ALU.mult), f"bias22_{m}")
            for tile in range(NTILE):
                v.wait_ge(s.s_xt, NTILE * NMEGA + NTILE * m + tile + 1)
                step(v.tensor_copy(s.y2Tp[:, :, 128 * tile:128 * tile + 128], s.ps_tr[:, :, :]),
                     f"y2pe{m}_{tile}")
            v.wait_ge(s.s_lg, 6 * m + 1)
            step(v.tensor_copy(s.logT[:, :], s.ps_qkv[0][0:64, :]), f"logTe{m}")
            v.wait_ge(s.s_lg, 6 * m + 2)
            step(v.tensor_copy(s.logits[:, 4 * m:4 * m + 4, :], s.ps_tr[:, 0:4, 0:64]), f"loge{m}")
            for ic in range(6):
                v.wait_ge(s.s_sh, 14 * m + ic + 1)
                step(v.tensor_copy(s.hshU[:, ic, :], s.ps_s[1][:, :]), f"shupe{m}_{ic}")
                v.wait_ge(s.sch, scb + 9 + ic + 1)
                step(v.tensor_tensor(s.hshT[:, ic, :], s.hshT[:, ic, :], s.hshU[:, ic, :],
                                     op=ALU.mult), f"shmul{m}_{ic}")
            for dc in range(8):
                v.wait_ge(s.s_sh, 14 * m + 6 + dc + 1)
                pbank = s.ps_s[2] if dc % 2 == 0 else s.ps_s[3]
                step(v.tensor_copy(s.shTs[:, dc, :], pbank[:, :]), f"shdne{m}_{dc}")
            for tile in range(NTILE):
                v.wait_ge(s.s_lg, 6 * m + 2 + tile + 1)
                if DBG and tile == 0:
                    v.wait_ge(s.s_dbg, 16 * (m + 1))
                x2t = s.x2full[:, NTILE * m + tile, :]
                step(v.tensor_tensor(x2t[:, :], x2t[:, :],
                                     s.ps_tr[:, :, :].rearrange("p a b -> p (a b)"),
                                     op=ALU.add), f"acc{m}_{tile}")
        ev["p2"] = vc[0]
        if PH == 2:
            return
        v.wait_ge(s.s_sh, 14 * NMEGA)
        v.wait_ge(s.s_lg, 6 * NMEGA)
        v.wait_ge(s.s_wl, 16 * NLD0)
        v.memset(s.onesc[:, :], 1.0)
        v.memset(s.onesf[:, :], 1.0)
        step(v.memset(s.srcM[:, :, :], 0.0), "p3init")
        step(v.scalar_tensor_tensor(s.ltb[:, :, :],
                                    s.eiob[:, None, :].to_broadcast([128, NTA, E]), -1e-7,
                                    s.logits[:, :, :], op0=ALU.mult, op1=ALU.add))
        for t_ in range(NTA):
            step(v.max(out=s.mx8[:, t_, :], in_=s.ltb[:, t_, :]))
        ev["mx8"] = vc[0]
        for t_ in range(NTA):
            v.tensor_scalar(s.mask01[:, t_, :], s.ltb[:, t_, :], s.mx8[:, t_, 3:4], None,
                            op0=ALU.is_ge)
        step(v.tensor_scalar(s.nmx[:, :], s.mx8[:, :, 0], -1.0, None, op0=ALU.mult), "mask01")
        v.wait_ge(s.sch, SC_P3B + NTA)
        step(v.tensor_tensor(s.wfull[:, :, :], s.wfull[:, :, :], s.mask01[:, :, :], op=ALU.mult), "e2")
        step(v.tensor_reduce(s.s4[:, :], s.wfull[:, :, :], axis=AX.X, op=ALU.add))
        step(v.reciprocal(s.s4[:, :], s.s4[:, :]), "s4")
        for t_ in range(NTA):
            step(v.tensor_scalar(s.wfull[:, t_, :], s.wfull[:, t_, :], s.s4[:, t_:t_ + 1], None,
                                 op0=ALU.mult), "wfull" if t_ == NTA - 1 else None)
        v.wait_ge(s.s_cnt, 1)
        step(v.tensor_copy(s.countrow[:, :], s.ps_s[0][0:1, 0:E]), "cntev")
        v.wait_ge(s.s_cnt, 2)
        step(v.tensor_copy(s.baserow[:, :], s.ps_s[1][0:1, 0:E]), "basev")
        v.wait_ge(s.s_cnt, 3)
        step(v.tensor_copy(s.rbsb[:, :], s.ps_s[3][0:NTA, 0:E]), "rbev")
        step(v.tensor_scalar(s.rbsb[:, :], s.rbsb[:, :], -1.0, CAP * 1.0,
                             op0=ALU.mult, op1=ALU.add), "capm")
        v.wait_ge(s.s_rb, 16 * 2)
        for t_ in range(NTA):
            v.wait_ge(s.s_cnt, 3 + t_ + 1)
            pb = s.ps_s[t_ % 2]
            step(v.tensor_tensor(s.keptf[:, t_, :], pb[0:128, 0:E], s.capall[:, t_, :],
                                 op=ALU.is_lt), f"kept{t_}")
            step(v.scalar_tensor_tensor(s.posR[:, t_, :], pb[0:128, 0:E], 1.0,
                                        s.evecb[:, :], op0=ALU.mult, op1=ALU.add))
            step(v.tensor_tensor(s.posR[:, t_, :], s.posR[:, t_, :], s.capall[:, t_, :],
                                 op=ALU.subtract))
            step(v.scalar_tensor_tensor(s.posR2[:, t_, :], pb[0:128, 0:E], 1.0,
                                        s.evec2b[:, :], op0=ALU.mult, op1=ALU.add))
            step(v.tensor_tensor(s.posR2[:, t_, :], s.posR2[:, t_, :], s.capall[:, t_, :],
                                 op=ALU.subtract), f"posr{t_}")
        step(v.tensor_tensor(s.wfull[:, :, :], s.wfull[:, :, :], s.keptf[:, :, :], op=ALU.mult), "wm")
        step(v.tensor_tensor(s.keptf[:, :, :], s.keptf[:, :, :], s.mask01[:, :, :], op=ALU.mult))
        step(v.tensor_tensor(s.posR[:, :, :], s.posR[:, :, :], s.keptf[:, :, :], op=ALU.mult))
        step(v.tensor_scalar(s.mask01[:, :, :], s.keptf[:, :, :], -1.0, 1.0,
                             op0=ALU.mult, op1=ALU.add))
        step(v.tensor_tensor(s.posR2[:, :, :], s.posR2[:, :, :], s.keptf[:, :, :], op=ALU.mult))
        step(v.tensor_tensor(s.capall[:, :, :], s.mask01[:, :, :],
                             s.etb[:, None, :].to_broadcast([128, NTA, E]), op=ALU.mult))
        step(v.tensor_tensor(s.posR[:, :, :], s.posR[:, :, :], s.capall[:, :, :], op=ALU.add))
        step(v.tensor_tensor(s.capall[:, :, :], s.mask01[:, :, :],
                             s.etb2[:, None, :].to_broadcast([128, NTA, E]), op=ALU.mult))
        step(v.tensor_tensor(s.posR2[:, :, :], s.posR2[:, :, :], s.capall[:, :, :], op=ALU.add),
             "posm")
        for t_ in range(NTA):
            step(v.max(out=s.w8[:, t_, :], in_=s.wfull[:, t_, :]))
            step(v.max_index(out=s.i8[:, t_, :], in_max=s.w8[:, t_, :], in_values=s.wfull[:, t_, :]))
        step(v.tensor_copy(s.i8f[:, :, :], s.i8[:, :, :]), "i8fd")
        for k in range(4):
            step(v.tensor_tensor(s.keptf[:, :, :], s.eiob[:, None, :].to_broadcast([128, NTA, E]),
                                 s.i8f[:, :, k:k + 1].to_broadcast([128, NTA, E]), op=ALU.is_equal))
            step(v.tensor_tensor(s.capall[:, :, :], s.posR[:, :, :], s.keptf[:, :, :], op=ALU.mult))
            step(v.tensor_reduce(s.Rkall[:, k, :], s.capall[:, :, :], axis=AX.X, op=ALU.add))
            step(v.tensor_tensor(s.capall[:, :, :], s.posR2[:, :, :], s.keptf[:, :, :], op=ALU.mult))
            step(v.tensor_reduce(s.RkCall[:, k, :], s.capall[:, :, :], axis=AX.X, op=ALU.add))
            step(v.tensor_tensor(s.capall[:, :, :], s.mask01[:, :, :], s.keptf[:, :, :],
                                 op=ALU.mult))
            step(v.tensor_reduce(s.Rtmp1[:, :], s.capall[:, :, :], axis=AX.X, op=ALU.add))
            step(v.tensor_scalar(s.Rtmp2[:, :], s.Rtmp1[:, :], -1.0, 1.0,
                                 op0=ALU.mult, op1=ALU.add))
            step(v.tensor_tensor(s.RkSf[:, k, :], s.Rkall[:, k, :], s.Rtmp2[:, :], op=ALU.mult))
            step(v.tensor_scalar(s.Rtmp2[:, :], s.tlv[:, :], 1.0, 1.0 * (MREC + 4096 * k),
                                 op0=ALU.mult, op1=ALU.add))
            step(v.tensor_tensor(s.Rtmp2[:, :], s.Rtmp2[:, :], s.Rtmp1[:, :], op=ALU.mult))
            step(v.tensor_tensor(s.RkSf[:, k, :], s.RkSf[:, k, :], s.Rtmp2[:, :], op=ALU.add))
        step(v.tensor_copy(s.RkC32[:, :], s.RkCall[:, :, :].rearrange("p k t -> p (k t)")), "rkc")
        step(v.tensor_copy(s.RkS32[:, :], s.RkSf[:, :, :].rearrange("p k t -> p (k t)")), "rks")
        srar = s.srcM.rearrange("p a e -> p (a e)")
        for k in range(4):
            step(v.tensor_copy(srar[:, 4 * 32 * k:4 * 32 * (k + 1):4], s.tvecs[:, :]))
            step(v.tensor_copy(srar[:, 4 * 32 * k + 1:4 * 32 * (k + 1):4],
                               s.w8[:, :, k:k + 1].rearrange("p t o -> p (t o)")),
                 "srcm" if k == 3 else None)
        for q in range(1, 8):
            v.wait_ge(s.s_m, 16 * (2 * q + 2))
            step(v.tensor_tensor(s.macc[:, :], s.macc[:, :], s.mtmp[(q - 1) % 2][:, :],
                                 op=ALU.add))
            step(v.tensor_tensor(s.macct[:, :], s.macct[:, :], s.mtmpt[(q - 1) % 2][:, :],
                                 op=ALU.add), f"mrg{q}")
        v.wait_ge(s.s_m, 16 * 19)
        step(v.tensor_copy(s.gidx32[:, :], s.gtf[:, :]), "gidxc")
        for ct in range(16):
            v.wait_ge(s.s_ex, ct + 1)
            step(v.tensor_copy(s.bufT[:, :, 128 * ct:128 * ct + 128], s.ps_tr[:, :, :]),
                 f"btev{ct}")
        for e in range(ELOC):
            v.wait_ge(s.s_eW, 16 * NLDE * (e + 1))
            if e >= 1:
                v.wait_ge(s.s_ex, 16 + 16 * e)
            for tl in range(11):
                if tl < 8:
                    N, w_, lo_, hi_, st_ = II, s.ewordsA[:, tl, :], s.eloA[:, tl, :], \
                        s.ehiA[:, tl, :], s.esctA[:, tl, :]
                else:
                    N, w_, lo_, hi_, st_ = D, s.ewordsB[:, tl - 8, :], s.eloB[:, tl - 8, :], \
                        s.ehiB[:, tl - 8, :], s.esctB[:, tl - 8, :]
                _emit_decode_tile(v, step, w_, [t[:, 0:N] for t in s.dtmp4], lo_, hi_, st_)
            ev[f"edec{e}"] = vc[0]
            exb = 16 + 16 * e
            for ic in range(6):
                v.wait_ge(s.s_ex, exb + ic + 1)
                step(v.tensor_copy(s.hup[:, ic, :], s.ps_s[1][0:128, 0:MT]), f"hup{e}_{ic}")
                v.wait_ge(s.sch, SC_P4B + SC_PE * e + ic + 1)
                step(v.tensor_tensor(s.hT[:, ic, :], s.hT[:, ic, :], s.hup[:, ic, :],
                                     op=ALU.mult), f"hmul{e}_{ic}")
            for dc in range(8):
                v.wait_ge(s.s_ex, exb + 6 + dc + 1)
                pbank = s.ps_s[2] if dc % 2 == 0 else s.ps_s[3]
                step(v.tensor_copy(s.eoTs[:, dc, :], pbank[0:128, 0:MT]), f"eoe{e}_{dc}")
            v.wait_ge(s.s_m, 16 * 20)
            if e > 0:
                v.wait_ge(s.s_eod, 16 * e)
            for grp in range(2):
                v.wait_ge(s.s_ex, exb + 14 + grp + 1)
                step(v.tensor_scalar(s.eor[:, grp, :].rearrange("p (dc x) -> p dc x", dc=8),
                                     s.ps_tr[:, :, :], s.galn[:, 2 * e + grp:2 * e + grp + 1],
                                     None, op0=ALU.mult), f"eor{e}_{grp}")
        ev["p4"] = vc[0]
        for mt in range(NTA):
            for k in range(4):
                v.wait_ge(s.s_cb[k], 16 * (mt + 1))
            if mt >= 2:
                v.wait_ge(s.s_fin, 16 * (mt - 1))
            cb = s.cbuf[mt % 4]
            of = s.outf[mt % 2]
            step(v.tensor_tensor(of[:, :], cb[:, 0, :], cb[:, 1, :], op=ALU.add))
            step(v.tensor_tensor(of[:, :], of[:, :], cb[:, 2, :], op=ALU.add))
            step(v.tensor_tensor(of[:, :], of[:, :], cb[:, 3, :], op=ALU.add))
            step(v.tensor_tensor(of[:, :], of[:, :], s.x2full[:, mt, :], op=ALU.add),
                 f"fin{mt}")
        ev["end"] = vc[0]

    def emit_scalar(sc):
        scc = [0]

        def sstep(ins):
            scc[0] += 1
            ins.then_inc(s.sch, 1)
        for m in range(min(MLIM, NMEGA)):
            x = s.x2full[:, NTILE * m:NTILE * m + NTILE, :]
            sc.wait_ge(s.s_x, 16 * (m + 1))
            sc.wait_ge(s.s_xt, NTILE * m)
            if m > 0:
                sc.wait_ge(s.s_op, NTILE * m)
            sc.wait_ge(s.vch, ev[f"negmean{m}"])
            for t_ in range(NTILE):
                sstep(sc.activation(out=s.xln[:, t_, :], in_=x[:, t_, :], func=AF.Square,
                                    bias=s.lnsc[:, t_:t_ + 1], scale=1.0,
                                    accum_out=s.varsum[:, t_:t_ + 1]))
            sc.wait_ge(s.vch, ev[f"rstd{m}"])
            sstep(sc.activation(out=s.rstd[:, :], in_=s.rstd[:, :], func=AF.Sqrt))
            sc.wait_ge(s.vch, ev[f"bias2{m}"])
            for t_ in range(NTILE):
                sstep(sc.activation(out=s.xln[:, t_, :], in_=x[:, t_, :], func=AF.Identity,
                                    bias=s.bias2[:, t_:t_ + 1], scale=s.rstd[:, t_:t_ + 1]))
            if SUB < 3 or 30 <= SUB <= 31:
                continue
            for b in range(NBLK):
                blk = NBLK * m + b
                sc.wait_ge(s.s_S, blk + 1)
                if blk > 0:
                    sc.wait_ge(s.vch, ev[f"pn{blk - 1}"])
                for j in range(8):
                    bank = s.ps_s[2 * (b % 2) + j // 4]
                    sstep(sc.activation(out=s.phat[:, j, :],
                                        in_=bank[:, 128 * (j % 4):128 * (j % 4) + 128],
                                        func=AF.Exp, scale=0.125,
                                        accum_out=s.sums[:, 8 * b + j:8 * b + j + 1]))
        if PH == 1:
            return
        for m in range(NMEGA):
            x2 = s.x2full[:, NTILE * m:NTILE * m + NTILE, :]
            sc.wait_ge(s.s_xt, NTILE * NMEGA + NTILE * m)
            sc.wait_ge(s.s_sc2, 16 * m)
            sc.wait_ge(s.vch, ev[f"negmean2_{m}"])
            for t_ in range(NTILE):
                sstep(sc.activation(out=s.y2[:, t_, :], in_=x2[:, t_, :], func=AF.Square,
                                    bias=s.lnsc[:, t_:t_ + 1], scale=1.0,
                                    accum_out=s.varsum[:, t_:t_ + 1]))
            sc.wait_ge(s.vch, ev[f"rstd2_{m}"])
            sstep(sc.activation(out=s.rstd[:, :], in_=s.rstd[:, :], func=AF.Sqrt))
            sc.wait_ge(s.vch, ev[f"bias22_{m}"])
            for t_ in range(NTILE):
                sstep(sc.activation(out=s.y2[:, t_, :], in_=x2[:, t_, :], func=AF.Identity,
                                    bias=s.bias2[:, t_:t_ + 1], scale=s.rstd[:, t_:t_ + 1]))
            for ic in range(6):
                sc.wait_ge(s.s_sh, 14 * m + ic + 1)
                sstep(sc.activation(out=s.hshT[:, ic, :], in_=s.ps_s[0][:, :], func=AF.Silu))
        if PH == 2:
            return
        sc.wait_ge(s.vch, ev["mask01"])
        for t_ in range(NTA):
            sstep(sc.activation(out=s.wfull[:, t_, :], in_=s.ltb[:, t_, :], func=AF.Exp,
                                bias=s.nmx[:, t_:t_ + 1], scale=1.0))
        for e in range(ELOC):
            for ic in range(6):
                sc.wait_ge(s.s_ex, 16 + 16 * e + ic + 1)
                sstep(sc.activation(out=s.hT[:, ic, :], in_=s.ps_s[0][0:128, 0:MT], func=AF.Silu))

    def emit_tensor(t):
        t.wait_ge(s.s_wl, 16 * NLD0A)
        for m in range(min(MLIM, NMEGA)):
            for tile in range(NTILE):
                t.wait_ge(s.sch, SC_P1 * m + NTILE + 1 + tile + 1)
                if tile > 0:
                    t.wait_ge(s.vch, ev[f"xte{m}_{tile - 1}"])
                elif m > 0:
                    t.wait_ge(s.vch, ev[f"xte{m - 1}_{NTILE - 1}"])
                    t.wait_ge(s.vch, ev[f"pnte{NBLK * m - 1}"])
                ins = None
                for c in range(NCH):
                    ins = t.matmul(s.ps_tr[:, c, :], s.xln[:, tile, 128 * c:128 * c + 128],
                                   s.ident[:, :], is_transpose=True, start=(c == 0), stop=(c == NCH - 1))
                ins.then_inc(s.s_xt, 1)
            if SUB < 2:
                continue
            for c in range(QKVCH):
                t.wait_ge(s.vch, ev[f"xte{m}_{NTILE - 1}"])
                prevc = QKVCH * m + c - 2
                if prevc >= 0:
                    mm_, cc_ = divmod(prevc, QKVCH)
                    t.wait_ge(s.vch, ev[f"qkve{mm_}_{cc_}"])
                pb = s.ps_qkv[c % 2]
                ins = None
                for k in range(NCH):
                    ins = t.matmul(pb[:, :], s.wqkv[:, k, 128 * c:128 * c + 128], s.xlnT[:, k, :],
                                   start=(k == 0), stop=(k == NCH - 1))
                ins.then_inc(s.s_qkv, 1)
            if SUB < 3 or SUB == 30:
                continue
            for b in range(NBLK):
                blk = NBLK * m + b
                t.wait_ge(s.s_lay, 16 * LAY * (m + 1))
                if blk >= 2 and SUB < 30:
                    pm, pb_ = divmod(blk - 2, NBLK)
                    t.wait_ge(s.sch, SC_P1 * pm + 9 + 8 * (pb_ + 1))
                    if SUB >= 5:
                        if b < 2:
                            t.wait_ge(s.vch, ev[f"x2_{m - 1}_{b + 2}"])
                        else:
                            t.wait_ge(s.vch, ev[f"ote{blk - 2}"])
                bank0 = s.ps_s[2 * (b % 2)]
                bank1 = s.ps_s[2 * (b % 2) + 1]
                ins = None
                for half4 in range(2):
                    bank = bank0 if half4 == 0 else bank1
                    for j4 in range(4):
                        t.matmul(bank[:, 128 * j4:128 * j4 + 128], s.maskU[:, :], s.maskV[:, :],
                                 start=(j4 == 0), stop=False)
                    for j4 in range(4):
                        slab = 8 * b + 4 * half4 + j4
                        ins = t.matmul(bank[:, 128 * j4:128 * j4 + 128], s.qP[:, slab, :],
                                       s.kL2[:, :, 8 * slab:8 * slab + 8],
                                       start=False, stop=(j4 == 3))
                ins.then_inc(s.s_S, 1)
                if SUB < 4 or 30 <= SUB <= 31:
                    continue
                t.wait_ge(s.vch, ev[f"pn{blk}"])
                if blk > 0:
                    t.wait_ge(s.vch, ev[f"pnte{blk - 1}"])
                ins = None
                for j in range(8):
                    ins = t.matmul(s.ps_tr[:, j, :], s.phat[:, j, :], s.ident[:, :],
                                   is_transpose=True, start=(j == 0), stop=(j == 7))
                ins.then_inc(s.s_pnt, 1)
                if blk > 0:
                    t.wait_ge(s.vch, ev[f"vste{blk - 1}"])
                t.wait_ge(s.vch, ev[f"qkve{m}_{QKVCH - 1}"])
                for j in range(8):
                    t.matmul(s.ps_vs[0:64, j, :], s.vP[0:64, 8 * b + j, :],
                             s.ident[0:64, 0:64],
                             is_transpose=True, start=(j == 0), stop=(j == 7))
                ins = None
                for j in range(8):
                    ins = t.matmul(s.ps_vs[64:128, j, :], s.vP[64:128, 8 * b + j, :],
                                   s.ident[64:128, 64:128],
                                   is_transpose=True, start=(j == 0), stop=(j == 7))
                ins.then_inc(s.s_vst, 1)
                if SUB < 5:
                    continue
                t.wait_ge(s.vch, ev[f"pnte{blk}"])
                t.wait_ge(s.vch, ev[f"vste{blk}"])
                t.wait_ge(s.sch, SC_P1 * m + 9 + 8 * (b + 1))
                ob_e = s.ps_s[2 * (b % 2)]
                ob_o = s.ps_s[2 * (b % 2) + 1]
                ins = None
                for j in range(8):
                    for par in range(2):
                        ob = ob_e if par == 0 else ob_o
                        out_ap = ob[64 * par:64 * par + 64, 64 * j:64 * j + 64]
                        ins = t.matmul(out_ap, s.vstk[:, j, :],
                                       s.pnT[:, j, 64 * par:64 * par + 64],
                                       start=(j == 0), stop=(j == 7),
                                       tile_position=(0, 64 * par))
                ins.then_inc(s.s_av, 1)
            if SUB < 6:
                continue
            for tile in range(NTILE):
                pa = s.ps_s[2 * (tile % 2)]
                pb2 = s.ps_s[2 * (tile % 2) + 1]
                if tile < 2:
                    t.wait_ge(s.vch, ev[f"ote{NBLK * m + 6 + tile}"])
                else:
                    t.wait_ge(s.vch, ev[f"x2_{m}_{tile - 2}"])
                for c in range(NCH):
                    lh = s.oT[:, c, 128 * tile:128 * tile + 128]
                    t.matmul(pa[:, :], lh, s.wout[:, c, 0:512],
                             start=(c == 0), stop=False)
                    t.matmul(pb2[:, :], lh, s.wout[:, c, 512:1024],
                             start=(c == 0), stop=False)
                t.matmul(pa[:, :], s.ones[0:1, :], s.bout[0:1, 0:512],
                         start=False, stop=True)
                t.matmul(pb2[:, :], s.ones[0:1, :], s.bout[0:1, 512:1024],
                         start=False, stop=True).then_inc(s.s_op, 1)
        if PH == 1:
            return
        for m in range(NMEGA):
            scb = SC_P2B + SC_P2 * m
            for tile in range(NTILE):
                t.wait_ge(s.sch, scb + NTILE + 1 + tile + 1)
                if tile > 0:
                    t.wait_ge(s.vch, ev[f"y2pe{m}_{tile - 1}"])
                elif m > 0:
                    t.wait_ge(s.vch, ev[f"y2pe{m - 1}_{NTILE - 1}"])
                    t.wait_ge(s.vch, ev[f"acc{m - 1}_{NTILE - 1}"])
                else:
                    t.wait_ge(s.vch, ev[f"pnte{NBLK * NMEGA - 1}"])
                ins = None
                for sl8 in range(NCH):
                    kap, par = sl8 // 2, sl8 % 2
                    dsel = s.y2[:, tile, 256 * kap + par:256 * kap + 256:2]
                    ins = t.matmul(s.ps_tr[:, sl8, :], dsel, s.ident[:, :],
                                   is_transpose=True, start=(sl8 == 0), stop=(sl8 == NCH - 1))
                ins.then_inc(s.s_xt, 1)
            t.wait_ge(s.vch, ev[f"y2pe{m}_{NTILE - 1}"])
            if m > 0:
                t.wait_ge(s.vch, ev[f"logTe{m - 1}"])
            else:
                t.wait_ge(s.vch, ev[f"qkve{NMEGA - 1}_{QKVCH - 1}"])
            ins = None
            for k in range(NCH):
                ins = t.matmul(s.ps_qkv[0][0:64, :], s.gw[:, k, :], s.y2Tp[:, k, :],
                               start=(k == 0), stop=(k == NCH - 1))
            ins.then_inc(s.s_lg, 1)
            t.wait_ge(s.vch, ev[f"logTe{m}"])
            if m > 0:
                t.wait_ge(s.vch, ev[f"loge{m - 1}"])
            t.wait_ge(s.vch, ev[f"y2pe{m}_{NTILE - 1}"])
            ins = None
            for tile in range(NTILE):
                ins = t.matmul(s.ps_tr[:, tile, 0:64], s.logT[0:64, 128 * tile:128 * tile + 128],
                               s.ident[0:64, 0:64], is_transpose=True,
                               start=(tile == 0), stop=(tile == NTILE - 1))
            ins.then_inc(s.s_lg, 1)
            t.wait_ge(s.vch, ev["shdec"])
            t.wait_ge(s.vch, ev[f"y2pe{m}_{NTILE - 1}"])
            for ic in range(6):
                p2, k2 = ic % 2, ic // 2
                isel = slice(256 * k2 + p2, 256 * k2 + 256, 2)
                if ic > 0:
                    t.wait_ge(s.sch, SC_P2B + SC_P2 * m + 9 + ic)
                    t.wait_ge(s.vch, ev[f"shupe{m}_{ic - 1}"])
                elif m > 0:
                    t.wait_ge(s.sch, SC_P2B + SC_P2 * (m - 1) + 15)
                    t.wait_ge(s.vch, ev[f"shupe{m - 1}_{5}"])
                else:
                    t.wait_ge(s.vch, ev[f"x2_{NMEGA - 1}_{2}"])
                ins = None
                for sl8 in range(8):
                    kap, par = sl8 // 2, sl8 % 2
                    wt = s.shloA if par == 0 else s.shhiA
                    t.matmul(s.ps_s[0][:, :], wt[:, kap, isel], s.y2Tp[:, sl8, :],
                             start=(sl8 == 0), stop=(sl8 == 7))
                    ins = t.matmul(s.ps_s[1][:, :], wt[:, 4 + kap, isel], s.y2Tp[:, sl8, :],
                                   start=(sl8 == 0), stop=(sl8 == 7))
                ins.then_inc(s.s_sh, 1)
            for dc in range(8):
                pm_prev = 14 * m + 6 + dc - 2
                if pm_prev >= 14 * m + 6:
                    t.wait_ge(s.vch, ev[f"shdne{m}_{dc - 2}"])
                elif m > 0 and dc < 2:
                    t.wait_ge(s.vch, ev[f"shdne{m - 1}_{6 + dc}"])
                elif m == 0 and dc < 2:
                    t.wait_ge(s.vch, ev[f"x2_{NMEGA - 1}_{3}"])
                t.wait_ge(s.vch, ev[f"shmul{m}_{5}"]) if dc == 0 else None
                pbank = s.ps_s[2] if dc % 2 == 0 else s.ps_s[3]
                ins = None
                for sl6 in range(6):
                    p2, k2 = sl6 % 2, sl6 // 2
                    wt = s.shloB if p2 == 0 else s.shhiB
                    ins = t.matmul(pbank[:, :], wt[:, k2, 128 * dc:128 * dc + 128],
                                   s.hshT[:, sl6, :], start=(sl6 == 0), stop=(sl6 == 5))
                ins.then_inc(s.s_sh, 1)
            for tile in range(NTILE):
                t.wait_ge(s.vch, ev[f"shdne{m}_{7}"])
                t.wait_ge(s.vch, ev[f"loge{m}"])
                if tile > 0:
                    t.wait_ge(s.vch, ev[f"acc{m}_{tile - 1}"])
                ins = None
                for dc in range(8):
                    ins = t.matmul(s.ps_tr[:, dc, :], s.shTs[:, dc, 128 * tile:128 * tile + 128],
                                   s.ident[:, :], is_transpose=True,
                                   start=(dc == 0), stop=(dc == 7))
                ins.then_inc(s.s_lg, 1)
        if PH == 2:
            return
        t.wait_ge(s.vch, ev["mask01"])
        t.wait_ge(s.vch, ev[f"acc{NMEGA - 1}_{NTILE - 1}"])
        t.wait_ge(s.sch, SC_P2B + SC_P2 * (NMEGA - 1) + 15)
        t.wait_ge(s.vch, ev[f"shupe{NMEGA - 1}_5"])
        t.wait_ge(s.vch, ev[f"shdne{NMEGA - 1}_6"])
        t.wait_ge(s.vch, ev[f"shdne{NMEGA - 1}_7"])
        ins = None
        for t_ in range(NTA):
            ins = t.matmul(s.ps_s[0][0:1, 0:E], s.onesf[:, :], s.mask01[:, t_, :],
                           start=(t_ == 0), stop=(t_ == NTA - 1))
        ins.then_inc(s.s_cnt, 1)
        t.wait_ge(s.s_gth, 16 * 2)
        t.matmul(s.ps_s[1][0:1, 0:E], s.stepm[:, :], s.agc[:, :],
                 start=True, stop=True).then_inc(s.s_cnt, 1)
        t.wait_ge(s.vch, ev["basev"])
        t.wait_ge(s.s_wl, 16 * NLD0)
        for t_ in range(NTA):
            t.matmul(s.ps_s[3][0:NTA, 0:E], s.su32sb[:, t_, :], s.mask01[:, t_, :],
                     start=(t_ == 0), stop=False)
        t.matmul(s.ps_s[3][0:NTA, 0:E], s.onesc[0:1, :], s.baserow[0:1, :],
                 start=False, stop=True).then_inc(s.s_cnt, 1)
        t.wait_ge(s.vch, ev["cntev"])
        for t_ in range(NTA):
            if t_ >= 2:
                t.wait_ge(s.vch, ev[f"posr{t_ - 2}"])
            pb = s.ps_s[t_ % 2]
            t.matmul(pb[0:128, 0:E], s.lstr[:, :], s.mask01[:, t_, :],
                     start=True, stop=True).then_inc(s.s_cnt, 1)
        t.wait_ge(s.s_gth, 16 * 18)
        t.wait_ge(s.vch, ev["srcm"])
        for ct in range(16):
            if ct >= 1:
                t.wait_ge(s.vch, ev[f"btev{ct - 1}"])
            ins = None
            for slab in range(8):
                kap, par = slab // 2, slab % 2
                ins = t.matmul(s.ps_tr[:, slab, :],
                               s.buf[:, ct, 256 * kap + par:256 * kap + 256:2],
                               s.ident[:, :], is_transpose=True,
                               start=(slab == 0), stop=(slab == 7))
            ins.then_inc(s.s_ex, 1)
        for e in range(ELOC):
            exb = 16 + 16 * e
            t.wait_ge(s.vch, ev[f"edec{e}"])
            t.wait_ge(s.vch, ev[f"btev{15}"])
            for ic in range(6):
                p2, k2 = ic % 2, ic // 2
                isel = slice(256 * k2 + p2, 256 * k2 + 256, 2)
                if e > 0 or ic > 0:
                    pe_, pic = divmod(6 * e + ic - 1, 6)
                    t.wait_ge(s.sch, SC_P4B + SC_PE * pe_ + pic + 1)
                    t.wait_ge(s.vch, ev[f"hup{pe_}_{pic}"])
                ins = None
                for slab in range(8):
                    kap, par = slab // 2, slab % 2
                    wt = s.eloA if par == 0 else s.ehiA
                    t.matmul(s.ps_s[0][0:128, 0:MT], wt[:, kap, isel],
                             s.bufT[:, slab, MT * e:MT * e + MT],
                             start=(slab == 0), stop=(slab == 7))
                    ins = t.matmul(s.ps_s[1][0:128, 0:MT], wt[:, 4 + kap, isel],
                                   s.bufT[:, slab, MT * e:MT * e + MT],
                                   start=(slab == 0), stop=(slab == 7))
                ins.then_inc(s.s_ex, 1)
            for dc in range(8):
                if e > 0 or dc >= 2:
                    pe_, pdc = divmod(8 * e + dc - 2, 8)
                    t.wait_ge(s.vch, ev[f"eoe{pe_}_{pdc}"])
                if dc == 0:
                    t.wait_ge(s.vch, ev[f"hmul{e}_{5}"])
                pbank = s.ps_s[2] if dc % 2 == 0 else s.ps_s[3]
                ins = None
                for sl6 in range(6):
                    p2, k2 = sl6 % 2, sl6 // 2
                    wt = s.eloB if p2 == 0 else s.ehiB
                    ins = t.matmul(pbank[0:128, 0:MT], wt[:, k2, 128 * dc:128 * dc + 128],
                                   s.hT[:, sl6, :], start=(sl6 == 0), stop=(sl6 == 5))
                ins.then_inc(s.s_ex, 1)
            for grp in range(2):
                t.wait_ge(s.vch, ev[f"eoe{e}_{7}"])
                if e > 0 or grp > 0:
                    t.wait_ge(s.vch, ev[f"eor{e - 1 if grp == 0 else e}_{1 if grp == 0 else 0}"])
                ins = None
                for dc in range(8):
                    ins = t.matmul(s.ps_tr[:, dc, :], s.eoTs[:, dc, 128 * grp:128 * grp + 128],
                                   s.ident[:, :], is_transpose=True,
                                   start=(dc == 0), stop=(dc == 7))
                ins.then_inc(s.s_ex, 1)


    def emit_gpsimd(g):
        W = s.s_wl
        for k in range(NCH):
            g.dma_start(out=s.wqkv[:, k, :], in_=DI["wq"][128 * k:128 * k + 128, :]).then_inc(W, 16)
            g.dma_start(out=s.wout[:, k, :], in_=DI["wo"][128 * k:128 * k + 128, :]).then_inc(W, 16)
        g.dma_start(out=s.bqkvP[:, :], in_=DI["bq"][:, :]).then_inc(W, 16)
        g.dma_start(out=s.bout[:, :], in_=DI["bo"][None, :]).then_inc(W, 16)
        g.dma_start(out=s.ident[:, :], in_=DI["idn"][:, :]).then_inc(W, 16)
        g.dma_start(out=s.maskU[:, :], in_=DI["mU"][:, :]).then_inc(W, 16)
        g.dma_start(out=s.maskV[:, :], in_=DI["mV"][:, :]).then_inc(W, 16)
        g.dma_start(out=s.stepm[:, :], in_=DI["stepmask"][:, :]).then_inc(W, 16)
        g.dma_start(out=s.shardi[:, :], in_=DI["coreid"][None, :].broadcast_to([128, 1])).then_inc(W, 16)
        for m in range(min(MLIM, NMEGA)):
            t0 = m * MEGA
            xv = DI["x"][t0:t0 + MEGA, :].rearrange("(a p) d -> p a d", p=128)
            g.dma_start(out=s.x2full[:, NTILE * m:NTILE * m + NTILE, :], in_=xv).then_inc(s.s_x, 16)
            if m > 0:
                g.wait_ge(s.s_S, NBLK * m)
            if SUB < 3:
                continue
            L = s.s_lay
            lb = 16 * LAY * m
            g.dma_start(out=s.kL2[:, :, :],
                        in_=DI["peT"][:, None, t0:t0 + MEGA].broadcast_to(
                            [64, 16, MEGA])).then_inc(L, 16)
            g.wait_ge(L, lb + 16)
            g.wait_ge(s.vch, ev[f"qkve{m}_{15}"])
            for c16 in range(16):
                g.dma_start(out=s.kL2[:, c16, :], in_=s.kst[:, c16, :],
                            accum_op=ALU.add).then_inc(L, 16)
        if PH == 1:
            if SUB >= 30:
                if SUB == 31:
                    g.wait_ge(s.s_S, NBLK * min(MLIM, NMEGA))
                else:
                    g.wait_ge(s.s_lay, 16 * LAY * min(MLIM, NMEGA))
                subev = None
            else:
                subev = {1: "xte{}_3", 2: "qkve{}_23", 3: "pn##", 4: "vste##", 5: "ote##",
                         6: "x2_{}_3"}[min(SUB, 6)]
            for m in range(min(MLIM, NMEGA)):
                if subev is None:
                    pass
                elif "##" in subev:
                    g.wait_ge(s.vch, ev[subev.replace("##", str(NBLK * (m + 1) - 1))])
                else:
                    g.wait_ge(s.vch, ev[subev.format(m)])
                if m > 0:
                    g.wait_ge(s.s_fin, 16 * m)
                ovx = out_d[MEGA * m:MEGA * m + MEGA, :].rearrange("(a p) d -> p a d", p=128)
                g.dma_start(out=ovx,
                            in_=s.x2full[:, NTILE * m:NTILE * m + NTILE, :]).then_inc(s.s_fin, 16)
            g.wait_ge(s.s_fin, 16 * min(MLIM, NMEGA))
            return
        g.wait_ge(s.s_qkv, QKVCH * NMEGA)
        g.wait_ge(s.s_op, NTILE * NMEGA)
        g.dma_start(out=s.gw[:, :, :], in_=DI["gwT"].rearrange("(k p) e -> p k e", p=128)).then_inc(W, 16)
        g.dma_start(out=s.shwordsA[:, 0:4, :],
                    in_=DI["shg"].rearrange("(k p) i -> p k i", p=128)).then_inc(W, 16)
        g.dma_start(out=s.shwordsA[:, 4:8, :],
                    in_=DI["shu"].rearrange("(k p) i -> p k i", p=128)).then_inc(W, 16)
        g.dma_start(out=s.shwordsB[:, :, :],
                    in_=DI["shd"].rearrange("(k p) i -> p k i", p=128)).then_inc(W, 16)
        for k in range(4):
            for h in range(2):
                g.dma_start(out=s.shsctA[64 * h:64 * h + 64, k, :],
                            in_=DI["scA"][k, h, 16, None, :].broadcast_to([64, II])).then_inc(W, 16)
                g.dma_start(out=s.shsctA[64 * h:64 * h + 64, 4 + k, :],
                            in_=DI["scA"][k, h, 17, None, :].broadcast_to([64, II])).then_inc(W, 16)
        for k in range(3):
            for h in range(2):
                g.dma_start(out=s.shsctB[64 * h:64 * h + 64, k, :],
                            in_=DI["scB"][k, h, 8, None, :].broadcast_to([64, D])).then_inc(W, 16)

        def eload(e):
            g.dma_start(out=s.ewordsA[:, 0:4, :],
                        in_=DI["gup"][e].rearrange("(k p) i -> p k i", p=128)).then_inc(s.s_eW, 16)
            g.dma_start(out=s.ewordsA[:, 4:8, :],
                        in_=DI["uup"][e].rearrange("(k p) i -> p k i", p=128)).then_inc(s.s_eW, 16)
            g.dma_start(out=s.ewordsB[:, :, :],
                        in_=DI["dnp"][e].rearrange("(k p) i -> p k i", p=128)).then_inc(s.s_eW, 16)
            for k in range(4):
                for h in range(2):
                    g.dma_start(out=s.esctA[64 * h:64 * h + 64, k, :],
                                in_=DI["scA"][k, h, 2 * e, None, :].broadcast_to([64, II])).then_inc(s.s_eW, 16)
                    g.dma_start(out=s.esctA[64 * h:64 * h + 64, 4 + k, :],
                                in_=DI["scA"][k, h, 2 * e + 1, None, :].broadcast_to([64, II])).then_inc(s.s_eW, 16)
            for k in range(3):
                for h in range(2):
                    g.dma_start(out=s.esctB[64 * h:64 * h + 64, k, :],
                                in_=DI["scB"][k, h, e, None, :].broadcast_to([64, D])).then_inc(s.s_eW, 16)
        for m in range(NMEGA):
            g.wait_ge(s.s_sc2, 16 * (m + 1))
            sl = slice(MEGA * D * m, MEGA * D * (m + 1))
            ob2 = 8 * MEGA * D * m
            g.collective_compute("AllGather", ALU.bypass,
                                 replica_groups=[[0, 1, 2, 3, 4, 5, 6, 7]],
                                 ins=[ag2_in[sl]],
                                 outs=[ag2_out[ob2:ob2 + 8 * MEGA * D]]).then_inc(s.s_ag2, 1)
        g.wait_ge(s.vch, ev[f"shdne{NMEGA - 1}_{7}"])
        g.wait_ge(s.s_lg, 6 * NMEGA)
        g.dma_start(out=s.lstr[:, :], in_=DI["lstrict"][:, :]).then_inc(W, 16)
        g.dma_start(out=s.su32sb[:, :, :],
                    in_=DI["su32"].rearrange("(o a) e -> o a e", o=1).broadcast_to(
                        [128, NTA, NTA])).then_inc(W, 16)
        g.dma_start(out=s.evecb[:, :], in_=DI["evec"][None, :].broadcast_to([128, E])).then_inc(W, 16)
        g.dma_start(out=s.etb[:, :], in_=DI["etrash"][None, :].broadcast_to([128, E])).then_inc(W, 16)
        g.dma_start(out=s.eiob[:, :], in_=DI["eiota"][None, :].broadcast_to([128, E])).then_inc(W, 16)
        g.dma_start(out=s.tvecs[:, :], in_=DI["tvec"][:, :]).then_inc(W, 16)
        g.dma_start(out=s.tlv[:, :], in_=DI["tloc"][:, :]).then_inc(W, 16)
        g.dma_start(out=s.evec2b[:, :], in_=DI["evec2"][None, :].broadcast_to([128, E])).then_inc(W, 16)
        g.dma_start(out=s.etb2[:, :], in_=DI["etrash2"][None, :].broadcast_to([128, E])).then_inc(W, 16)
        if PH == 2:
            for m in range(NMEGA):
                g.wait_ge(s.vch, ev[f"acc{m}_{NTILE - 1}"])
                if m > 0:
                    g.wait_ge(s.s_fin, 16 * m)
                ovx = out_d[MEGA * m:MEGA * m + MEGA, :].rearrange("(a p) d -> p a d", p=128)
                g.dma_start(out=ovx,
                            in_=s.x2full[:, NTILE * m:NTILE * m + NTILE, :]).then_inc(s.s_fin, 16)
            g.wait_ge(s.s_fin, 16 * NMEGA)
            return
        g.wait_ge(s.vch, ev["cntev"])
        g.dma_start(out=ag1_in[None, :], in_=s.countrow[0:1, :]).then_inc(s.s_gth, 16)
        g.wait_ge(s.s_gth, 16)
        g.collective_compute("AllGather", ALU.bypass, replica_groups=[[0, 1, 2, 3, 4, 5, 6, 7]],
                             ins=[ag1_in[:]], outs=[ag1_out[:, :]]).then_inc(s.s_ag, 1)
        g.wait_ge(s.s_ag, 1)
        g.dma_start(out=s.agc[:, :], in_=ag1_out[:, :]).then_inc(s.s_gth, 16)
        g.wait_ge(s.vch, ev["capm"])
        g.dma_start(out=rb_d.rearrange("(a e) -> a e", e=E), in_=s.rbsb[:, :]).then_inc(s.s_rb, 16)
        g.wait_ge(s.s_rb, 16)
        g.dma_start(out=s.capall.rearrange("p a e -> p (a e)"),
                    in_=rb_d[None, :].broadcast_to([128, NTA * E])).then_inc(s.s_rb, 16)
        g.wait_ge(s.vch, ev["rks"])
        g.wait_ge(s.vch, ev["srcm"])
        g.wait_ge(s.s_az, 16 * 17)
        for j in range(128):
            if j >= 8:
                g.wait_ge(s.s_ms[j % 8], 16 * (j // 8))
            mv = m_sub[j % 8].rearrange("(r c) -> r c", c=4)
            g.indirect_dma_start(
                out=mv, out_offset=bass.IndirectOffsetOnAxis(ap=s.RkS32[:, j:j + 1], axis=0),
                in_=s.srcM[:, j, :], in_offset=None).then_inc(s.s_ms[j % 8], 16)
        for q in range(8):
            g.wait_ge(s.s_ms[q], 16 * 16)
            if q >= 1:
                g.wait_ge(s.s_m, 16 * 2 * q)
            if q >= 3:
                g.wait_ge(s.vch, ev[f"mrg{q - 2}"])
            dstm = s.macc if q == 0 else s.mtmp[(q - 1) % 2]
            dstt = s.macct if q == 0 else s.mtmpt[(q - 1) % 2]
            g.dma_start(out=dstm[:, :],
                        in_=m_sub[q][0:65536].rearrange("(p x) -> p x", p=128)).then_inc(s.s_m, 16)
            g.dma_start(out=dstt[:, :],
                        in_=m_sub[q][65536:MREC * 4].rearrange("(r c) -> r c", c=4)).then_inc(s.s_m, 16)
        g.wait_ge(s.vch, ev["mrg7"])
        g.wait_ge(s.s_m, 16 * 16)
        g.dma_start(out=m_in[0:65536].rearrange("(p x) -> p x", p=128),
                    in_=s.macc[:, :]).then_inc(s.s_m, 16)
        g.wait_ge(s.s_m, 16 * 17)
        g.dma_start(out=m_in[65536:MREC * 4].rearrange("(r c) -> r c", c=4),
                    in_=s.macct[:, :]).then_inc(s.s_m, 16)
        g.wait_ge(s.s_m, 16 * 18)
        g.collective_compute("ReduceScatter", ALU.add, replica_groups=[[0, 1, 2, 3, 4, 5, 6, 7]],
                             ins=[m_in[0:MREC * 4]], outs=[m_my[:]]).then_inc(s.s_ag, 1)
        g.wait_ge(s.s_ag, 2)
        g.collective_compute("AllGather", ALU.bypass, replica_groups=[[0, 1, 2, 3, 4, 5, 6, 7]],
                             ins=[m_my[:]], outs=[m_all[:, :]]).then_inc(s.s_ag, 1)
        mm4 = m_my.rearrange("(r c) -> r c", c=4)
        g.dma_start(out=s.gtf[:, :],
                    in_=mm4[0:NROWS, 0:1].rearrange("(ct p) o -> p (ct o)", p=128),
                    allow_slow_non_contiguous=True).then_inc(s.s_m, 16)
        g.wait_ge(s.s_m, 16 * 19)
        g.dma_start(out=s.galn[:, :],
                    in_=mm4[0:NROWS, 1:2].rearrange("(ct p) o -> p (ct o)", p=128),
                    allow_slow_non_contiguous=True).then_inc(s.s_m, 16)
        g.wait_ge(s.vch, ev["gidxc"])
        g.wait_ge(s.s_ag2, NMEGA)
        ag2v = ag2_out.rearrange("(t d) -> t d", d=D)
        for ct in range(16):
            g.indirect_dma_start(
                out=s.buf[:, ct, :], out_offset=None, in_=ag2v,
                in_offset=bass.IndirectOffsetOnAxis(
                    ap=s.gidx32[:, ct:ct + 1], axis=0)).then_inc(s.s_gth, 16)
        for e in range(ELOC):
            if e > 0:
                g.wait_ge(s.vch, ev[f"edec{e - 1}"])
            eload(e)
        g.wait_ge(s.s_az, 16 * 17)
        for c4 in range(4):
            g.wait_ge(s.s_eod, 16 * 2 * (c4 + 1))
            lo4 = 512 * D * c4
            hi4 = (512 * D * (c4 + 1)) if c4 < 3 else (MROW * D)
            ob = 8 * 512 * D * c4
            g.collective_compute("AllGather", ALU.bypass,
                                 replica_groups=[[0, 1, 2, 3, 4, 5, 6, 7]],
                                 ins=[ag4_in[lo4:hi4]],
                                 outs=[ag4_out[ob:ob + 8 * (hi4 - lo4)]]).then_inc(s.s_ag4, 1)
        g.wait_ge(s.s_ag4, 4)
        ag4f = ag4_out.rearrange("(r d) -> r d", d=D)
        for mt in range(NTA):
            if mt >= 4:
                g.wait_ge(s.vch, ev[f"fin{mt - 4}"])
            for k in range(4):
                if mt >= 1:
                    g.wait_ge(s.s_cb[k], 16 * mt)
                g.indirect_dma_start(
                    out=s.cbuf[mt % 4][:, k, :], out_offset=None, in_=ag4f,
                    in_offset=bass.IndirectOffsetOnAxis(
                        ap=s.RkC32[:, NTA * k + mt:NTA * k + mt + 1],
                        axis=0)).then_inc(s.s_cb[k], 16)
        if DBG:
            g.dma_start(out=mmy_d[:], in_=m_my[:]).then_inc(s.s_dbg2, 16)
            g.wait_ge(s.s_dbg2, 16)
        g.wait_ge(s.s_fin, 16 * NTA)

    def emit_sync(sy):
        if PH == 1:
            return
        sy.wait_ge(s.vch, ev["init0"])
        for q in range(8):
            sy.dma_start(out=m_sub[q][0:65536].rearrange("(p a c) -> p a c", p=128, c=E),
                         in_=s.zf[:, None, :].broadcast_to([128, 8, E])).then_inc(s.s_az, 16)
            sy.dma_start(out=m_sub[q][65536:MREC * 4].rearrange("(r c) -> r c", c=4),
                         in_=s.zf[0:(MREC * 4 - 65536) // 4, 0:4]).then_inc(s.s_az, 16)
        ag4v = ag4_in.rearrange("(c d) -> c d", d=D)
        sy.dma_start(out=ag4v[NROWS:MROW, :], in_=s.zt[0:1, :]).then_inc(s.s_az, 16)
        if DBG:
            for m in range(NMEGA):
                sy.wait_ge(s.vch, ev[f"x2_{m}_{NTILE - 1}"])
                sy.wait_ge(s.s_dbg, 16 * m)
                xp = x2p1_d[MEGA * m:MEGA * m + MEGA, :].rearrange("(a p) d -> p a d", p=128)
                sy.dma_start(out=xp,
                             in_=s.x2full[:, NTILE * m:NTILE * m + NTILE, :]).then_inc(s.s_dbg, 16)
        ag2vw = ag2_in.rearrange("(t d) -> t d", d=D)
        for m in range(NMEGA):
            sy.wait_ge(s.sch, SC_P2B + SC_P2 * m + 9)
            sy.wait_ge(s.s_sc2, 16 * m)
            sy.dma_start(out=ag2vw[MEGA * m:MEGA * m + MEGA, :].rearrange("(a p) d -> p a d", p=128),
                         in_=s.y2[:, :, :]).then_inc(s.s_sc2, 16)
        if PH == 2:
            return
        for e in range(ELOC):
            sy.wait_ge(s.vch, ev[f"eor{e}_1"])
            sy.wait_ge(s.s_eod, 16 * e)
            sy.dma_start(out=ag4v[MT * e:MT * e + MT, :].rearrange("(f p) d -> p f d", p=128),
                         in_=s.eor[:, :, :]).then_inc(s.s_eod, 16)
        for mt in range(NTA):
            sy.wait_ge(s.vch, ev[f"fin{mt}"])
            sy.wait_ge(s.s_fin, 16 * mt)
            ov = out_d[128 * mt:128 * mt + 128, :]
            sy.dma_start(out=ov, in_=s.outf[mt % 2][:, :]).then_inc(s.s_fin, 16)
        sy.wait_ge(s.s_fin, 16 * NTA)
        if DBG:
            dv = dbg_d
            sy.wait_ge(s.s_dbg, 16 * NMEGA)
            sy.dma_start(out=dbgb_d[:, 0:2048],
                         in_=s.logits[:, :, :].rearrange("p a e -> p (a e)")).then_inc(s.s_dbg, 16)
            sy.wait_ge(s.s_dbg, 16 * (NMEGA + 1))
            sy.dma_start(out=dv[:, 2048:2064], in_=s.galn[:, :]).then_inc(s.s_dbg, 16)
            sy.wait_ge(s.s_dbg, 16 * (NMEGA + 2))
            sy.dma_start(out=dv[:, 2304:2320], in_=s.galn[:, :]).then_inc(s.s_dbg, 16)
            sy.wait_ge(s.s_dbg, 16 * (NMEGA + 3))
            sy.dma_start(out=dbgu_d[:, 0:128],
                         in_=s.RkC32[:, :]).then_inc(s.s_dbg, 16)
            sy.wait_ge(s.s_dbg, 16 * (NMEGA + 4))
            sy.dma_start(out=dv[:, 2688:2704], in_=s.galn[:, :]).then_inc(s.s_dbg, 16)
            sy.wait_ge(s.s_dbg, 16 * (NMEGA + 5))
            sy.dma_start(out=dbgu_d[:, 128:144], in_=s.gidx32[:, :]).then_inc(s.s_dbg, 16)
            sy.wait_ge(s.s_dbg, 16 * (NMEGA + 6))

    with nc.Block() as block:
        @block.vector
        def _(v):
            emit_vector(v)

        @block.scalar
        def _(sc):
            emit_scalar(sc)

        @block.tensor
        def _(t):
            emit_tensor(t)

        @block.gpsimd
        def _(g):
            emit_gpsimd(g)

        @block.sync
        def _(sy):
            emit_sync(sy)

    stk.close()
    return nc


_NC_CACHE = {}


def kernel(**inputs):
    if "nc" not in _NC_CACHE:
        nc_ = build_nc()
        lov.lower_extended_insts(nc_)
        _NC_CACHE["nc"] = nc_
    nc = _NC_CACHE["nc"]
    in_maps = []
    names = [t[0] for t in _IN_SPECS]
    for c in range(NCORE):
        H = _host_prep(c, inputs)
        in_maps.append({k: np.ascontiguousarray(H[k]) for k in names})
    res = run_bass_kernel_spmd(nc, in_maps, core_ids=list(range(NCORE)))
    outs = [np.asarray(res.results[c]["out"], dtype=np.float32) for c in range(NCORE)]
    return np.concatenate(outs, axis=0).reshape(2, 16384, 1024)
